# version 01
# baseline (speedup 1.0000x reference)
import contextlib
import numpy as np
import concourse.bass as bass
import concourse.mybir as mybir
from concourse.bass_utils import run_bass_kernel_spmd

F32 = mybir.dt.float32
BF16 = mybir.dt.bfloat16
AF = mybir.ActivationFunctionType
ALU = mybir.AluOpType
AX = mybir.AxisListType

DM = 2048
DIP = 8224
CONVD = 6144
NH = 32
DS = 64
HD = 64
DFF = 8192
EPS = 1e-5
TOK = 1024
TOKH = TOK + 3
NCH = 8
NDC = DM // 128
NEG = -1e30

FUSED = False
DBG_P1 = False

BOUND = {
    "xn_sp":  ([NDC, 128, TOKH], BF16, "p0"),
    "u0_sp":  ([NDC, 128, TOKH], BF16, "p0"),
    "h1_sp":  ([NDC, 128, TOK], BF16, "p1"),
    "s_sp":   ([NDC, 128, TOK], BF16, "p2"),
    "x1_sp":  ([TOK, DM], F32, "p3"),
    "xn1_sp": ([NDC, 128, TOK], BF16, "p4"),
    "s2_sp":  ([NDC, 128, TOK], BF16, "p4"),
}

CONSUMES = {
    "p0": ["x"],
    "p1": ["xn_sp", "u0_sp"],
    "p2": ["h1_sp"],
    "p3": ["xn_sp", "s_sp", "x"],
    "p4": ["x1_sp"],
    "p5": ["xn1_sp", "x1_sp", "s2_sp"],
}

WEIGHTS = {
    "p0": {},
    "p1": {"ipT": [DM, DIP], "opT": [DM, DM], "cw128": [128, 48, 4],
           "cb128": [128, 48], "zb128": [128, NDC], "n1w": [128, NDC],
           "D64": [128, NDC]},
    "p2": {"gwT": [DM, DFF], "uwT": [DM, DFF], "dwT": [DFF, DM],
           "n2w": [128, NDC]},
    "p3": {"fupT": [DM, DFF], "fgT": [DM, DFF], "fdT": [DFF, DM],
           "sug": [128, NDC], "sdp": [128, 64], "wsu": [128, 64],
           "wsg": [128, 64], "wsd": [128, NDC]},
    "p4": {},
    "p5": {"mgT": [16, DM, 1024], "muT": [16, DM, 1024], "mdT": [16, 1024, DM],
           "sgT": [DM, 1024], "suT": [DM, 1024], "sdT": [1024, DM],
           "gw1T": [DM, 16], "eb16": [16, 1]},
}

PHASE_ORDER = ["p0", "p1", "p2", "p3", "p4", "p5"]


class Pg:

    def __init__(self, detect_races=False):
        self.nc = bass.Bass("TRN2", target_bir_lowering=False,
                            detect_race_conditions=detect_races)
        self.stack = contextlib.ExitStack()
        self._const_sb = {}
        self._sem_n = 0
        self.bar_sem = self.sem("bar")
        self.bar_count = 0
        self.io = {}
        self.const_loads = []

    def sem(self, name):
        self._sem_n += 1
        return self.stack.enter_context(self.nc.semaphore(f"{name}_{self._sem_n}"))

    def sbuf(self, name, shape, dtype):
        return self.stack.enter_context(self.nc.sbuf_tensor(name, shape, dtype))

    def psum(self, name, shape, dtype=F32):
        return self.stack.enter_context(self.nc.psum_tensor(name, shape, dtype))

    def const(self, name, arr, dtype):
        if name in self._const_sb:
            return self._const_sb[name]
        arr = np.ascontiguousarray(arr)
        dr = self.nc.inline_tensor(arr, name)
        sb = self.sbuf("c_" + name, list(arr.shape), dtype)
        self.const_loads.append((sb, dr))
        self._const_sb[name] = sb
        return sb

    def flush_consts(self):
        if not self.const_loads:
            return
        nc = self.nc
        loads = self.const_loads
        self.const_loads = []
        csem = self.sem("csem")
        n = len(loads)
        with nc.Block() as blk:
            @blk.sync
            def _(s):
                for sb, dr in loads:
                    s.dma_start(out=sb[...], in_=dr.ap()).then_inc(csem, 16)
                s.wait_ge(csem, 16 * n)
        self.barrier()

    def barrier(self):
        nc = self.nc
        self.bar_count += 5
        target = self.bar_count
        s = self.bar_sem
        with nc.Block() as blk:
            for en in ("sync", "gpsimd", "tensor", "vector", "scalar"):
                def mk(e, tgt=target):
                    e.sem_inc(s, 1)
                    e.wait_ge(s, tgt)
                getattr(blk, en)(mk)


def _bf16(a):
    import ml_dtypes
    return np.asarray(a, dtype=ml_dtypes.bfloat16)


class Chain:

    def __init__(self, sem):
        self.sem = sem
        self.n = 0

    def step(self, eng, inst):
        self.n += 1
        eng.drain(fusable=True).then_inc(self.sem, 1)
        return inst

    def sync(self, eng):
        eng.wait_ge(self.sem, self.n)


def emit_p0(pg):
    nc = pg.nc
    x = pg.io["x"]
    xn_sp = pg.io["xn_sp"]
    u0_sp = pg.io["u0_sp"]

    ident = pg.const("identbf", _bf16(np.eye(128, dtype=np.float32)), BF16)
    pg.flush_consts()

    chunks = [(0, 3)] + [(3 + 128 * c, 128) for c in range(NCH)]
    NCHK = len(chunks)

    x_tm = pg.sbuf("p0_x", [128, 2, DM], F32)
    xn_tm = pg.sbuf("p0_xn", [128, DM], BF16)
    u_tm = pg.sbuf("p0_u", [128, DM], BF16)
    junk = pg.sbuf("p0_junk", [128, DM], F32)
    st_xn = pg.sbuf("p0_sxn", [128, 2, NDC, 128], BF16)
    st_u = pg.sbuf("p0_su", [128, 2, NDC, 128], BF16)
    stats = pg.sbuf("p0_stats", [128, 8], F32)
    ps_tr = [pg.psum(f"p0_ps{i}", [128, 128], BF16) for i in range(4)]

    sx = [pg.sem("p0_sx0"), pg.sem("p0_sx1")]
    sv = pg.sem("p0_sv")
    sp = pg.sem("p0_sp")
    se = pg.sem("p0_se")
    ss = [pg.sem("p0_ss0"), pg.sem("p0_ss1")]
    cs = Chain(pg.sem("p0_cs"))
    cv = Chain(pg.sem("p0_cv"))

    with nc.Block() as blk:
        @blk.sync
        def _(s):
            for i, (off, w) in enumerate(chunks):
                if i >= 2:
                    s.wait_ge(sv, 2 * (i - 2) + 1)
                    s.wait_ge(cs.sem, 6 * (i - 2) + 1)
                s.dma_start(out=x_tm[0:w, i % 2, :], in_=x[off:off + w, :]
                            ).then_inc(sx[i % 2], 16)

        @blk.gpsimd
        def _(g):
            for i, (off, w) in enumerate(chunks):
                g.wait_ge(se, 32 * (i + 1))
                g.dma_start(out=xn_sp.ap()[:, :, off:off + w].rearrange("c p t -> p c t"),
                            in_=st_xn[:, i % 2, :, 0:w]).then_inc(ss[i % 2], 16)
                g.dma_start(out=u0_sp.ap()[:, :, off:off + w].rearrange("c p t -> p c t"),
                            in_=st_u[:, i % 2, :, 0:w]).then_inc(ss[i % 2], 16)
            g.wait_ge(ss[0], 32 * ((NCHK + 1) // 2))
            g.wait_ge(ss[1], 32 * (NCHK // 2))

        @blk.scalar
        def _(sc):
            for i, (off, w) in enumerate(chunks):
                xi = x_tm[0:w, i % 2, :]
                sc.wait_ge(sx[i % 2], 16 * (i // 2 + 1))
                if i >= 1:
                    sc.wait_ge(cv.sem, 8 * (i - 1) + 3)
                cs.step(sc, sc.activation(junk[0:w, :], xi, AF.Square,
                                      accum_out=stats[0:w, 0:1]))
                sc.wait_ge(cv.sem, 8 * i + 6)
                cs.step(sc, sc.activation(stats[0:w, 4:5], stats[0:w, 4:5], AF.Ln))
                cs.sync(sc)
                cs.step(sc, sc.activation(stats[0:w, 5:6], stats[0:w, 4:5], AF.Exp,
                                      scale=-0.5))
                sc.wait_ge(sv, 2 * i + 1)
                cs.step(sc, sc.activation(junk[0:w, :], xn_tm[0:w, :], AF.Square,
                                      accum_out=stats[0:w, 6:7]))
                sc.wait_ge(cv.sem, 8 * i + 8)
                cs.step(sc, sc.activation(stats[0:w, 6:7], stats[0:w, 6:7], AF.Ln))
                cs.sync(sc)
                cs.step(sc, sc.activation(stats[0:w, 7:8], stats[0:w, 6:7], AF.Exp,
                                      scale=-0.5))

        @blk.vector
        def _(v):
            for i, (off, w) in enumerate(chunks):
                xi = x_tm[0:w, i % 2, :]
                v.wait_ge(sx[i % 2], 16 * (i // 2 + 1))
                cv.step(v, v.reduce_sum(stats[0:w, 1:2], xi, axis=AX.X))
                v.wait_ge(cs.sem, 6 * i + 1)
                cv.sync(v)
                cv.step(v, v.tensor_scalar(stats[0:w, 2:3], stats[0:w, 1:2], 1.0 / DM, None, ALU.mult))
                cv.step(v, v.tensor_scalar(stats[0:w, 3:4], stats[0:w, 0:1], 1.0 / DM, None, ALU.mult))
                cv.sync(v)
                cv.step(v, v.tensor_tensor(stats[0:w, 4:5], stats[0:w, 2:3], stats[0:w, 2:3], ALU.mult))
                cv.sync(v)
                cv.step(v, v.tensor_tensor(stats[0:w, 4:5], stats[0:w, 3:4], stats[0:w, 4:5], ALU.subtract))
                cv.sync(v)
                cv.step(v, v.tensor_scalar(stats[0:w, 4:5], stats[0:w, 4:5], EPS, None,
                                        ALU.add))
                v.wait_ge(cs.sem, 6 * i + 3)
                if i >= 1:
                    v.wait_ge(sp, 32 * (i - 1) + 16)
                v.tensor_scalar(xn_tm[0:w, :], xi, stats[0:w, 2:3], stats[0:w, 5:6],
                                ALU.subtract, ALU.mult)
                v.drain(fusable=True).then_inc(sv, 1)
                v.wait_ge(cs.sem, 6 * i + 4)
                cv.step(v, v.tensor_scalar(stats[0:w, 6:7], stats[0:w, 6:7], 1.0 / DM, None, ALU.mult))
                cv.sync(v)
                cv.step(v, v.tensor_scalar(stats[0:w, 6:7], stats[0:w, 6:7], EPS, None,
                                        ALU.add))
                v.wait_ge(cs.sem, 6 * i + 6)
                if i >= 1:
                    v.wait_ge(sp, 32 * i)
                v.tensor_scalar(u_tm[0:w, :], xn_tm[0:w, :], stats[0:w, 7:8], None,
                                ALU.mult)
                v.drain(fusable=True).then_inc(sv, 1)
                if i >= 2:
                    v.wait_ge(ss[i % 2], 32 * (i // 2))
                for t in range(2 * NDC):
                    g = 32 * i + t + 1
                    v.wait_ge(sp, g)
                    src = ps_tr[t % 4]
                    dst = st_xn if t < NDC else st_u
                    dc = t % NDC
                    v.tensor_copy(dst[:, i % 2, dc, 0:w], src[:, 0:w])
                    v.drain(fusable=True).then_inc(se, 1)

        @blk.tensor
        def _(t):
            for i, (off, w) in enumerate(chunks):
                t.wait_ge(sv, 2 * i + 1)
                for dc in range(NDC):
                    g = 32 * i + dc + 1
                    if g > 4:
                        t.wait_ge(se, g - 4)
                    t.transpose(ps_tr[(g - 1) % 4][:, 0:w],
                                xn_tm[0:w, 128 * dc:128 * dc + 128],
                                ident[0:w, 0:w])
                    t.drain().then_inc(sp, 1)
                t.wait_ge(sv, 2 * i + 2)
                for dc in range(NDC):
                    g = 32 * i + NDC + dc + 1
                    t.wait_ge(se, g - 4)
                    t.transpose(ps_tr[(g - 1) % 4][:, 0:w],
                                u_tm[0:w, 128 * dc:128 * dc + 128],
                                ident[0:w, 0:w])
                    t.drain().then_inc(sp, 1)

    pg.barrier()


def emit_p1(pg):
    nc = pg.nc
    xn_sp, u0_sp, h1_sp = pg.io["xn_sp"], pg.io["u0_sp"], pg.io["h1_sp"]
    ipT, opT = pg.io["ipT"], pg.io["opT"]
    cwv, cbv, zbv, n1v, D64v = (pg.io[k] for k in
                                ("cw128", "cb128", "zb128", "n1w", "D64"))

    identbf = pg.const("identbf", _bf16(np.eye(128, dtype=np.float32)), BF16)
    identf = pg.const("identf", np.eye(128, dtype=np.float32), F32)
    sel = np.zeros((32, 32, 128), np.float32)
    for h in range(32):
        sel[h, h, :] = 1.0
    self_sel = pg.const("sel32f", sel, F32)
    maskbT = pg.const("maskbT",
                      np.where(np.arange(128)[:, None] <= np.arange(128)[None, :],
                               0.0, NEG).astype(np.float32), F32)
    diag32 = pg.const("diag32", np.eye(32, dtype=np.float32), F32)
    ones32x128 = pg.const("o32x128", np.ones((32, 128), np.float32), F32)
    ones32x64 = pg.const("o32x64", np.ones((32, 64), np.float32), F32)
    rep = np.zeros((32, 2048), np.float32)
    for h in range(32):
        rep[h, 64 * h:64 * (h + 1)] = 1.0
    repmask = pg.const("repmask", rep, F32)

    n1_sb = pg.sbuf("p1_n1", [128, NDC], F32)
    zb_sb = pg.sbuf("p1_zb", [128, NDC], F32)
    cb_sb = pg.sbuf("p1_cb", [128, 48], F32)
    cw_sb = pg.sbuf("p1_cw", [128, 48, 4], F32)
    dd_sb = pg.sbuf("p1_dd", [128, NDC], F32)

    u_q = pg.sbuf("p1_uq", [128, NDC, 259], BF16)
    wslab = pg.sbuf("p1_ws", [128, 2, NDC, 128], BF16)
    zst = pg.sbuf("p1_zst", [128, 2, 256], BF16)
    dt_q = pg.sbuf("p1_dt", [32, 256], F32)
    xbc = pg.sbuf("p1_xbc", [128, 48, 256], BF16)
    cacc = pg.sbuf("p1_cacc", [128, 256], F32)
    y_sb = pg.sbuf("p1_y", [128, NDC, TOK], BF16)
    xT = pg.sbuf("p1_xT", [128, NDC, 128], BF16)
    BT = pg.sbuf("p1_BT", [128, NDC, 128], BF16)
    tA = pg.sbuf("p1_tA", [32, 128], F32)
    tL = pg.sbuf("p1_tL", [32, 128], F32)
    tR = pg.sbuf("p1_tR", [32, 128], F32)
    alog = pg.sbuf("p1_alog", [32, 128], F32)
    cum = pg.sbuf("p1_cum", [32, 128], F32)
    cumT = pg.sbuf("p1_cumT", [128, 32], F32)
    cumlast = pg.sbuf("p1_cl", [32, 1], F32)
    runsum = pg.sbuf("p1_run", [32, 2], F32)
    pref = pg.sbuf("p1_pref", [32, NCH], F32)
    rhsd = pg.sbuf("p1_rhsd", [32, 32], F32)
    dte = pg.sbuf("p1_dte", [128, 32], F32)
    acolb = pg.sbuf("p1_acolb", [64, 32], F32)
    ecT_all = pg.sbuf("p1_ecT", [128, NCH, 32], F32)
    tS = pg.sbuf("p1_tS", [128, 2, 128], F32)
    tE2 = pg.sbuf("p1_tE2", [128, 2, 128], F32)
    St = pg.sbuf("p1_St", [128, 2, 128], BF16)
    Bd = pg.sbuf("p1_Bd", [128, 2, 64], BF16)
    H = pg.sbuf("p1_H", [64, NH * DS], F32)
    hl_st = pg.sbuf("p1_hlst", [64, 2, NH * DS], BF16)
    hl_ld = pg.sbuf("p1_hlld", [64, 2, NH * DS], BF16)
    Hin = pg.sbuf("p1_Hin", [64, NH * DS], F32)
    Hg = pg.sbuf("p1_Hg", [64, NH * DS], F32)
    Hg16 = pg.sbuf("p1_Hg16", [64, NH * DS], BF16)
    pack = pg.sbuf("p1_pack", [64, NH * DS], F32)
    rem_s = pg.sbuf("p1_rems", [64, NH * DS], F32)
    rem_a = pg.sbuf("p1_rema", [64, NH * DS], F32)
    lA = pg.sbuf("p1_lA", [32, 64], F32)
    ecr = pg.sbuf("p1_ecr", [32, 128], F32)
    prefx = pg.sbuf("p1_prefx", [32, NCH], F32)
    gm_sb = pg.sbuf("p1_gm", [64, 8], F32)
    EC = pg.sbuf("p1_EC", [64, 2, 128], BF16)
    Cch = pg.sbuf("p1_Cch", [128, 2, NDC, 128], BF16)
    zs_t = pg.sbuf("p1_zs", [128, 2, TOK], BF16)
    xn_m = pg.sbuf("p1_xnm", [128, 2, TOK], BF16)
    h1_st = pg.sbuf("p1_h1st", [128, 2, TOK], BF16)

    ps = [pg.psum(f"p1_ps{i}", [128, 512], F32) for i in range(6)]
    psT = [pg.psum(f"p1_psT{i}", [128, 128], BF16) for i in range(2)]

    C_sp = nc.dram_tensor("p1_C_sp", [NDC, 128, TOK], BF16, kind="Internal")
    zs_sp = nc.dram_tensor("p1_zs_sp", [NDC, 128, TOK], BF16, kind="Internal")
    hl_sp = nc.dram_tensor("p1_hl_sp", [NCH, 64, NH * DS], BF16, kind="Internal")
    cc_in = nc.dram_tensor("p1_ccin", [128, NH * DS], F32, kind="Internal")
    cc_out = nc.dram_tensor("p1_ccout", [512, NH * DS], F32, kind="Internal")

    if DBG_P1:
        dbg = {n: nc.dram_tensor("dbg_" + n, sh, dt2, kind="ExternalOutput")
               for n, sh, dt2 in [
                   ("xbc", [128, 48, 256], BF16), ("dt", [32, 256], F32),
                   ("cum", [32, 128], F32), ("cumT", [128, 32], F32),
                   ("dte", [128, 32], F32), ("acolb", [64, 32], F32),
                   ("y1", [128, NDC, TOK], BF16), ("H", [64, 2048], F32),
                   ("hin", [64, 2048], F32), ("y2", [128, NDC, TOK], BF16),
                   ("zs", [NDC, 128, TOK], BF16), ("ecr", [32, 128], F32),
                   ("hg", [64, 2048], F32)]}
    s_dbg = pg.sem("p1_dbg")
    s_cst = pg.sem("p1_cst")
    s_wl = [pg.sem("p1_wl0"), pg.sem("p1_wl1")]
    s_uq = pg.sem("p1_uq")
    s_un = pg.sem("p1_un")
    s_m1 = pg.sem("p1_m1")
    s_ez = pg.sem("p1_ez")
    s_ec = pg.sem("p1_ec")
    s_spl = pg.sem("p1_spl")
    s_cum = pg.sem("p1_cum")
    s_rd = pg.sem("p1_rd")
    s_pbc = pg.sem("p1_pbc")
    s_ptc = pg.sem("p1_ptc")
    s_ctv = pg.sem("p1_ctv")
    s_dte = pg.sem("p1_dte")
    s_acb = pg.sem("p1_acb")
    s_vtr = pg.sem("p1_vtr")
    s_ptr = pg.sem("p1_ptr")
    s_pcb = pg.sem("p1_pcb")
    s_px0 = pg.sem("p1_px0")
    s_py = pg.sem("p1_py")
    s_psm = pg.sem("p1_psm")
    s_dl = pg.sem("p1_dl")
    s_xe = pg.sem("p1_xe")
    s_st = pg.sem("p1_st")
    s_bd = pg.sem("p1_bd")
    s_hu = pg.sem("p1_hu")
    s_ya = pg.sem("p1_ya")
    s_hlc = pg.sem("p1_hlc")
    s_scd = pg.sem("p1_scd")
    s_csp = pg.sem("p1_csp")
    s_cc = pg.sem("p1_cc")
    s_pk = pg.sem("p1_pk")
    s_pkd = pg.sem("p1_pkd")
    s_rm = [pg.sem("p1_rm0"), pg.sem("p1_rm1"), pg.sem("p1_rm2")]
    s_hbc = pg.sem("p1_hbc")
    s_hg = pg.sem("p1_hg")
    s_hgc = pg.sem("p1_hgc")
    s_ptc2 = pg.sem("p1_ptc2")
    s_ecr = pg.sem("p1_ecr")
    s_pec = pg.sem("p1_pec")
    s_ecm = pg.sem("p1_ecm")
    s_py2 = pg.sem("p1_py2")
    s_ya2 = pg.sem("p1_ya2")
    s_cld = [pg.sem("p1_cld0"), pg.sem("p1_cld1")]
    s_hinz = pg.sem("p1_hinz")
    s_gz = pg.sem("p1_gz")
    s_gy = pg.sem("p1_gy")
    s_m2 = pg.sem("p1_m2")
    s_xnl = [pg.sem("p1_xnl0"), pg.sem("p1_xnl1")]
    s_h1e = pg.sem("p1_h1e")
    s_h1s = [pg.sem("p1_h1s0"), pg.sem("p1_h1s1")]
    s_zs = [pg.sem("p1_zs0"), pg.sem("p1_zs1")]
    s_zl = [pg.sem("p1_zl0"), pg.sem("p1_zl1")]
    s_hlw = pg.sem("p1_hlw")
    s_hls = [pg.sem("p1_hls0"), pg.sem("p1_hls1")]
    s_hld = [pg.sem("p1_hld0"), pg.sem("p1_hld1")]
    s_pkh = pg.sem("p1_pkh")
    s_dtp = pg.sem("p1_dtp")
    s_ect = pg.sem("p1_ect")
    s_aex = pg.sem("p1_aex")
    s_pfx = pg.sem("p1_pfx")
    s_la = pg.sem("p1_la")
    s_la2 = pg.sem("p1_la2")
    s_gm = pg.sem("p1_gm")
    cve = Chain(pg.sem("p1_cve"))
    cse = Chain(pg.sem("p1_cse"))

    pg.flush_consts()

    QW = 259
    NM = 65

    def mwidth(m):
        return 32 if m == 64 else 128

    with nc.Block() as blk:
        @blk.sync
        def _(s):
            for name, sb in (("n1w", n1_sb), ("zb128", zb_sb), ("cb128", cb_sb),
                             ("D64", dd_sb)):
                s.dma_start(out=sb[...], in_=pg.io[name]).then_inc(s_cst, 16)
            s.dma_start(out=cw_sb[...], in_=pg.io["cw128"]).then_inc(s_cst, 16)
            s.dma_start(out=gm_sb[...], in_=pg.io["gmask"]).then_inc(s_gm, 16)
            for Q in range(4):
                if Q >= 1:
                    s.wait_ge(s_m1, NM * Q)
                s.dma_start(out=u_q[:, :, :],
                            in_=u0_sp.ap()[:, :, 256 * Q:256 * Q + QW]
                            .rearrange("c p t -> p c t")).then_inc(s_uq, 16)
            for lc in range(NCH):
                s.wait_ge(s_cc, 1)
                if lc >= 2:
                    s.wait_ge(s_ya2, 16 * (lc - 1))
                    s.wait_ge(s_hg, lc - 1)
                s.dma_start(out=Cch[:, lc % 2, :, :],
                            in_=C_sp.ap()[:, :, 128 * lc:128 * (lc + 1)]
                            .rearrange("c p t -> p c t")).then_inc(s_cld[lc % 2], 16)
                s.dma_start(out=hl_ld[:, lc % 2, :],
                            in_=hl_sp.ap()[lc, :, :]).then_inc(s_hld[lc % 2], 16)
            for dc in range(NDC):
                s.wait_ge(s_ya2, 16 * NCH)
                if dc >= 2:
                    s.wait_ge(s_gy, dc - 1)
                s.dma_start(out=zs_t[:, dc % 2, :],
                            in_=zs_sp.ap()[dc, :, :]).then_inc(s_zl[dc % 2], 16)
            for m in range(NDC + 2):
                if m >= 2:
                    s.wait_ge(s_h1e, m - 1)
                if m < NDC:
                    s.dma_start(out=xn_m[:, m % 2, :],
                                in_=xn_sp.ap()[m, :, 3:TOKH]).then_inc(s_xnl[m % 2], 16)
                if m >= 2:
                    s.dma_start(out=h1_sp.ap()[m - 2, :, :],
                                in_=h1_st[:, m % 2, :]).then_inc(s_h1s[m % 2], 16)
            s.wait_ge(s_h1s[0], 16 * 8)
            s.wait_ge(s_h1s[1], 16 * 8)

        @blk.gpsimd
        def _(g):
            for Q in range(4):
                for m in range(NM):
                    gidx = NM * Q + m
                    if gidx >= 2:
                        g.wait_ge(s_m1, gidx - 1)
                    mw = mwidth(m)
                    g.dma_start(out=wslab[:, gidx % 2, :, 0:mw],
                                in_=ipT[:, 128 * m:128 * m + mw]
                                .rearrange("(c p) m -> p c m", p=128)
                                ).then_inc(s_wl[gidx % 2], 16)
                g.wait_ge(s_ec, 48 * (Q + 1))
                g.dma_start(out=C_sp.ap()[:, :, 256 * Q:256 * (Q + 1)]
                            .rearrange("c p t -> p c t"),
                            in_=xbc[:, 32:48, :]).then_inc(s_csp, 16)
                for cq in range(2):
                    lc = 2 * Q + cq
                    g.wait_ge(s_hlw, lc + 1)
                    g.dma_start(out=hl_sp.ap()[lc, :, :],
                                in_=hl_st[:, lc % 2, :]).then_inc(s_hls[lc % 2], 16)
            g.wait_ge(s_scd, NCH)
            g.dma_start(out=cc_in.ap()[0:64, :], in_=H[:, :]).then_inc(s_pkh, 16)
            g.wait_ge(s_pk, 1)
            g.dma_start(out=cc_in.ap()[64:128, :], in_=pack[:, :]).then_inc(s_pkd, 16)
            g.wait_ge(s_pkd, 16)
            g.wait_ge(s_pkh, 16)
            g.collective_compute(
                "AllGather", ALU.bypass,
                replica_groups=[[0, 1, 2, 3], [4, 5, 6, 7]],
                ins=[cc_in.ap().opt()], outs=[cc_out.ap().opt()],
            ).then_inc(s_cc, 1)
            g.wait_ge(s_cc, 1)
            for r in range(4):
                if r >= 1:
                    g.wait_ge(s_hinz, r)
                g.dma_start(out=rem_s[:, :], in_=cc_out.ap()[128 * r:128 * r + 64, :]
                            ).then_inc(s_rm[0], 16)
                g.dma_start(out=rem_a[:, :],
                            in_=cc_out.ap()[128 * r + 64:128 * r + 128, :]
                            ).then_inc(s_rm[1], 16)
            if DBG_P1:
                g.wait_ge(s_ya, 16 * NCH)
                g.wait_ge(s_scd, NCH)
                g.dma_start(out=dbg["y1"].ap(), in_=y_sb[...]).then_inc(s_dbg, 16)
                g.dma_start(out=dbg["xbc"].ap(), in_=xbc[...]).then_inc(s_dbg, 16)
                g.dma_start(out=dbg["dt"].ap(), in_=dt_q[...]).then_inc(s_dbg, 16)
                g.dma_start(out=dbg["cum"].ap(), in_=cum[...]).then_inc(s_dbg, 16)
                g.dma_start(out=dbg["cumT"].ap(), in_=cumT[...]).then_inc(s_dbg, 16)
                g.dma_start(out=dbg["dte"].ap(), in_=dte[...]).then_inc(s_dbg, 16)
                g.dma_start(out=dbg["acolb"].ap(), in_=acolb[...]).then_inc(s_dbg, 16)
                g.dma_start(out=dbg["H"].ap(), in_=H[...]).then_inc(s_dbg, 16)
                g.wait_ge(s_hinz, 4)
                g.dma_start(out=dbg["hin"].ap(), in_=Hin[...]).then_inc(s_dbg, 16)
                g.wait_ge(s_ya2, 16 * NCH)
                g.wait_ge(s_hg, NCH)
                g.dma_start(out=dbg["y2"].ap(), in_=y_sb[...]).then_inc(s_dbg, 16)
                g.dma_start(out=dbg["zs"].ap(), in_=zs_sp.ap()).then_inc(s_dbg, 16)
                g.dma_start(out=dbg["ecr"].ap(), in_=ecr[...]).then_inc(s_dbg, 16)
                g.dma_start(out=dbg["hg"].ap(), in_=Hg[...]).then_inc(s_dbg, 16)
                g.wait_ge(s_dbg, 16 * 13)
            for m in range(NDC):
                if m < 2:
                    g.wait_ge(s_m1, 4 * NM - 2 + m + 1)
                else:
                    g.wait_ge(s_m2, m - 1)
                g.dma_start(out=wslab[:, m % 2, :, :],
                            in_=opT[:, 128 * m:128 * (m + 1)]
                            .rearrange("(c p) m -> p c m", p=128)
                            ).then_inc(s_wl[m % 2], 16)


        @blk.tensor
        def _(t):
            for Q in range(4):
                t.wait_ge(s_un, 16 * (Q + 1))
                for m in range(NM):
                    gidx = NM * Q + m
                    mw = mwidth(m)
                    t.wait_ge(s_wl[gidx % 2], 16 * (gidx // 2 + 1))
                    if Q >= 1 and m < 4:
                        t.wait_ge(s_scd, 2 * Q)
                        t.wait_ge(s_ez, 17 * Q)
                        t.wait_ge(s_ec, 48 * Q)
                    elif m >= 4:
                        pm = m - 4
                        if pm <= 15:
                            t.wait_ge(s_ez, 17 * Q + pm + 1)
                        elif pm == 64:
                            pass
                        else:
                            t.wait_ge(s_ec, 48 * Q + pm - 15)
                    bank = ps[m % 4]
                    for k in range(NDC):
                        t.matmul(bank[0:mw, 0:QW], wslab[:, gidx % 2, k, 0:mw],
                                 u_q[:, k, :], start=(k == 0),
                                 stop=(k == NDC - 1))
                        if k == NDC - 1:
                            t.drain().then_inc(s_m1, 1)
                for cq in range(2):
                    lc = 2 * Q + cq
                    cqs = slice(128 * cq, 128 * cq + 128)
                    t.wait_ge(s_cum, lc + 1)
                    if lc >= 1:
                        t.wait_ge(s_ctv, lc)
                        t.wait_ge(s_ya, 16 * lc)
                        t.wait_ge(s_dte, lc)
                        t.wait_ge(s_acb, lc)
                    t.transpose(ps[4][:, 384:416], cum[0:32, :],
                                identf[0:32, 0:32])
                    t.drain().then_inc(s_ptc, 1)
                    t.wait_ge(s_rd, lc + 1)
                    t.matmul(ps[5][:, 384:416], ones32x128[:, :], rhsd[:, :],
                             start=True, stop=True)
                    t.drain().then_inc(s_pbc, 1)
                    t.wait_ge(s_ec, 48 * (Q + 1))
                    for tt in range(32):
                        g2 = 32 * lc + tt + 1
                        if g2 > 2:
                            t.wait_ge(s_vtr, g2 - 2)
                        t.transpose(psT[tt % 2][:, :], xbc[:, tt, cqs],
                                    identbf[:, :])
                        t.drain().then_inc(s_ptr, 1)
                    def cb_mm(h):
                        if h >= 2:
                            t.wait_ge(s_st, 32 * lc + h - 1)
                        elif lc >= 1:
                            t.wait_ge(s_st, 32 * lc)
                        t.matmul(ps[h % 2][:, 0:128],
                                 xbc[64 * (h % 2):64 * (h % 2) + 64, 16 + h // 2, cqs],
                                 xbc[64 * (h % 2):64 * (h % 2) + 64, 32 + h // 2, cqs],
                                 start=True, stop=True)
                        t.drain().then_inc(s_pcb, 1)

                    def x0_mm(h):
                        if h >= 1:
                            t.wait_ge(s_dl, 32 * lc + h)
                        elif lc >= 1:
                            t.wait_ge(s_dl, 32 * lc)
                        t.matmul(ps[2][:, 0:128], self_sel[:, h, :], cum[:, :],
                                 start=True, stop=True)
                        t.drain().then_inc(s_px0, 1)

                    cb_mm(0)
                    x0_mm(0)
                    cb_mm(1)
                    for h in range(32):
                        dc = h // 2
                        t.wait_ge(s_st, 32 * lc + h + 1)
                        t.wait_ge(s_vtr, 32 * lc + dc + 1)
                        if h % 2 == 0 and 16 * lc + dc - 1 > 0:
                            t.wait_ge(s_ya, 16 * lc + dc - 1)
                        t.matmul(ps[4 + dc % 2][64 * (h % 2):64 * (h % 2) + 64, 0:128],
                                 xT[:, dc, 64 * (h % 2):64 * (h % 2) + 64],
                                 St[:, h % 2, :], start=True, stop=True)
                        t.drain().then_inc(s_py, 1)
                        t.wait_ge(s_bd, 32 * lc + h + 1)
                        if h >= 1:
                            t.wait_ge(s_hu, 32 * lc + h)
                        elif lc >= 1:
                            t.wait_ge(s_hu, 32 * lc)
                        t.matmul(ps[3][0:64, 0:64], Bd[:, h % 2, :],
                                 xT[:, dc, 64 * (h % 2):64 * (h % 2) + 64],
                                 start=True, stop=True)
                        t.drain().then_inc(s_psm, 1)
                        if h + 2 < 32:
                            cb_mm(h + 2)
                        if h + 1 < 32:
                            x0_mm(h + 1)
            t.wait_ge(s_la, 1)
            t.wait_ge(s_scd, NCH)
            for j in range(4):
                t.matmul(ps[j][0:64, 0:512], lA[:, :], repmask[:, 512 * j:512 * (j + 1)],
                         start=True, stop=True)
                t.drain().then_inc(s_hbc, 1)
            for lc in range(NCH):
                t.wait_ge(s_la2, lc + 1)
                if lc == 0:
                    t.wait_ge(s_pk, 1)
                else:
                    t.wait_ge(s_hg, lc)
                for j in range(4):
                    t.matmul(ps[j][0:64, 0:512], lA[:, :],
                             repmask[:, 512 * j:512 * (j + 1)],
                             start=True, stop=True)
                    t.drain().then_inc(s_hbc, 1)
                t.wait_ge(s_ect, lc + 1)
                if lc >= 1:
                    t.wait_ge(s_ecr, lc)
                    t.wait_ge(s_ya2, 16 * lc)
                t.transpose(ps[4][0:32, 256:384], ecT_all[:, lc, :],
                            identf[:, :])
                t.drain().then_inc(s_ptc2, 1)
                for h in range(32):
                    dc = h // 2
                    if h >= 2:
                        t.wait_ge(s_ecm, 32 * lc + h - 1)
                    else:
                        t.wait_ge(s_hg, lc + 1)
                        t.wait_ge(s_ecr, lc + 1)
                    t.matmul(ps[h % 2][0:64, 0:128], self_sel[:, h, 0:64],
                             ecr[:, :], start=True, stop=True)
                    t.drain().then_inc(s_pec, 1)
                    t.wait_ge(s_ecm, 32 * lc + h + 1)
                    t.wait_ge(s_hgc, lc + 1)
                    if h % 2 == 0 and 16 * lc + dc - 1 > 0:
                        t.wait_ge(s_ya2, 16 * lc + dc - 1)
                    t.matmul(ps[4 + dc % 2][64 * (h % 2):64 * (h % 2) + 64, 0:128],
                             Hg16[:, 64 * h:64 * h + 64], EC[:, h % 2, :],
                             start=True, stop=True)
                    t.drain().then_inc(s_py2, 1)
            t.wait_ge(s_gy, 16)
            t.wait_ge(s_hg, NCH)
            t.wait_ge(s_ecm, 32 * NCH)
            for m in range(NDC):
                gidx = 4 * NM + m
                t.wait_ge(s_wl[gidx % 2], 16 * (gidx // 2 + 1))
                if m >= 2:
                    t.wait_ge(s_h1e, m - 1)
                b0, b1 = ps[2 * (m % 2)], ps[2 * (m % 2) + 1]
                for k in range(NDC):
                    t.matmul(b0[:, 0:512], wslab[:, gidx % 2, k, :],
                             y_sb[:, k, 0:512], start=(k == 0), stop=(k == NDC - 1))
                    t.matmul(b1[:, 0:512], wslab[:, gidx % 2, k, :],
                             y_sb[:, k, 512:1024], start=(k == 0), stop=(k == NDC - 1))
                    if k == NDC - 1:
                        t.drain().then_inc(s_m2, 1)

        @blk.vector
        def _(v):
            v.wait_ge(s_cst, 16 * 5)
            v.memset(H[:, :], 0.0)
            v.memset(Hin[:, :], 0.0)
            v.memset(runsum[:, 0:1], 0.0)
            for Q in range(4):
                v.wait_ge(s_uq, 16 * (Q + 1))
                for dc in range(NDC):
                    v.tensor_scalar(u_q[:, dc, :], u_q[:, dc, :], n1_sb[:, dc:dc + 1],
                                    None, ALU.mult)
                    v.drain(fusable=True).then_inc(s_un, 1)
                if Q >= 1:
                    v.wait_ge(s_csp, 16 * Q)
                for m in range(16, 64):
                    dc = m - 16
                    v.wait_ge(s_m1, NM * Q + m + 1)
                    bank = ps[m % 4]
                    v.tensor_scalar(cacc[:, :], bank[:, 0:256], cw_sb[:, dc, 0:1],
                                    cb_sb[:, dc:dc + 1], ALU.mult, ALU.add)
                    for k2 in (1, 2):
                        v.scalar_tensor_tensor(cacc[:, :], bank[:, k2:k2 + 256],
                                               cw_sb[:, dc, k2:k2 + 1], cacc[:, :],
                                               ALU.mult, ALU.add)
                    v.scalar_tensor_tensor(xbc[:, dc, :], bank[:, 3:259],
                                           cw_sb[:, dc, 3:4], cacc[:, :],
                                           ALU.mult, ALU.add)
                    v.drain(fusable=True).then_inc(s_ec, 1)
                for cq in range(2):
                    lc = 2 * Q + cq
                    cqs = slice(128 * cq, 128 * cq + 128)
                    qcols = slice(256 * Q + 128 * cq, 256 * Q + 128 * cq + 128)
                    v.wait_ge(s_ez, 17 * (Q + 1))
                    v.tensor_scalar(tR[:, :], dt_q[0:32, cqs], 0.0, None, ALU.max)
                    v.wait_ge(s_spl, lc + 1)
                    v.scalar_tensor_tensor(alog[:, :], tR[:, :], -1.0, tL[:, :],
                                           ALU.mult, ALU.subtract)
                    if lc >= 1:
                        v.wait_ge(s_px0, 32 * lc)
                    v.tensor_tensor_scan(cum[:, :], alog[:, :], alog[:, :], 0.0,
                                         ALU.add, ALU.bypass)
                    v.drain(fusable=True).then_inc(s_cum, 1)
                    cve.step(v, v.tensor_copy(cumlast[:, :], cum[:, 127:128]))
                    cve.sync(v)
                    cve.step(v, v.tensor_copy(pref[:, lc:lc + 1], runsum[:, 0:1]))
                    cve.sync(v)
                    cve.step(v, v.tensor_tensor(runsum[:, 0:1], runsum[:, 0:1],
                                             cumlast[:, :], ALU.add))
                    cve.sync(v)
                    if lc >= 1:
                        v.wait_ge(s_pbc, lc)
                    v.tensor_scalar(rhsd[:, :], diag32[:, :], cumlast[:, :],
                                    None, ALU.mult)
                    v.drain(fusable=True).then_inc(s_rd, 1)
                    v.wait_ge(s_ptc, lc + 1)
                    cve.step(v, v.tensor_copy(cumT[:, :], ps[4][:, 384:416]))
                    cve.sync(v)
                    v.sem_inc(s_ctv, 1)
                    v.wait_ge(s_pbc, lc + 1)
                    v.tensor_tensor(dte[:, :], ps[5][:, 384:416], cumT[:, :],
                                    ALU.subtract).then_inc(s_dtp, 1)
                    if lc >= 2:
                        v.wait_ge(s_hls[lc % 2], 16 * (lc // 2))
                    v.tensor_copy(hl_st[:, lc % 2, :], H[:, :])
                    v.drain(fusable=True).then_inc(s_hlw, 1)
                    for tt in range(32):
                        v.wait_ge(s_ptr, 32 * lc + tt + 1)
                        dst = xT if tt < 16 else BT
                        v.tensor_copy(dst[:, tt % 16, :], psT[tt % 2][:, :])
                        v.drain(fusable=True).then_inc(s_vtr, 1)
                    v.wait_ge(s_dte, lc + 1)
                    v.wait_ge(s_acb, lc + 1)
                    for h in range(32):
                        dc = h // 2
                        v.wait_ge(s_px0, 32 * lc + h + 1)
                        if 32 * lc + h - 1 > 0:
                            v.wait_ge(s_xe, 32 * lc + h - 1)
                        v.tensor_scalar(tS[:, h % 2, :], ps[2][:, 0:128],
                                        cumT[:, h:h + 1], None, ALU.subtract)
                        v.tensor_tensor(tS[:, h % 2, :], tS[:, h % 2, :],
                                        maskbT[:, :], ALU.add).then_inc(s_dl, 1)
                        if 32 * lc + h - 1 > 0:
                            v.wait_ge(s_psm, 32 * lc + h - 1)
                        v.tensor_scalar(Bd[:, h % 2, :],
                                        BT[:, dc, 64 * (h % 2):64 * (h % 2) + 64],
                                        dte[:, h:h + 1], None, ALU.mult)
                        v.drain(fusable=True).then_inc(s_bd, 1)
                        v.wait_ge(s_xe, 32 * lc + h + 1)
                        if 32 * lc + h - 1 > 0:
                            v.wait_ge(s_py, 32 * lc + h - 1)
                        v.tensor_tensor(St[:, h % 2, :], tE2[:, h % 2, :],
                                        ps[h % 2][:, 0:128], ALU.mult)
                        v.drain(fusable=True).then_inc(s_st, 1)
                        v.wait_ge(s_psm, 32 * lc + h + 1)
                        v.scalar_tensor_tensor(H[0:64, 64 * h:64 * h + 64],
                                               H[0:64, 64 * h:64 * h + 64],
                                               acolb[:, h:h + 1],
                                               ps[3][0:64, 0:64],
                                               ALU.mult, ALU.add).then_inc(s_hu, 1)
                        if h % 2 == 1:
                            v.wait_ge(s_py, 32 * lc + h + 1)
                            v.scalar_tensor_tensor(y_sb[:, dc, qcols],
                                                   xbc[:, dc, cqs],
                                                   dd_sb[:, dc:dc + 1],
                                                   ps[4 + dc % 2][:, 0:128],
                                                   ALU.mult, ALU.add
                                                   ).then_inc(s_ya, 1)
                    v.sem_inc(s_scd, 1)
            v.wait_ge(s_aex, 1)
            v.tensor_scalar(lA[:, :], ones32x64[:, :], runsum[:, 1:2], None,
                            ALU.mult)
            v.drain(fusable=True).then_inc(s_la, 1)
            v.wait_ge(s_hbc, 4)
            for j in range(4):
                v.tensor_copy(pack[:, 512 * j:512 * (j + 1)],
                              ps[j][0:64, 0:512])
                if j == 3:
                    v.drain(fusable=True).then_inc(s_pk, 1)
            v.wait_ge(s_gm, 16)
            for r in range(4):
                v.wait_ge(s_rm[0], 16 * (r + 1))
                v.wait_ge(s_rm[1], 16 * (r + 1))
                v.tensor_scalar(Hg[:, :], rem_a[:, :],
                                gm_sb[:, r:r + 1], gm_sb[:, 4 + r:5 + r],
                                ALU.mult, ALU.add)
                v.tensor_tensor(Hin[:, :], Hin[:, :], Hg[:, :], ALU.mult)
                v.tensor_scalar(Hg[:, :], rem_s[:, :],
                                gm_sb[:, r:r + 1], None, ALU.mult)
                v.tensor_tensor(Hin[:, :], Hin[:, :], Hg[:, :], ALU.add
                                ).then_inc(s_hinz, 1)
            if DBG_P1:
                v.wait_ge(s_dbg, 16 * 8)
            for lc in range(NCH):
                v.wait_ge(s_pfx, lc + 1)
                if lc >= 1:
                    v.wait_ge(s_hbc, 4 + 4 * lc)
                v.tensor_scalar(lA[:, :], ones32x64[:, :], prefx[:, lc:lc + 1],
                                None, ALU.mult)
                v.drain(fusable=True).then_inc(s_la2, 1)
                v.wait_ge(s_hbc, 4 + 4 * (lc + 1))
                for j in range(4):
                    v.tensor_tensor(Hg[:, 512 * j:512 * (j + 1)],
                                    Hin[:, 512 * j:512 * (j + 1)],
                                    ps[j][0:64, 0:512], ALU.mult)
                v.wait_ge(s_hld[lc % 2], 16 * (lc // 2 + 1))
                v.tensor_tensor(Hg[:, :], Hg[:, :], hl_ld[:, lc % 2, :], ALU.add
                                ).then_inc(s_hg, 1)
                v.tensor_copy(Hg16[:, :], Hg[:, :])
                v.drain(fusable=True).then_inc(s_hgc, 1)
                v.wait_ge(s_ptc2, lc + 1)
                v.tensor_copy(ecr[:, :], ps[4][0:32, 256:384])
                v.drain(fusable=True).then_inc(s_ecr, 1)
                v.wait_ge(s_cld[lc % 2], 16 * (lc // 2 + 1))
                for h in range(32):
                    dc = h // 2
                    v.wait_ge(s_pec, 32 * lc + h + 1)
                    if 32 * lc + h - 1 > 0:
                        v.wait_ge(s_py2, 32 * lc + h - 1)
                    v.tensor_tensor(EC[:, h % 2, :],
                                    Cch[64 * (h % 2):64 * (h % 2) + 64, lc % 2,
                                        dc, :],
                                    ps[h % 2][0:64, 0:128], ALU.mult)
                    v.drain(fusable=True).then_inc(s_ecm, 1)
                    if h % 2 == 1:
                        v.wait_ge(s_py2, 32 * lc + h + 1)
                        lcol = slice(128 * lc, 128 * lc + 128)
                        v.tensor_tensor(y_sb[:, dc, lcol], y_sb[:, dc, lcol],
                                        ps[4 + dc % 2][:, 0:128], ALU.add
                                        ).then_inc(s_ya2, 1)
            if DBG_P1:
                v.wait_ge(s_dbg, 16 * 8)
            for dc in range(NDC):
                v.wait_ge(s_zl[dc % 2], 16 * (dc // 2 + 1))
                v.tensor_tensor(y_sb[:, dc, :], y_sb[:, dc, :], zs_t[:, dc % 2, :],
                                ALU.mult)
                v.drain(fusable=True).then_inc(s_gy, 1)
            for m in range(NDC):
                v.wait_ge(s_m2, m + 1)
                v.wait_ge(s_xnl[m % 2], 16 * (m // 2 + 1))
                if m >= 2:
                    v.wait_ge(s_h1s[m % 2], 16 * (m // 2))
                b0, b1 = ps[2 * (m % 2)], ps[2 * (m % 2) + 1]
                v.tensor_tensor(h1_st[:, m % 2, 0:512], b0[:, 0:512],
                                xn_m[:, m % 2, 0:512], ALU.add)
                v.tensor_tensor(h1_st[:, m % 2, 512:1024], b1[:, 0:512],
                                xn_m[:, m % 2, 512:1024], ALU.add)
                v.drain(fusable=True).then_inc(s_h1e, 1)

        @blk.scalar
        def _(sc):
            for Q in range(4):
                for m in range(16):
                    sc.wait_ge(s_m1, NM * Q + m + 1)
                    gi = 16 * Q + m
                    if gi >= 2:
                        sc.wait_ge(s_zs[gi % 2], 16 * (gi // 2))
                    sc.activation(zst[:, gi % 2, :], ps[m % 4][:, 3:259], AF.Silu,
                                  bias=zb_sb[:, m:m + 1])
                    sc.drain(fusable=True).then_inc(s_ez, 1)
                    sc.dma_start(out=zs_sp.ap()[m, :, 256 * Q:256 * (Q + 1)],
                                 in_=zst[:, gi % 2, :]).then_inc(s_zs[gi % 2], 16)
                sc.wait_ge(s_m1, NM * (Q + 1))
                if Q >= 1:
                    sc.wait_ge(s_cum, 2 * Q)
                sc.copy(dt_q[0:32, :], ps[0][0:32, 3:259])
                sc.drain(fusable=True).then_inc(s_ez, 1)
                for cq in range(2):
                    lc = 2 * Q + cq
                    sc.wait_ge(s_ez, 17 * (Q + 1))
                    if lc >= 1:
                        sc.wait_ge(s_cum, lc)
                    cse.step(sc, sc.activation(tA[:, :], dt_q[0:32, 128 * cq:128 * cq + 128],
                                           AF.Abs))
                    cse.sync(sc)
                    cse.step(sc, sc.activation(tA[:, :], tA[:, :], AF.Exp, scale=-1.0))
                    cse.sync(sc)
                    sc.activation(tL[:, :], tA[:, :], AF.Ln, bias=1.0
                                  ).then_inc(s_spl, 1)
                    sc.wait_ge(s_dtp, lc + 1)
                    sc.activation(dte[:, :], dte[:, :], AF.Exp).then_inc(s_dte, 1)
                    sc.wait_ge(s_pbc, lc + 1)
                    sc.activation(acolb[:, :], ps[5][0:64, 384:416], AF.Exp
                                  ).then_inc(s_acb, 1)
                    sc.wait_ge(s_ctv, lc + 1)
                    sc.activation(ecT_all[:, lc, :], cumT[:, :], AF.Exp)
                    sc.drain(fusable=True).then_inc(s_ect, 1)
                    for h in range(32):
                        sc.wait_ge(s_dl, 32 * lc + h + 1)
                        if 32 * lc + h - 1 > 0:
                            sc.wait_ge(s_st, 32 * lc + h - 1)
                        sc.activation(tE2[:, h % 2, :], tS[:, h % 2, :], AF.Exp
                                      ).then_inc(s_xe, 1)
            sc.wait_ge(s_scd, NCH)
            sc.activation(runsum[:, 1:2], runsum[:, 0:1], AF.Exp).then_inc(s_aex, 1)
            for lc in range(NCH):
                sc.activation(prefx[:, lc:lc + 1], pref[:, lc:lc + 1], AF.Exp
                              ).then_inc(s_pfx, 1)

    pg.barrier()


def emit_p2(pg):
    nc = pg.nc
    h1_sp, s_sp = pg.io["h1_sp"], pg.io["s_sp"]
    gwT, uwT, dwT = pg.io["gwT"], pg.io["uwT"], pg.io["dwT"]

    onescol = pg.const("ones128c", _bf16(np.ones((128, 1), np.float32)), BF16)
    onesrow = pg.const("ones1x128f", np.ones((1, 128), np.float32), F32)
    pg.flush_consts()

    n2_sb = pg.sbuf("p2_n2", [128, NDC], F32)
    h1 = pg.sbuf("p2_h1", [128, NDC, TOK], BF16)
    sq = pg.sbuf("p2_sq", [128, 2, TOK], BF16)
    rstd = pg.sbuf("p2_rstd", [1, TOK], F32)
    v_h = pg.sbuf("p2_v", [128, NDC, 512], BF16)
    a_h = pg.sbuf("p2_a", [128, 64, 512], BF16)
    gs = pg.sbuf("p2_gs", [128, 2, 512], BF16)
    wsl = pg.sbuf("p2_wsl", [128, 2, NDC, 128], BF16)
    wsl2 = pg.sbuf("p2_wsl2", [128, 2, 64, 128], BF16)
    sst = pg.sbuf("p2_sst", [128, 2, 512], BF16)

    psr = [pg.psum(f"p2_ps{i}", [128, 512], F32) for i in range(4)]
    psA = pg.psum("p2_psA", [128, 512], F32)
    psB = pg.psum("p2_psB", [128, 512], F32)

    s_pr = pg.sem("p2_pr")
    s_h1 = pg.sem("p2_h1")
    s_sq = pg.sem("p2_sq")
    s_pq = pg.sem("p2_pq")
    s_eps = pg.sem("p2_eps")
    s_pst = pg.sem("p2_pst")
    s_bc = pg.sem("p2_bc")
    s_vh = pg.sem("p2_vh")
    s_wl = [pg.sem("p2_wl0"), pg.sem("p2_wl1")]
    s_w2 = [pg.sem("p2_w20"), pg.sem("p2_w21")]
    s_mm = pg.sem("p2_mm")
    s_sil = pg.sem("p2_sil")
    s_am = pg.sem("p2_am")
    s_se = pg.sem("p2_se")
    s_ss = [pg.sem("p2_ss0"), pg.sem("p2_ss1")]
    s_sr = pg.sem("p2_sr2")

    NM2 = 64

    with nc.Block() as blk:
        @blk.sync
        def _(s):
            s.dma_start(out=n2_sb[...], in_=pg.io["n2w"]).then_inc(s_pr, 16)
            s.dma_start(out=h1[...],
                        in_=h1_sp.ap().rearrange("m p t -> p m t")).then_inc(s_h1, 16)
            for th in range(2):
                for m2 in range(NDC):
                    gi = NDC * th + m2
                    s.wait_ge(s_se, gi + 1)
                    s.dma_start(out=s_sp.ap()[m2, :, 512 * th:512 * (th + 1)],
                                in_=sst[:, gi % 2, :]).then_inc(s_ss[gi % 2], 16)
            s.wait_ge(s_ss[0], 16 * NDC)
            s.wait_ge(s_ss[1], 16 * NDC)

        @blk.gpsimd
        def _(g):
            for th in range(2):
                for m in range(NM2):
                    for j, wt in ((0, gwT), (1, uwT)):
                        si = th * 128 + 2 * m + j
                        if si >= 2:
                            pg_cnt = (si - 2 + 1) if si - 2 < 128 else (144 + si - 2 - 128 + 1)
                            g.wait_ge(s_mm, pg_cnt)
                        g.dma_start(out=wsl[:, si % 2, :, :],
                                    in_=wt[:, 128 * m:128 * (m + 1)]
                                    .rearrange("(c p) m -> p c m", p=128)
                                    ).then_inc(s_wl[si % 2], 16)
                for m2 in range(NDC):
                    gi2 = th * NDC + m2
                    if gi2 >= 2:
                        g.wait_ge(s_mm, th * 144 + 128 + m2 - 1)
                    g.dma_start(out=wsl2[:, gi2 % 2, :, :],
                                in_=dwT[:, 128 * m2:128 * (m2 + 1)]
                                .rearrange("(c p) m -> p c m", p=128)
                                ).then_inc(s_w2[gi2 % 2], 16)

        @blk.scalar
        def _(sc):
            sc.wait_ge(s_h1, 16)
            for m in range(NDC):
                if m >= 2:
                    sc.wait_ge(s_pq, m - 1)
                sc.activation(sq[:, m % 2, :], h1[:, m, :], AF.Square)
                sc.drain(fusable=True).then_inc(s_sq, 1)
            sc.wait_ge(s_eps, 2)
            sc.activation(rstd[0:1, 0:512], psA[0:1, 0:512], AF.Ln)
            sc.activation(rstd[0:1, 512:1024], psB[0:1, 0:512], AF.Ln)
            sc.activation(rstd[0:1, 0:1024], rstd[0:1, 0:1024], AF.Exp,
                          scale=-0.5)
            sc.drain(fusable=True).then_inc(s_sr, 1)
            for th in range(2):
                for m in range(NM2):
                    gm = th * NM2 + m
                    sc.wait_ge(s_mm, th * 144 + 2 * m + 1)
                    if gm >= 2:
                        sc.wait_ge(s_am, gm - 1)
                    sc.activation(gs[:, gm % 2, :], psr[(2 * m) % 4][:, :], AF.Silu)
                    sc.drain(fusable=True).then_inc(s_sil, 1)

        @blk.vector
        def _(v):
            v.wait_ge(s_pr, 16)
            v.wait_ge(s_h1, 16)
            v.wait_ge(s_pst, 1)
            v.tensor_scalar(psA[0:1, 0:512], psA[0:1, 0:512],
                            1.0 / DM, EPS, ALU.mult, ALU.add)
            v.drain(fusable=True).then_inc(s_eps, 1)
            v.tensor_scalar(psB[0:1, 0:512], psB[0:1, 0:512],
                            1.0 / DM, EPS, ALU.mult, ALU.add)
            v.drain(fusable=True).then_inc(s_eps, 1)
            for th in range(2):
                ths = slice(512 * th, 512 * (th + 1))
                v.wait_ge(s_bc, th + 1)
                for m in range(NDC):
                    v.scalar_tensor_tensor(v_h[:, m, :], h1[:, m, ths],
                                           n2_sb[:, m:m + 1],
                                           psA[:, 0:512] if th == 0 else psB[:, 0:512],
                                           ALU.mult, ALU.mult)
                    v.drain(fusable=True).then_inc(s_vh, 1)
                for m in range(NM2):
                    gm = th * NM2 + m
                    v.wait_ge(s_sil, gm + 1)
                    v.wait_ge(s_mm, th * 144 + 2 * m + 2)
                    v.tensor_tensor(a_h[:, m, :], gs[:, gm % 2, :],
                                    psr[(2 * m + 1) % 4][:, :], ALU.mult)
                    v.drain(fusable=True).then_inc(s_am, 1)
                for m2 in range(NDC):
                    gi = th * NDC + m2
                    v.wait_ge(s_mm, th * 144 + 128 + m2 + 1)
                    if gi >= 2:
                        v.wait_ge(s_ss[gi % 2], 16 * (gi // 2))
                    v.tensor_tensor(sst[:, gi % 2, :], psr[m2 % 4][:, :],
                                    h1[:, m2, ths], ALU.add)
                    v.drain(fusable=True).then_inc(s_se, 1)

        @blk.tensor
        def _(t):
            for m in range(NDC):
                t.wait_ge(s_sq, m + 1)
                t.matmul(psA[0:1, 0:512], onescol[:, :], sq[:, m % 2, 0:512],
                         start=(m == 0), stop=(m == NDC - 1))
                t.matmul(psB[0:1, 0:512], onescol[:, :], sq[:, m % 2, 512:1024],
                         start=(m == 0), stop=(m == NDC - 1))
                t.drain().then_inc(s_pq, 1)
            t.drain().then_inc(s_pst, 1)
            for th in range(2):
                t.wait_ge(s_sr, 1)
                if th == 1:
                    t.wait_ge(s_vh, 16)
                pb = psA if th == 0 else psB
                t.matmul(pb[:, 0:512], onesrow[:, :],
                         rstd[0:1, 512 * th:512 * (th + 1)], start=True, stop=True)
                t.drain().then_inc(s_bc, 1)
                t.wait_ge(s_vh, 16 * (th + 1))
                for m in range(NM2):
                    g0 = th * 144 + 2 * m
                    for j in range(2):
                        gi = g0 + j
                        si = th * 128 + 2 * m + j
                        t.wait_ge(s_wl[si % 2], 16 * (si // 2 + 1))
                        bank = psr[(2 * m + j) % 4]
                        if 2 * m + j >= 4:
                            pm = 2 * m + j - 4
                            if pm % 2 == 0:
                                t.wait_ge(s_sil, th * NM2 + pm // 2 + 1)
                            else:
                                t.wait_ge(s_am, th * NM2 + pm // 2 + 1)
                        elif th == 1 and 2 * m + j < 4:
                            t.wait_ge(s_se, th * NDC)
                        for k in range(NDC):
                            t.matmul(bank[:, :], wsl[:, gi % 2, k, :],
                                     v_h[:, k, :], start=(k == 0),
                                     stop=(k == NDC - 1))
                        t.drain().then_inc(s_mm, 1)
                for m2 in range(NDC):
                    gi2 = th * NDC + m2
                    t.wait_ge(s_w2[gi2 % 2], 16 * (gi2 // 2 + 1))
                    bank = psr[m2 % 4]
                    t.wait_ge(s_am, NM2 * (th + 1))
                    if gi2 >= 4:
                        t.wait_ge(s_se, gi2 - 3)
                    for k in range(NM2):
                        t.matmul(bank[:, :], wsl2[:, gi2 % 2, k, :],
                                 a_h[:, k, :], start=(k == 0), stop=(k == NM2 - 1))
                    t.drain().then_inc(s_mm, 1)

    pg.barrier()


def emit_p3(pg):
    nc = pg.nc
    xn_sp, s_sp, x1_sp = pg.io["xn_sp"], pg.io["s_sp"], pg.io["x1_sp"]
    x_in = pg.io["x"]
    fupT, fgT, fdT = pg.io["fupT"], pg.io["fgT"], pg.io["fdT"]

    def hadamard(n):
        h = np.array([[1.0]], np.float32)
        while h.shape[0] < n:
            h = np.block([[h, h], [h, -h]])
        return h

    H128 = pg.const("h128bf", _bf16(hadamard(128)), BF16)
    identbf = pg.const("identbf", _bf16(np.eye(128, dtype=np.float32)), BF16)
    pg.flush_consts()

    sug_sb = pg.sbuf("p3_sug", [128, NDC], F32)
    wprod = pg.sbuf("p3_wprod", [128, 64], F32)
    wsd_sb = pg.sbuf("p3_wsd", [128, NDC], F32)
    sdp_sb = pg.sbuf("p3_sdp", [128, 64], F32)
    wsu_sb = pg.sbuf("p3_wsu", [128, 64], F32)
    wsg_sb = pg.sbuf("p3_wsg", [128, 64], F32)
    xf = pg.sbuf("p3_xf", [128, NDC, TOK], BF16)
    s_sb = pg.sbuf("p3_s", [128, NDC, TOK], BF16)
    tmpb = pg.sbuf("p3_tmp", [128, TOK], BF16)
    a_h = pg.sbuf("p3_a", [128, 64, 512], BF16)
    tmp5 = pg.sbuf("p3_tmp5", [128, 512], BF16)
    gs = pg.sbuf("p3_gs", [128, 2, 512], BF16)
    gsd = pg.sbuf("p3_gsd", [128, 2, 512], BF16)
    wsl = pg.sbuf("p3_wsl", [128, 2, NDC, 128], BF16)
    wsl2 = pg.sbuf("p3_wsl2", [128, 2, 32, 128], BF16)
    x_ld = pg.sbuf("p3_xld", [128, 2, DM], F32)
    x1_st = pg.sbuf("p3_x1st", [128, 2, DM], F32)

    psr = [pg.psum(f"p3_ps{i}", [128, 512], F32) for i in range(4)]
    psT = [pg.psum(f"p3_psT{i}", [128, 128], BF16) for i in range(2)]

    s_pr = pg.sem("p3_pr")
    s_ld = pg.sem("p3_ld")
    s_su = pg.sem("p3_su")
    s_bf = pg.sem("p3_bf")
    s_fwm = pg.sem("p3_fwm")
    s_fa = pg.sem("p3_fa")
    s_wv = pg.sem("p3_wv")
    s_wl = [pg.sem("p3_wl0"), pg.sem("p3_wl1")]
    s_w2 = [pg.sem("p3_w20"), pg.sem("p3_w21")]
    s_mm = pg.sem("p3_mm")
    s_sil = pg.sem("p3_sil")
    s_am = pg.sem("p3_am")
    s_bf8 = pg.sem("p3_bf8")
    s_fw8 = pg.sem("p3_fw8")
    s_fa8 = pg.sem("p3_fa8")
    s_dsl = pg.sem("p3_dsl")
    s_sg = pg.sem("p3_sg")
    s_ptr = pg.sem("p3_ptr")
    s_cmb = pg.sem("p3_cmb")
    s_xl = [pg.sem("p3_xl0"), pg.sem("p3_xl1")]
    s_xs = [pg.sem("p3_xs0"), pg.sem("p3_xs1")]

    NM2 = 64
    RS2 = float(1.0 / np.sqrt(2048.0))
    RS8 = float(1.0 / np.sqrt(8192.0))

    with nc.Block() as blk:
        @blk.sync
        def _(s):
            for name, sb in (("sug", sug_sb), ("wsd", wsd_sb), ("sdp", sdp_sb),
                             ("wsu", wsu_sb), ("wsg", wsg_sb)):
                s.dma_start(out=sb[...], in_=pg.io[name]).then_inc(s_pr, 16)
            s.dma_start(out=xf[...],
                        in_=xn_sp.ap()[:, :, 3:TOKH].rearrange("m p t -> p m t")
                        ).then_inc(s_ld, 16)
            s.dma_start(out=s_sb[...],
                        in_=s_sp.ap().rearrange("m p t -> p m t")).then_inc(s_ld, 16)
            for tc in range(NCH + 2):
                if tc >= 2:
                    s.wait_ge(s_cmb, 16 * (tc - 1))
                if tc < NCH:
                    s.dma_start(out=x_ld[:, tc % 2, :],
                                in_=x_in[3 + 128 * tc:3 + 128 * (tc + 1), :]
                                ).then_inc(s_xl[tc % 2], 16)
                if tc >= 2:
                    s.dma_start(out=x1_sp.ap()[128 * (tc - 2):128 * (tc - 1), :],
                                in_=x1_st[:, tc % 2, :]).then_inc(s_xs[tc % 2], 16)
            s.wait_ge(s_xs[0], 16 * 4)
            s.wait_ge(s_xs[1], 16 * 4)

        @blk.gpsimd
        def _(g):
            for th in range(2):
                for m in range(NM2):
                    for j, wt in ((0, fgT), (1, fupT)):
                        si = th * 128 + 2 * m + j
                        gi = th * 144 + 2 * m + j
                        if si >= 2:
                            pgi = si - 2
                            pcnt = (pgi + 1) if pgi < 128 else (144 + pgi - 128 + 1)
                            g.wait_ge(s_mm, pcnt)
                        g.dma_start(out=wsl[:, si % 2, :, :],
                                    in_=wt[:, 128 * m:128 * (m + 1)]
                                    .rearrange("(c p) m -> p c m", p=128)
                                    ).then_inc(s_wl[si % 2], 16)
                for m2 in range(NDC):
                    for kh in range(2):
                        si2 = th * 32 + 2 * m2 + kh
                        if si2 >= 2:
                            pgi = si2 - 2
                            pth, prem = pgi // 32, pgi % 32
                            pcnt = pth * 144 + 128 + prem // 2 + 1
                            g.wait_ge(s_mm, pcnt)
                        g.dma_start(out=wsl2[:, si2 % 2, :, :],
                                    in_=fdT[4096 * kh:4096 * (kh + 1),
                                            128 * m2:128 * (m2 + 1)]
                                    .rearrange("(c p) m -> p c m", p=128)
                                    ).then_inc(s_w2[si2 % 2], 16)

        @blk.scalar
        def _(sc):
            for th in range(2):
                for m in range(NM2):
                    gm = th * NM2 + m
                    sc.wait_ge(s_mm, th * 144 + 2 * m + 1)
                    if gm >= 2:
                        sc.wait_ge(s_am, gm - 1)
                    sc.activation(gs[:, gm % 2, :], psr[(2 * m) % 4][:, :], AF.Silu,
                                  scale=wsg_sb[:, m:m + 1])
                    sc.drain(fusable=True).then_inc(s_sil, 1)
                for m2 in range(NDC):
                    gi = th * NDC + m2
                    sc.wait_ge(s_mm, th * 144 + 128 + m2 + 1)
                    if gi >= 2:
                        sc.wait_ge(s_sg, gi - 1)
                    sc.activation(gsd[:, gi % 2, :], psr[m2 % 4][:, :], AF.Silu,
                                  scale=wsd_sb[:, m2:m2 + 1])
                    sc.drain(fusable=True).then_inc(s_dsl, 1)

        @blk.vector
        def _(v):
            v.wait_ge(s_pr, 16 * 5)
            v.wait_ge(s_ld, 32)
            v.tensor_tensor(wprod[:, :], wsu_sb[:, :], sdp_sb[:, :], ALU.mult)
            v.drain(fusable=True).then_inc(s_wv, 1)
            for m in range(NDC):
                v.tensor_scalar(xf[:, m, :], xf[:, m, :], sug_sb[:, m:m + 1],
                                None, ALU.mult)
                v.drain(fusable=True).then_inc(s_su, 1)
            st = 8
            while st >= 1:
                for blk0 in range(0, NDC, 2 * st):
                    for c in range(blk0, blk0 + st):
                        a, b = xf[:, c, :], xf[:, c + st, :]
                        v.tensor_tensor(tmpb[:, :], a, b, ALU.subtract)
                        v.tensor_tensor(a, a, b, ALU.add)
                        v.tensor_copy(b, tmpb[:, :])
                st //= 2
            v.drain(fusable=True).then_inc(s_bf, 1)
            for c in range(NDC):
                v.wait_ge(s_fwm, c + 1)
                v.tensor_scalar(xf[:, c, 0:512], psr[(2 * c) % 4][:, :], RS2,
                                None, ALU.mult)
                v.tensor_scalar(xf[:, c, 512:1024], psr[(2 * c + 1) % 4][:, :], RS2,
                                None, ALU.mult)
                v.drain(fusable=True).then_inc(s_fa, 1)
            for th in range(2):
                for m in range(NM2):
                    gm = th * NM2 + m
                    v.wait_ge(s_sil, gm + 1)
                    v.wait_ge(s_mm, th * 144 + 2 * m + 2)
                    v.scalar_tensor_tensor(a_h[:, m, :], psr[(2 * m + 1) % 4][:, :],
                                           wprod[:, m:m + 1], gs[:, gm % 2, :],
                                           ALU.mult, ALU.mult)
                    v.drain(fusable=True).then_inc(s_am, 1)
                st = 32
                while st >= 1:
                    for blk0 in range(0, NM2, 2 * st):
                        for c in range(blk0, blk0 + st):
                            a, b = a_h[:, c, :], a_h[:, c + st, :]
                            v.tensor_tensor(tmp5[:, :], a, b, ALU.subtract)
                            v.tensor_tensor(a, a, b, ALU.add)
                            v.tensor_copy(b, tmp5[:, :])
                    st //= 2
                v.drain(fusable=True).then_inc(s_bf8, 1)
                for c in range(NM2):
                    v.wait_ge(s_fw8, th * NM2 + c + 1)
                    v.tensor_scalar(a_h[:, c, :], psr[c % 4][:, :], RS8,
                                    None, ALU.mult)
                    v.drain(fusable=True).then_inc(s_fa8, 1)
                for m2 in range(NDC):
                    gi = th * NDC + m2
                    ths = slice(512 * th, 512 * (th + 1))
                    v.wait_ge(s_dsl, gi + 1)
                    v.tensor_tensor(s_sb[:, m2, ths], s_sb[:, m2, ths],
                                    gsd[:, gi % 2, :], ALU.mult)
                    v.drain(fusable=True).then_inc(s_sg, 1)
            for tc in range(NCH):
                for dc in range(NDC):
                    gtr = 16 * tc + dc
                    v.wait_ge(s_ptr, gtr + 1)
                    v.wait_ge(s_xl[tc % 2], 16 * (tc // 2 + 1))
                    if tc >= 2 and dc == 0:
                        v.wait_ge(s_xs[tc % 2], 16 * (tc // 2))
                    v.tensor_tensor(x1_st[:, tc % 2, 128 * dc:128 * (dc + 1)],
                                    psT[gtr % 2][:, :],
                                    x_ld[:, tc % 2, 128 * dc:128 * (dc + 1)],
                                    ALU.add)
                    v.drain(fusable=True).then_inc(s_cmb, 1)

        @blk.tensor
        def _(t):
            t.wait_ge(s_bf, 1)
            for c in range(NDC):
                if c >= 2:
                    t.wait_ge(s_fa, c - 1)
                t.matmul(psr[(2 * c) % 4][:, :], H128[:, :], xf[:, c, 0:512],
                         start=True, stop=True)
                t.matmul(psr[(2 * c + 1) % 4][:, :], H128[:, :], xf[:, c, 512:1024],
                         start=True, stop=True)
                t.drain().then_inc(s_fwm, 1)
            for th in range(2):
                t.wait_ge(s_fa, NDC)
                t.wait_ge(s_wv, 1)
                for m in range(NM2):
                    for j in range(2):
                        si = th * 128 + 2 * m + j
                        t.wait_ge(s_wl[si % 2], 16 * (si // 2 + 1))
                        bank = psr[(2 * m + j) % 4]
                        if 2 * m + j >= 4:
                            pm = 2 * m + j - 4
                            if pm % 2 == 0:
                                t.wait_ge(s_sil, th * NM2 + pm // 2 + 1)
                            else:
                                t.wait_ge(s_am, th * NM2 + pm // 2 + 1)
                        elif th == 1:
                            t.wait_ge(s_dsl, NDC)
                        for k in range(NDC):
                            ths = slice(512 * th, 512 * (th + 1))
                            t.matmul(bank[:, :], wsl[:, si % 2, k, :],
                                     xf[:, k, ths], start=(k == 0),
                                     stop=(k == NDC - 1))
                        t.drain().then_inc(s_mm, 1)
                t.wait_ge(s_bf8, th + 1)
                for c in range(NM2):
                    if c >= 4:
                        t.wait_ge(s_fa8, th * NM2 + c - 3)
                    t.matmul(psr[c % 4][:, :], H128[:, :], a_h[:, c, :],
                             start=True, stop=True)
                    t.drain().then_inc(s_fw8, 1)
                t.wait_ge(s_fa8, NM2 * (th + 1))
                for m2 in range(NDC):
                    gi2 = th * NDC + m2
                    bank = psr[m2 % 4]
                    if gi2 >= 4:
                        t.wait_ge(s_dsl, gi2 - 3)
                    for kh in range(2):
                        si2 = th * 32 + 2 * m2 + kh
                        t.wait_ge(s_w2[si2 % 2], 16 * (si2 // 2 + 1))
                        for k in range(32):
                            t.matmul(bank[:, :], wsl2[:, si2 % 2, k, :],
                                     a_h[:, 32 * kh + k, :],
                                     start=(kh == 0 and k == 0),
                                     stop=(kh == 1 and k == 31))
                    t.drain().then_inc(s_mm, 1)
            t.wait_ge(s_sg, 2 * NDC)
            for tc in range(NCH):
                for dc in range(NDC):
                    gtr = 16 * tc + dc
                    if gtr >= 2:
                        t.wait_ge(s_cmb, gtr - 1)
                    t.transpose(psT[gtr % 2][:, :],
                                s_sb[:, dc, 128 * tc:128 * (tc + 1)],
                                identbf[:, :])
                    t.drain().then_inc(s_ptr, 1)

    pg.barrier()


def emit_p4(pg):
    nc = pg.nc
    x1_sp, xn1_sp, s2_sp = pg.io["x1_sp"], pg.io["xn1_sp"], pg.io["s2_sp"]

    identbf = pg.const("identbf", _bf16(np.eye(128, dtype=np.float32)), BF16)
    identf = pg.const("identf", np.eye(128, dtype=np.float32), F32)
    onesrow = pg.const("ones1x128f", np.ones((1, 128), np.float32), F32)
    pg.flush_consts()

    x1_ld = pg.sbuf("p4_x1", [128, 2, DM], F32)
    xn1T = pg.sbuf("p4_xT", [128, NCH, DM], BF16)
    xn1f = pg.sbuf("p4_xf", [128, NDC, TOK], BF16)
    junk = pg.sbuf("p4_junk", [128, DM], F32)
    stats = pg.sbuf("p4_stats", [128, 8], F32)
    est = pg.sbuf("p4_est", [128, DM], F32)
    ebuf = pg.sbuf("p4_ebuf", [128, 2, 256], F32)
    expT = pg.sbuf("p4_expT", [128, 2, 256], BF16)
    mbc = pg.sbuf("p4_mbc", [128, DM], F32)
    m_all = pg.sbuf("p4_mall", [128, NDC], F32)
    rz_all = pg.sbuf("p4_rz", [128, NDC], F32)
    mrow = pg.sbuf("p4_mrow", [1, DM], F32)
    e_st = pg.sbuf("p4_est2", [128, 2, DM], F32)
    s2st = pg.sbuf("p4_s2st", [128, 2, TOK], BF16)

    psT = [pg.psum(f"p4_psT{i}", [128, 128], BF16) for i in range(2)]
    psO = [pg.psum(f"p4_psO{i}", [128, TOK], F32) for i in range(2)]
    psS = pg.psum("p4_psS", [128, 512], F32)
    psR = pg.psum("p4_psR", [1, 128], F32)

    E_loc = nc.dram_tensor("p4_Eloc", [DM, DM], F32, kind="Internal")
    E_sum = nc.dram_tensor("p4_Esum", [DM, DM], F32, kind="Internal")

    s_xl = [pg.sem("p4_xl0"), pg.sem("p4_xl1")]
    cs = Chain(pg.sem("p4_cs"))
    cv = Chain(pg.sem("p4_cv"))
    s_sv = pg.sem("p4_sv")
    s_ptr = pg.sem("p4_ptr")
    s_vtr = pg.sem("p4_vtr")
    s_fms = [pg.sem("p4_fms0"), pg.sem("p4_fms1")]
    s_pe = pg.sem("p4_pe")
    s_ee = pg.sem("p4_ee")
    s_es = [pg.sem("p4_es0"), pg.sem("p4_es1")]
    s_cc = pg.sem("p4_cc")
    s_sl = pg.sem("p4_sl")
    s_sx = pg.sem("p4_sx")
    s_sm = pg.sem("p4_sm")
    s_mt = pg.sem("p4_mt")
    s_mr = pg.sem("p4_mr")
    s_mb = pg.sem("p4_mb")
    s_mbe = pg.sem("p4_mbe")
    s_el = [pg.sem("p4_el0"), pg.sem("p4_el1")]
    s_esb = pg.sem("p4_esb")
    s_ex = pg.sem("p4_ex")
    s_pa = pg.sem("p4_pa")
    s_paj = pg.sem("p4_paj")
    s_ae = pg.sem("p4_ae")
    s_ss = [pg.sem("p4_ss0"), pg.sem("p4_ss1")]

    with nc.Block() as blk:
        @blk.sync
        def _(s):
            for tc in range(NCH + 2):
                if tc < NCH:
                    if tc >= 2:
                        s.wait_ge(s_sv, tc - 1)
                    s.dma_start(out=x1_ld[:, tc % 2, :],
                                in_=x1_sp.ap()[128 * tc:128 * (tc + 1), :]
                                ).then_inc(s_xl[tc % 2], 16)
                if tc >= 2:
                    s.wait_ge(s_vtr, 16 * (tc - 1))
                    s.dma_start(out=xn1_sp.ap()[:, :, 128 * (tc - 2):128 * (tc - 1)]
                                .rearrange("c p t -> p c t"),
                                in_=xn1f[:, :, 128 * (tc - 2):128 * (tc - 1)]
                                ).then_inc(s_fms[tc % 2], 16)
            for i in range(NDC):
                s.wait_ge(s_cc, 1)
                if i >= 1:
                    s.wait_ge(s_sx, i)
                s.dma_start(out=est[:, :], in_=E_sum.ap()[128 * i:128 * (i + 1), :]
                            ).then_inc(s_sl, 16)
            for g in range(8):
                for J in range(NDC):
                    li = NDC * g + J
                    if li >= 2:
                        s.wait_ge(s_ex, li - 1)
                    s.dma_start(out=ebuf[:, li % 2, :],
                                in_=E_sum.ap()[128 * J:128 * (J + 1),
                                               256 * g:256 * g + 256]
                                ).then_inc(s_el[li % 2], 16)
                if g >= 1:
                    for il in range(2):
                        gi = 2 * (g - 1) + il
                        s.wait_ge(s_ae, gi + 1)
                        s.dma_start(out=s2_sp.ap()[gi, :, :],
                                    in_=s2st[:, gi % 2, :]).then_inc(s_ss[gi % 2], 16)
            for il in range(2):
                gi = 14 + il
                s.wait_ge(s_ae, gi + 1)
                s.dma_start(out=s2_sp.ap()[gi, :, :],
                            in_=s2st[:, gi % 2, :]).then_inc(s_ss[gi % 2], 16)
            s.wait_ge(s_ss[0], 16 * 8)
            s.wait_ge(s_ss[1], 16 * 8)

        @blk.gpsimd
        def _(g):
            for m in range(NDC):
                g.wait_ge(s_ee, 4 * (m + 1))
                g.dma_start(out=E_loc.ap()[128 * m:128 * (m + 1), :],
                            in_=e_st[:, m % 2, :]).then_inc(s_es[m % 2], 16)
            g.wait_ge(s_es[0], 16 * 8)
            g.wait_ge(s_es[1], 16 * 8)
            g.collective_compute(
                "AllReduce", ALU.add,
                replica_groups=[[0, 1, 2, 3], [4, 5, 6, 7]],
                ins=[E_loc.ap().opt()], outs=[E_sum.ap().opt()],
            ).then_inc(s_cc, 1)

        @blk.scalar
        def _(sc):
            for tc in range(NCH):
                xi = x1_ld[:, tc % 2, :]
                sc.wait_ge(s_xl[tc % 2], 16 * (tc // 2 + 1))
                if tc >= 1:
                    sc.wait_ge(cv.sem, 6 * (tc - 1) + 3)
                cs.step(sc, sc.activation(junk[:, :], xi, AF.Square,
                                          accum_out=stats[:, 0:1]))
                sc.wait_ge(cv.sem, 6 * tc + 5)
                cs.step(sc, sc.activation(stats[:, 4:5], stats[:, 4:5], AF.Ln))
                cs.sync(sc)
                cs.step(sc, sc.activation(stats[:, 5:6], stats[:, 4:5], AF.Exp,
                                          scale=-0.5))
                cs.step(sc, sc.activation(junk[:, 0:1], stats[:, 0:1], AF.Copy))
            for i in range(NDC):
                sc.wait_ge(s_sm, i + 1)
                sc.activation(junk[:, :], est[:, :], AF.Exp,
                              bias=stats[:, 6:7], accum_out=stats[:, 7:8])
                sc.drain(fusable=True).then_inc(s_sx, 1)
            for g in range(8):
                for J in range(NDC):
                    li = NDC * g + J
                    sc.wait_ge(s_esb, li + 1)
                    if li >= 2:
                        sc.wait_ge(s_paj, li - 1)
                    sc.activation(expT[:, li % 2, :], ebuf[:, li % 2, :], AF.Exp)
                    sc.drain(fusable=True).then_inc(s_ex, 1)

        @blk.vector
        def _(v):
            for tc in range(NCH):
                xi = x1_ld[:, tc % 2, :]
                v.wait_ge(s_xl[tc % 2], 16 * (tc // 2 + 1))
                cv.step(v, v.reduce_sum(stats[:, 1:2], xi, axis=AX.X))
                v.wait_ge(cs.sem, 4 * tc + 1)
                cv.sync(v)
                cv.step(v, v.tensor_scalar(stats[:, 2:3], stats[:, 1:2], 1.0 / DM, None, ALU.mult))
                cv.step(v, v.tensor_scalar(stats[:, 3:4], stats[:, 0:1], 1.0 / DM, None, ALU.mult))
                cv.sync(v)
                cv.step(v, v.tensor_tensor(stats[:, 4:5], stats[:, 2:3], stats[:, 2:3], ALU.mult))
                cv.sync(v)
                cv.step(v, v.tensor_tensor(stats[:, 4:5], stats[:, 3:4], stats[:, 4:5], ALU.subtract))
                cv.sync(v)
                cv.step(v, v.tensor_scalar(stats[:, 4:5], stats[:, 4:5], EPS, None,
                                           ALU.add))
                v.wait_ge(cs.sem, 4 * tc + 3)
                if tc >= 1:
                    v.wait_ge(s_ptr, 16 * tc)
                v.tensor_scalar(xn1T[:, tc, :], xi, stats[:, 2:3], stats[:, 5:6],
                                ALU.subtract, ALU.mult)
                v.drain(fusable=True).then_inc(s_sv, 1)
                for dc in range(NDC):
                    gtr = 16 * tc + dc
                    v.wait_ge(s_ptr, gtr + 1)
                    if tc >= 2 and dc == 0:
                        v.wait_ge(s_fms[tc % 2], 16 * (tc // 2))
                    v.tensor_copy(xn1f[:, dc, 128 * tc:128 * (tc + 1)],
                                  psT[gtr % 2][:, :])
                    v.drain(fusable=True).then_inc(s_vtr, 1)
            for m in range(NDC):
                for jb in range(4):
                    gi = 4 * m + jb
                    v.wait_ge(s_pe, gi + 1)
                    if m >= 2 and jb == 0:
                        v.wait_ge(s_es[m % 2], 16 * (m // 2))
                    v.tensor_copy(e_st[:, m % 2, 512 * jb:512 * (jb + 1)],
                                  psO[gi % 2][:, 0:512])
                    v.drain(fusable=True).then_inc(s_ee, 1)
            for i in range(NDC):
                v.wait_ge(s_sl, 16 * (i + 1))
                cv.step(v, v.reduce_max(stats[:, 6:7], est[:, :], axis=AX.X))
                cv.sync(v)
                cv.step(v, v.tensor_scalar(stats[:, 6:7], stats[:, 6:7], -1.0,
                                           None, ALU.mult))
                cv.sync(v)
                cv.step(v, v.tensor_copy(m_all[:, i:i + 1], stats[:, 6:7]))
                v.drain(fusable=True).then_inc(s_sm, 1)
                v.wait_ge(s_sx, i + 1)
                cv.step(v, v.reciprocal(stats[:, 7:8], stats[:, 7:8]))
                cv.sync(v)
                cv.step(v, v.tensor_copy(rz_all[:, i:i + 1], stats[:, 7:8]))
            for i in range(NDC):
                v.wait_ge(s_mt, i + 1)
                cv.step(v, v.tensor_copy(mrow[0:1, 128 * i:128 * (i + 1)],
                                         psR[0:1, :]))
                v.drain(fusable=True).then_inc(s_mr, 1)
            for j in range(4):
                v.wait_ge(s_mb, j + 1)
                v.tensor_copy(mbc[:, 512 * j:512 * (j + 1)], psS[:, 0:512])
                v.drain(fusable=True).then_inc(s_mbe, 1)
            for g in range(8):
                for J in range(NDC):
                    li = NDC * g + J
                    v.wait_ge(s_el[li % 2], 16 * (li // 2 + 1))
                    if li >= 2:
                        v.wait_ge(s_ex, li - 1)
                    v.tensor_tensor(ebuf[:, li % 2, :], ebuf[:, li % 2, :],
                                    mbc[:, 256 * g:256 * g + 256], ALU.add)
                    v.drain(fusable=True).then_inc(s_esb, 1)
                for il in range(2):
                    gi = 2 * g + il
                    v.wait_ge(s_pa, gi + 1)
                    if gi >= 2:
                        v.wait_ge(s_ss[gi % 2], 16 * (gi // 2))
                    v.tensor_scalar(s2st[:, gi % 2, :], psO[il][:, :],
                                    rz_all[:, gi:gi + 1], None, ALU.mult)
                    v.drain(fusable=True).then_inc(s_ae, 1)

        @blk.tensor
        def _(t):
            for tc in range(NCH):
                t.wait_ge(s_sv, tc + 1)
                for dc in range(NDC):
                    gtr = 16 * tc + dc
                    if gtr >= 2:
                        t.wait_ge(s_vtr, gtr - 1)
                    t.transpose(psT[gtr % 2][:, :],
                                xn1T[:, tc, 128 * dc:128 * (dc + 1)],
                                identbf[:, :])
                    t.drain().then_inc(s_ptr, 1)
            for m in range(NDC):
                for jb in range(4):
                    gi = 4 * m + jb
                    if gi >= 2:
                        t.wait_ge(s_ee, gi - 1)
                    for tc in range(NCH):
                        t.matmul(psO[gi % 2][:, 0:512],
                                 xn1T[:, tc, 128 * m:128 * (m + 1)],
                                 xn1T[:, tc, 512 * jb:512 * (jb + 1)],
                                 start=(tc == 0), stop=(tc == NCH - 1))
                    t.drain().then_inc(s_pe, 1)
            for i in range(NDC):
                t.wait_ge(s_sm, i + 1)
                if i >= 1:
                    t.wait_ge(s_mr, i)
                t.transpose(psR[0:1, :], m_all[:, i:i + 1], identf[:, :])
                t.drain().then_inc(s_mt, 1)
            for j in range(4):
                t.wait_ge(s_mr, NDC)
                if j >= 1:
                    t.wait_ge(s_mbe, j)
                t.matmul(psS[:, 0:512], onesrow[:, :],
                         mrow[0:1, 512 * j:512 * (j + 1)], start=True, stop=True)
                t.drain().then_inc(s_mb, 1)
            t.wait_ge(s_mbe, 4)
            for g in range(8):
                for il in range(2):
                    gi = 2 * g + il
                    if gi >= 2:
                        t.wait_ge(s_ae, gi - 1)
                for J in range(NDC):
                    li = NDC * g + J
                    t.wait_ge(s_ex, li + 1)
                    for il in range(2):
                        for nb in range(2):
                            t.matmul(psO[il][:, 512 * nb:512 * (nb + 1)],
                                     expT[:, li % 2, 128 * il:128 * (il + 1)],
                                     xn1f[:, J, 512 * nb:512 * (nb + 1)],
                                     start=(J == 0), stop=(J == NDC - 1))
                    t.drain().then_inc(s_paj, 1)
                t.drain().then_inc(s_pa, 2)

    pg.barrier()


def emit_p5(pg):
    nc = pg.nc
    xn1_sp, x1_sp, s2_sp = pg.io["xn1_sp"], pg.io["x1_sp"], pg.io["s2_sp"]
    out_t = pg.io["out"]
    mgT, muT, mdT = pg.io["mgT"], pg.io["muT"], pg.io["mdT"]
    sgT, suT, sdT = pg.io["sgT"], pg.io["suT"], pg.io["sdT"]

    identbf = pg.const("identbf", _bf16(np.eye(128, dtype=np.float32)), BF16)
    identf = pg.const("identf", np.eye(128, dtype=np.float32), F32)
    sel16 = np.zeros((16, 16, 128), np.float32)
    for e in range(16):
        sel16[e, e, :] = 1.0
    sel16c = pg.const("sel16f", sel16, F32)
    pg.flush_consts()

    BIG = 1e30
    xn1f = pg.sbuf("p5_xn", [128, NDC, TOK], BF16)
    s2_sb = pg.sbuf("p5_s2", [128, NDC, TOK], BF16)
    gw_sb = pg.sbuf("p5_gw", [128, NDC, 16], BF16)
    eb_sb = pg.sbuf("p5_eb", [16, 1], F32)
    scoF = pg.sbuf("p5_sco", [16, TOK], F32)
    biF = pg.sbuf("p5_bi", [16, TOK], F32)
    scT = pg.sbuf("p5_scT", [128, NCH, 16], F32)
    biT = pg.sbuf("p5_biT", [128, NCH, 16], F32)
    wk = pg.sbuf("p5_wk", [128, 64], F32)
    msk0 = pg.sbuf("p5_msk", [128, 16], F32)
    cwTe = pg.sbuf("p5_cwT", [16, NCH, 128], F32)
    cwb = pg.sbuf("p5_cwb", [128, 2, 512], BF16)
    aE = pg.sbuf("p5_aE", [128, 2, 8, 512], BF16)
    a1 = pg.sbuf("p5_a1", [128, 512], BF16)
    gsE = pg.sbuf("p5_gsE", [128, 2, 512], BF16)
    g2 = pg.sbuf("p5_g2", [128, NDC, 512], F32)
    gg = pg.sbuf("p5_gg", [128, 2, 512], BF16)
    wsl = pg.sbuf("p5_wsl", [128, 2, NDC, 128], BF16)
    wsl2 = pg.sbuf("p5_wsl2", [128, 2, 8, 128], BF16)
    x1_ld = pg.sbuf("p5_x1", [128, 2, DM], F32)
    o_st = pg.sbuf("p5_ost", [128, 2, DM], F32)

    pga = pg.psum("p5_pga", [128, TOK], F32)
    pua = pg.psum("p5_pua", [128, TOK], F32)
    pdn = pg.psum("p5_pdn", [128, TOK], F32)
    psm = pg.psum("p5_psm", [128, 512], F32)
    psT = pg.psum("p5_psT", [128, 128], BF16)
    psTf = None

    s_ld = pg.sem("p5_ld")
    s_gw = pg.sem("p5_gw")
    s_lg = pg.sem("p5_lg")
    s_sig = pg.sem("p5_sig")
    s_bi = pg.sem("p5_bi")
    s_rt = pg.sem("p5_rt")
    s_rte = pg.sem("p5_rte")
    s_cw = pg.sem("p5_cw")
    s_cwt = pg.sem("p5_cwt")
    s_cwe = pg.sem("p5_cwe")
    s_wl = [pg.sem("p5_wl0"), pg.sem("p5_wl1")]
    s_w2 = [pg.sem("p5_w20"), pg.sem("p5_w21")]
    s_mm = pg.sem("p5_mm")
    s_md = pg.sem("p5_md")
    s_cb = pg.sem("p5_cb")
    s_cbe = pg.sem("p5_cbe")
    s_sil = pg.sem("p5_sil")
    s_a1 = pg.sem("p5_a1")
    s_am = pg.sem("p5_am")
    s_ga = pg.sem("p5_ga")
    s_fs = pg.sem("p5_fs")
    s_sg = pg.sem("p5_sg")
    s_ptr = pg.sem("p5_ptr")
    s_cmb = pg.sem("p5_cmb")
    s_xl = [pg.sem("p5_xl0"), pg.sem("p5_xl1")]
    s_os = [pg.sem("p5_os0"), pg.sem("p5_os1")]
    cv = Chain(pg.sem("p5_cv"))

    GU_H = 16 + 16 * 16
    DN_H = 16 + 8 * 16

    def gu_cnt(th, ei, fc, j):
        base = th * GU_H
        if ei < 0:
            return base + 2 * fc + j + 1
        return base + 16 + ei * 16 + 2 * fc + j + 1

    def dn_cnt(th, pi, m2):
        base = th * DN_H
        if pi < 0:
            return base + m2 + 1
        return base + 16 + pi * 16 + m2 + 1

    with nc.Block() as blk:
        @blk.sync
        def _(s):
            s.dma_start(out=eb_sb[...], in_=pg.io["eb16"]).then_inc(s_ld, 16)
            s.dma_start(out=xn1f[...],
                        in_=xn1_sp.ap().rearrange("m p t -> p m t")).then_inc(s_ld, 16)
            s.dma_start(out=s2_sb[...],
                        in_=s2_sp.ap().rearrange("m p t -> p m t")).then_inc(s_ld, 16)
            for tc in range(NCH + 2):
                if tc < NCH:
                    if tc >= 2:
                        s.wait_ge(s_cmb, 16 * (tc - 1))
                    s.dma_start(out=x1_ld[:, tc % 2, :],
                                in_=x1_sp.ap()[128 * tc:128 * (tc + 1), :]
                                ).then_inc(s_xl[tc % 2], 16)
                if tc >= 2:
                    s.wait_ge(s_cmb, 16 * (tc - 1))
                    s.dma_start(out=out_t[128 * (tc - 2):128 * (tc - 1), :],
                                in_=o_st[:, tc % 2, :]).then_inc(s_os[tc % 2], 16)
            s.wait_ge(s_os[0], 16 * 4)
            s.wait_ge(s_os[1], 16 * 4)

        @blk.gpsimd
        def _(g):
            g.dma_start(out=gw_sb[...],
                        in_=pg.io["gw1T"].rearrange("(c p) e -> p c e", p=128)
                        ).then_inc(s_gw, 16)
            def gu_slab_consumer(si):
                thp, r = divmod(si, 272)
                eip = r // 16 - 1
                fcp, jp = (r % 16) // 2, (r % 16) % 2
                return gu_cnt(thp, eip, fcp, jp)

            def dn_slab_consumer(si2):
                thp, r = divmod(si2, 272)
                if r < 16:
                    return dn_cnt(thp, -1, r)
                r2 = r - 16
                return dn_cnt(thp, r2 // 32, (r2 % 32) // 2)

            def emit_gu_slabs(g, th, ei):
                for fc in range(8):
                    for j in range(2):
                        si = th * 272 + (ei + 1) * 16 + 2 * fc + j
                        if ei < 0:
                            wt = (sgT if j == 0 else suT)[
                                :, 128 * fc:128 * (fc + 1)]
                        else:
                            wt = (mgT if j == 0 else muT)[
                                ei, :, 128 * fc:128 * (fc + 1)]
                        if si >= 2:
                            g.wait_ge(s_mm, gu_slab_consumer(si - 2))
                        g.dma_start(out=wsl[:, si % 2, :, :],
                                    in_=wt.rearrange("(c p) m -> p c m", p=128)
                                    ).then_inc(s_wl[si % 2], 16)

            def emit_dn_slabs(g, th, pi):
                for m2 in range(NDC):
                    for eo in range(2 if pi >= 0 else 1):
                        if pi < 0:
                            si2 = th * 272 + m2
                            wt = sdT[:, 128 * m2:128 * (m2 + 1)]
                        else:
                            si2 = th * 272 + 16 + 32 * pi + 2 * m2 + eo
                            wt = mdT[2 * pi + eo, :, 128 * m2:128 * (m2 + 1)]
                        if si2 >= 2:
                            g.wait_ge(s_md, dn_slab_consumer(si2 - 2))
                        g.dma_start(out=wsl2[:, si2 % 2, :, :],
                                    in_=wt.rearrange("(c p) m -> p c m", p=128)
                                    ).then_inc(s_w2[si2 % 2], 16)

            for th in range(2):
                emit_gu_slabs(g, th, -1)
                emit_dn_slabs(g, th, -1)
                for pi in range(8):
                    emit_gu_slabs(g, th, 2 * pi)
                    emit_gu_slabs(g, th, 2 * pi + 1)
                    emit_dn_slabs(g, th, pi)

        @blk.scalar
        def _(sc):
            sc.wait_ge(s_lg, 1)
            sc.activation(scoF[:, 0:TOK], pga[0:16, 0:TOK], AF.Sigmoid)
            sc.drain(fusable=True).then_inc(s_sig, 1)
            for th in range(2):
                for ei in range(-1, 16):
                    for fc in range(8):
                        gid = th * 136 + (ei + 1) * 8 + fc
                        sc.wait_ge(s_mm, gu_cnt(th, ei, fc, 0))
                        if gid >= 2:
                            sc.wait_ge(s_a1, gid - 1)
                        sc.activation(gsE[:, gid % 2, :], pga[:, 0:512]
                                      if fc % 2 == 0 else pga[:, 512:1024], AF.Silu)
                        sc.drain(fusable=True).then_inc(s_sil, 1)
                for m2 in range(NDC):
                    fid = th * NDC + m2
                    sc.wait_ge(s_ga, 144 * (th + 1))
                    if fid >= 2:
                        sc.wait_ge(s_sg, fid - 1)
                    sc.activation(gg[:, fid % 2, :], g2[:, m2, :], AF.Silu)
                    sc.drain(fusable=True).then_inc(s_fs, 1)

        @blk.vector
        def _(v):
            v.wait_ge(s_ld, 48)
            v.wait_ge(s_sig, 1)
            v.tensor_scalar(biF[:, :], scoF[:, :], eb_sb[:, 0:1], None, ALU.add)
            v.drain(fusable=True).then_inc(s_bi, 1)
            for tc in range(NCH):
                for j in range(2):
                    gi = 2 * tc + j
                    v.wait_ge(s_rt, gi + 1)
                    dst = scT if j == 0 else biT
                    v.tensor_copy(dst[:, tc, :], psm[:, 0:16])
                    v.drain(fusable=True).then_inc(s_rte, 1)
            for tc in range(NCH):
                v.wait_ge(s_rte, 2 * tc + 2)
                if tc >= 1:
                    v.wait_ge(s_cwt, tc)
                b3 = biT[:, tc, :].rearrange("p (g s) -> p g s", g=4)
                W = wk
                cv.step(v, v.tensor_tensor(W[:, 0:4], b3[:, :, 0], b3[:, :, 1], ALU.max))
                cv.step(v, v.tensor_tensor(W[:, 4:8], b3[:, :, 0], b3[:, :, 1], ALU.min))
                cv.step(v, v.tensor_tensor(W[:, 8:12], b3[:, :, 2], b3[:, :, 3], ALU.max))
                cv.step(v, v.tensor_tensor(W[:, 12:16], b3[:, :, 2], b3[:, :, 3], ALU.min))
                cv.sync(v)
                cv.step(v, v.tensor_tensor(W[:, 16:20], W[:, 0:4], W[:, 8:12], ALU.max))
                cv.step(v, v.tensor_tensor(W[:, 20:24], W[:, 0:4], W[:, 8:12], ALU.min))
                cv.step(v, v.tensor_tensor(W[:, 24:28], W[:, 4:8], W[:, 12:16], ALU.max))
                cv.sync(v)
                cv.step(v, v.tensor_tensor(W[:, 28:32], W[:, 20:24], W[:, 24:28], ALU.max))
                cv.sync(v)
                cv.step(v, v.tensor_tensor(W[:, 32:36], W[:, 16:20], W[:, 28:32], ALU.add))
                cv.sync(v)
                cv.step(v, v.reduce_max(W[:, 36:37], W[:, 32:36], axis=AX.X))
                cv.sync(v)
                cv.step(v, v.tensor_scalar(W[:, 40:44], W[:, 32:36], W[:, 36:37],
                                           None, ALU.is_equal))
                cv.sync(v)
                cv.step(v, v.tensor_scalar(W[:, 40:44], W[:, 40:44], BIG, None, ALU.mult))
                cv.sync(v)
                cv.step(v, v.tensor_tensor(W[:, 44:48], W[:, 32:36], W[:, 40:44], ALU.subtract))
                cv.sync(v)
                cv.step(v, v.reduce_max(W[:, 36:37], W[:, 44:48], axis=AX.X))
                cv.sync(v)
                cv.step(v, v.tensor_scalar(W[:, 48:52], W[:, 32:36], W[:, 36:37],
                                           None, ALU.is_ge))
                cv.sync(v)
                m0 = msk0[:, :].rearrange("p (g s) -> p g s", g=4)
                for sslot in range(4):
                    cv.step(v, v.tensor_scalar(m0[:, :, sslot], W[:, 48:52], 1.0,
                                               None, ALU.subtract))
                cv.sync(v)
                cv.step(v, v.tensor_scalar(msk0[:, :], msk0[:, :], BIG, None, ALU.mult))
                cv.sync(v)
                cv.step(v, v.tensor_tensor(msk0[:, :], biT[:, tc, :], msk0[:, :], ALU.add))
                cv.sync(v)
                cv.step(v, v.tensor_copy(W[:, 48:64], msk0[:, :]))
                cv.sync(v)
                for it in range(3):
                    cv.step(v, v.reduce_max(W[:, 36:37], W[:, 48:64], axis=AX.X))
                    cv.sync(v)
                    cv.step(v, v.tensor_scalar(W[:, 0:16], W[:, 48:64], W[:, 36:37],
                                               None, ALU.is_equal))
                    cv.sync(v)
                    cv.step(v, v.tensor_scalar(W[:, 0:16], W[:, 0:16], BIG, None, ALU.mult))
                    cv.sync(v)
                    cv.step(v, v.tensor_tensor(W[:, 48:64], W[:, 48:64], W[:, 0:16],
                                               ALU.subtract))
                    cv.sync(v)
                cv.step(v, v.reduce_max(W[:, 36:37], W[:, 48:64], axis=AX.X))
                cv.sync(v)
                cv.step(v, v.tensor_scalar(W[:, 0:16], msk0[:, :], W[:, 36:37],
                                           None, ALU.is_ge))
                cv.sync(v)
                cv.step(v, v.tensor_tensor(W[:, 16:32], scT[:, tc, :], W[:, 0:16],
                                           ALU.mult))
                cv.sync(v)
                cv.step(v, v.reduce_sum(W[:, 36:37], W[:, 16:32], axis=AX.X))
                cv.sync(v)
                cv.step(v, v.tensor_scalar(W[:, 36:37], W[:, 36:37], 1e-20, None, ALU.add))
                cv.sync(v)
                cv.step(v, v.reciprocal(W[:, 37:38], W[:, 36:37]))
                cv.sync(v)
                cv.step(v, v.tensor_scalar(W[:, 16:32], W[:, 16:32], W[:, 37:38],
                                           2.5, ALU.mult, ALU.mult))
                v.drain(fusable=True).then_inc(s_cw, 1)
                v.wait_ge(s_cwt, tc + 1)
                v.tensor_copy(cwTe[:, tc, :], psm[0:16, 16:144])
                v.drain(fusable=True).then_inc(s_cwe, 1)
            for th in range(2):
                ths = slice(512 * th, 512 * (th + 1))
                for ei in range(-1, 16):
                    if ei >= 0:
                        v.wait_ge(s_md, th * 144 + 16 + (ei // 2) * 16)
                    if ei >= 0:
                        v.wait_ge(s_cb, th * 16 + ei + 1)
                        v.tensor_copy(cwb[:, ei % 2, :], psm[:, 0:512])
                        v.drain(fusable=True).then_inc(s_cbe, 1)
                    for fc in range(8):
                        gid = th * 136 + (ei + 1) * 8 + fc
                        v.wait_ge(s_sil, gid + 1)
                        v.wait_ge(s_mm, gu_cnt(th, ei, fc, 1))
                        v.tensor_tensor(a1[:, :], gsE[:, gid % 2, :],
                                        pua[:, 0:512] if fc % 2 == 0
                                        else pua[:, 512:1024], ALU.mult)
                        v.drain(fusable=True).then_inc(s_a1, 1)
                        if ei < 0:
                            v.tensor_copy(aE[:, 0, fc, :], a1[:, :])
                        else:
                            v.tensor_tensor(aE[:, ei % 2, fc, :], a1[:, :],
                                            cwb[:, ei % 2, :], ALU.mult)
                        v.drain(fusable=True).then_inc(s_am, 1)
                    if ei == -1:
                        for m2 in range(NDC):
                            v.wait_ge(s_md, dn_cnt(th, -1, m2))
                            v.tensor_copy(g2[:, m2, :], pdn[:, 0:512]
                                          if m2 % 2 == 0 else pdn[:, 512:1024])
                            v.drain(fusable=True).then_inc(s_ga, 1)
                    elif ei % 2 == 1:
                        pi = ei // 2
                        for m2 in range(NDC):
                            v.wait_ge(s_md, dn_cnt(th, pi, m2))
                            v.tensor_tensor(g2[:, m2, :], g2[:, m2, :],
                                            pdn[:, 0:512] if m2 % 2 == 0
                                            else pdn[:, 512:1024], ALU.add)
                            v.drain(fusable=True).then_inc(s_ga, 1)
                for m2 in range(NDC):
                    fid = th * NDC + m2
                    v.wait_ge(s_fs, fid + 1)
                    v.tensor_tensor(s2_sb[:, m2, ths], s2_sb[:, m2, ths],
                                    gg[:, fid % 2, :], ALU.mult)
                    v.drain(fusable=True).then_inc(s_sg, 1)
            for tc in range(NCH):
                for dc in range(NDC):
                    gtr = 16 * tc + dc
                    v.wait_ge(s_ptr, gtr + 1)
                    v.wait_ge(s_xl[tc % 2], 16 * (tc // 2 + 1))
                    if tc >= 2 and dc == 0:
                        v.wait_ge(s_os[tc % 2], 16 * (tc // 2))
                    v.tensor_tensor(o_st[:, tc % 2, 128 * dc:128 * (dc + 1)],
                                    psT[:, :], x1_ld[:, tc % 2, 128 * dc:128 * (dc + 1)],
                                    ALU.add)
                    v.drain(fusable=True).then_inc(s_cmb, 1)

        @blk.tensor
        def _(t):
            t.wait_ge(s_gw, 16)
            t.wait_ge(s_ld, 32)
            for k in range(NDC):
                for nb in range(2):
                    t.matmul(pga[0:16, 512 * nb:512 * (nb + 1)], gw_sb[:, k, :],
                             xn1f[:, k, 512 * nb:512 * (nb + 1)],
                             start=(k == 0), stop=(k == NDC - 1))
            t.drain().then_inc(s_lg, 1)
            for tc in range(NCH):
                for j in range(2):
                    gi = 2 * tc + j
                    src = scoF if j == 0 else biF
                    if j == 1:
                        t.wait_ge(s_bi, 1)
                    else:
                        t.wait_ge(s_sig, 1)
                    if gi >= 1:
                        t.wait_ge(s_rte, gi)
                    t.transpose(psm[0:128, 0:16], src[:, 128 * tc:128 * (tc + 1)],
                                identf[0:16, 0:16])
                    t.drain().then_inc(s_rt, 1)
            for tc in range(NCH):
                t.wait_ge(s_cw, tc + 1)
                if tc >= 1:
                    t.wait_ge(s_cwe, tc)
                t.transpose(psm[0:16, 16:144], wk[:, 16:32], identf[:, :])
                t.drain().then_inc(s_cwt, 1)
            for th in range(2):
                ths = slice(512 * th, 512 * (th + 1))
                for ei in range(-1, 16):
                    if ei >= 0:
                        t.wait_ge(s_cwe, NCH)
                        if th * 16 + ei >= 2:
                            t.wait_ge(s_cbe, th * 16 + ei - 1)
                        cwv = cwTe[:, :, :].rearrange("e c t -> e (c t)")
                        t.matmul(psm[:, 0:512], sel16c[:, ei, :], cwv[:, ths],
                                 start=True, stop=True)
                        t.drain().then_inc(s_cb, 1)
                    for fc in range(8):
                        for j in range(2):
                            si = th * 272 + (ei + 1) * 16 + 2 * fc + j
                            t.wait_ge(s_wl[si % 2], 16 * (si // 2 + 1))
                            gid = th * 136 + (ei + 1) * 8 + fc
                            bank = pga if j == 0 else pua
                            half = slice(0, 512) if fc % 2 == 0 else slice(512, 1024)
                            if gid >= 2:
                                if j == 0:
                                    t.wait_ge(s_sil, gid - 1)
                                else:
                                    t.wait_ge(s_a1, gid - 1)
                            elif th == 0 and ei == -1 and fc < 2 and j == 0:
                                t.wait_ge(s_sig, 1)
                            for k in range(NDC):
                                t.matmul(bank[:, half], wsl[:, si % 2, k, :],
                                         xn1f[:, k, ths], start=(k == 0),
                                         stop=(k == NDC - 1))
                            t.drain().then_inc(s_mm, 1)
                    if ei == -1:
                        for m2 in range(NDC):
                            si2 = th * 272 + m2
                            t.wait_ge(s_w2[si2 % 2], 16 * (si2 // 2 + 1))
                            t.wait_ge(s_am, th * 136 + 8)
                            half = slice(0, 512) if m2 % 2 == 0 else slice(512, 1024)
                            gseq = th * 144 + m2
                            if gseq >= 2:
                                t.wait_ge(s_ga, gseq - 1)
                            for k in range(8):
                                t.matmul(pdn[:, half], wsl2[:, si2 % 2, k, :],
                                         aE[:, 0, k, :], start=(k == 0),
                                         stop=(k == 7))
                            t.drain().then_inc(s_md, 1)
                    elif ei % 2 == 1:
                        pi = ei // 2
                        for m2 in range(NDC):
                            si2 = th * 272 + 16 + 32 * pi + 2 * m2
                            t.wait_ge(s_am, th * 136 + (ei + 2) * 8)
                            half = slice(0, 512) if m2 % 2 == 0 else slice(512, 1024)
                            gseq = th * 144 + 16 + pi * 16 + m2
                            if gseq >= 2:
                                t.wait_ge(s_ga, gseq - 1)
                            for eo in range(2):
                                sii = si2 + eo
                                t.wait_ge(s_w2[sii % 2], 16 * (sii // 2 + 1))
                                for k in range(8):
                                    t.matmul(pdn[:, half], wsl2[:, sii % 2, k, :],
                                             aE[:, eo, k, :],
                                             start=(eo == 0 and k == 0),
                                             stop=(eo == 1 and k == 7))
                            t.drain().then_inc(s_md, 1)
            t.wait_ge(s_sg, 2 * NDC)
            for tc in range(NCH):
                for dc in range(NDC):
                    gtr = 16 * tc + dc
                    if gtr >= 1:
                        t.wait_ge(s_cmb, gtr)
                    t.transpose(psT[:, :], s2_sb[:, dc, 128 * tc:128 * (tc + 1)],
                                identbf[:, :])
                    t.drain().then_inc(s_ptr, 1)

    pg.barrier()


def _col128(v):
    v = np.ascontiguousarray(v, dtype=np.float32)
    return np.ascontiguousarray(v.reshape(-1, 128).T)


def host_layout(inputs):
    T = lambda a: np.ascontiguousarray(np.asarray(a, dtype=np.float32).T)
    w = {}
    w["ipT"] = T(inputs["L0_attn_in_proj_w"])
    w["opT"] = T(inputs["L0_attn_out_proj_w"])
    w["gwT"] = T(inputs["L0_attn_gate_w"])
    w["uwT"] = T(inputs["L0_attn_up_w"])
    w["dwT"] = T(inputs["L0_attn_down_w"])
    w["fupT"] = T(inputs["L0_ffn_W_up"])
    w["fgT"] = T(inputs["L0_ffn_W_gate"])
    w["fdT"] = T(inputs["L0_ffn_W_down"])
    w["mgT"] = np.ascontiguousarray(np.asarray(inputs["L1_ffn_gate_proj"], dtype=np.float32).transpose(0, 2, 1))
    w["muT"] = np.ascontiguousarray(np.asarray(inputs["L1_ffn_up_proj"], dtype=np.float32).transpose(0, 2, 1))
    w["mdT"] = np.ascontiguousarray(np.asarray(inputs["L1_ffn_down_proj"], dtype=np.float32).transpose(0, 2, 1))
    w["sgT"] = T(inputs["L1_ffn_shared_gate"])
    w["suT"] = T(inputs["L1_ffn_shared_up"])
    w["sdT"] = T(inputs["L1_ffn_shared_down"])
    w["gw1T"] = T(inputs["L1_ffn_gate_weight"])
    w["eb16"] = np.ascontiguousarray(np.asarray(inputs["L1_ffn_expert_bias"], dtype=np.float32).reshape(16, 1))
    w["zb128"] = _col128(inputs["L0_attn_z_bias"])
    w["cb128"] = _col128(inputs["L0_attn_conv_b"])
    cw = np.ascontiguousarray(np.asarray(inputs["L0_attn_conv_w"], dtype=np.float32))
    w["cw128"] = np.ascontiguousarray(cw.reshape(48, 128, 4).transpose(1, 0, 2))
    w["n1w"] = _col128(inputs["L0_attn_norm1_w"])
    w["n2w"] = _col128(inputs["L0_attn_norm2_w"])
    w["sug"] = _col128(inputs["L0_ffn_SU_ug"])
    w["sdp"] = _col128(inputs["L0_ffn_SU_dp"])
    wug = np.asarray(inputs["L0_ffn_Wscale_ug"], dtype=np.float32)
    w["wsu"] = _col128(wug[:DFF])
    w["wsg"] = _col128(wug[DFF:])
    w["wsd"] = _col128(inputs["L0_ffn_Wscale_dp"])
    D = np.asarray(inputs["L0_attn_D"], dtype=np.float32)
    w["D64"] = _col128(np.repeat(D, HD))
    xf = np.asarray(inputs["L0_attn_hidden_states"], dtype=np.float32)
    xs = []
    for core in range(8):
        b, q = core // 4, core % 4
        lo = TOK * q
        if q == 0:
            sl = np.concatenate([np.zeros((3, DM), np.float32), xf[b, 0:TOK + 0]], axis=0)
        else:
            sl = xf[b, lo - 3:lo + TOK]
        xs.append(np.ascontiguousarray(sl))
    return w, xs


EMITTERS = {"p0": emit_p0, "p1": emit_p1, "p2": emit_p2, "p3": emit_p3, "p4": emit_p4, "p5": emit_p5}


def build_program(phases):
    pg = Pg()
    nc = pg.nc
    produced = {BOUND[b][2]: None for b in BOUND}
    inside = set(phases)
    if any("x" in CONSUMES[p] for p in phases):
        pg.io["x"] = nc.dram_tensor("x", [TOKH, DM], F32, kind="ExternalInput").ap()
    if "p1" in phases:
        pg.io["gmask"] = nc.dram_tensor("gmask", [64, 8], F32, kind="ExternalInput").ap()
    if "p5" in phases:
        pg.io["out"] = nc.dram_tensor("out", [TOK, DM], F32, kind="ExternalOutput").ap()
    for name, (shape, dt, producer) in BOUND.items():
        used = any(name in CONSUMES[p] for p in phases)
        made = producer in inside
        if not (used or made):
            continue
        if made and used:
            kind = "Internal"
        elif made:
            kind = "ExternalOutput"
        else:
            kind = "ExternalInput"
        pg.io[name] = nc.dram_tensor(name, shape, dt, kind=kind)
    for p in phases:
        for name, shape in WEIGHTS[p].items():
            if name not in pg.io:
                pg.io[name] = nc.dram_tensor(name, shape, F32, kind="ExternalInput").ap()
    for p in phases:
        EMITTERS[p](pg)
    return pg


def _run(pg, in_map, n_cores=8):
    res = run_bass_kernel_spmd(pg.nc, in_map, core_ids=list(range(n_cores)))
    return res


def kernel(**inputs):
    w, xs = host_layout(inputs)
    if FUSED:
        groups = [PHASE_ORDER]
    else:
        groups = [[p] for p in PHASE_ORDER]

    state = [dict() for _ in range(8)]
    out = None
    for phases in groups:
        pg = build_program(phases)
        in_maps = []
        for core in range(8):
            m = {}
            if any("x" in CONSUMES[p] for p in phases):
                m["x"] = xs[core]
            if "p1" in phases:
                q = core % 4
                g = np.zeros((64, 8), np.float32)
                for r in range(4):
                    g[:, r] = 1.0 if r < q else 0.0
                    g[:, 4 + r] = 0.0 if r < q else 1.0
                m["gmask"] = g
            for p in phases:
                for name in WEIGHTS[p]:
                    m[name] = w[name]
            for name in BOUND:
                if name in pg.io and any(name in CONSUMES[p] for p in phases) \
                        and BOUND[name][2] not in phases:
                    m[name] = state[core][name]
            in_maps.append(m)
        res = _run(pg, in_maps)
        for core in range(8):
            r = res.results[core]
            for name in BOUND:
                if name in r:
                    state[core][name] = r[name]
            if "out" in r:
                if out is None:
                    out = np.zeros((2, 4096, DM), np.float32)
                b, q = core // 4, core % 4
                out[b, TOK * q:TOK * (q + 1)] = r["out"]
        pg.stack.close()
    return out


# version 02
# speedup vs baseline: 1.0724x; 1.0724x over previous
import contextlib
import numpy as np
import concourse.bass as bass
import concourse.mybir as mybir
from concourse.bass_utils import run_bass_kernel_spmd

F32 = mybir.dt.float32
BF16 = mybir.dt.bfloat16
AF = mybir.ActivationFunctionType
ALU = mybir.AluOpType
AX = mybir.AxisListType

DM = 2048
DIP = 8224
CONVD = 6144
NH = 32
DS = 64
HD = 64
DFF = 8192
EPS = 1e-5
TOK = 1024
TOKH = TOK + 3
NCH = 8
NDC = DM // 128
NEG = -1e30

FUSED = True
DBG_P1 = False

BOUND = {
    "xn_sp":  ([NDC, 128, TOKH], BF16, "p0"),
    "u0_sp":  ([NDC, 128, TOKH], BF16, "p0"),
    "h1_sp":  ([NDC, 128, TOK], BF16, "p1"),
    "s_sp":   ([NDC, 128, TOK], BF16, "p2"),
    "x1_sp":  ([TOK, DM], F32, "p3"),
    "xn1_sp": ([NDC, 128, TOK], BF16, "p4"),
    "s2_sp":  ([NDC, 128, TOK], BF16, "p4"),
}

CONSUMES = {
    "p0": ["x"],
    "p1": ["xn_sp", "u0_sp"],
    "p2": ["h1_sp"],
    "p3": ["xn_sp", "s_sp", "x"],
    "p4": ["x1_sp"],
    "p5": ["xn1_sp", "x1_sp", "s2_sp"],
}

WEIGHTS = {
    "p0": {},
    "p1": {"ipT": [DM, DIP], "opT": [DM, DM], "cw128": [128, 48, 4],
           "cb128": [128, 48], "zb128": [128, NDC], "n1w": [128, NDC],
           "D64": [128, NDC]},
    "p2": {"gwT": [DM, DFF], "uwT": [DM, DFF], "dwT": [DFF, DM],
           "n2w": [128, NDC]},
    "p3": {"fupT": [DM, DFF], "fgT": [DM, DFF], "fdT": [DFF, DM],
           "sug": [128, NDC], "sdp": [128, 64], "wsu": [128, 64],
           "wsg": [128, 64], "wsd": [128, NDC]},
    "p4": {},
    "p5": {"mgT": [16, DM, 1024], "muT": [16, DM, 1024], "mdT": [16, 1024, DM],
           "sgT": [DM, 1024], "suT": [DM, 1024], "sdT": [1024, DM],
           "gw1T": [DM, 16], "eb16": [16, 1]},
}

PHASE_ORDER = ["p0", "p1", "p2", "p3", "p4", "p5"]


class Pg:

    def __init__(self, detect_races=False):
        self.nc = bass.Bass("TRN2", target_bir_lowering=False,
                            detect_race_conditions=detect_races)
        self.stack = contextlib.ExitStack()
        self._const_sb = {}
        self._sem_n = 0
        self._phase_stack = None
        self._phase_sems = []
        self.bar_sem = self.nc.alloc_semaphore("barrier_sem")
        self.bar_count = 0
        self.io = {}
        self.const_loads = []

    def begin_phase(self):
        self._phase_stack = contextlib.ExitStack()
        self._phase_sems = []
        self._const_sb = {}

    def end_phase(self):
        self.barrier()
        if self._phase_sems:
            with self.nc.Block() as blk:
                sems = list(self._phase_sems)

                @blk.gpsimd
                def _(g):
                    pass
            self.nc.clear_and_free_semaphores(sems)
            self.barrier()
        self._phase_sems = []
        if self._phase_stack is not None:
            self._phase_stack.close()
            self._phase_stack = None

    def sem(self, name):
        self._sem_n += 1
        h = self.nc.alloc_semaphore(f"{name}_{self._sem_n}")
        self._phase_sems.append(h)
        return h

    def sbuf(self, name, shape, dtype):
        st = self._phase_stack if self._phase_stack is not None else self.stack
        return st.enter_context(self.nc.sbuf_tensor(name, shape, dtype))

    def psum(self, name, shape, dtype=F32):
        st = self._phase_stack if self._phase_stack is not None else self.stack
        return st.enter_context(self.nc.psum_tensor(name, shape, dtype))

    _const_seq = 0

    def const(self, name, arr, dtype):
        if name in self._const_sb:
            return self._const_sb[name]
        arr = np.ascontiguousarray(arr)
        Pg._const_seq += 1
        uname = f"{name}_{Pg._const_seq}"
        dr = self.nc.inline_tensor(arr, uname)
        sb = self.sbuf("c_" + uname, list(arr.shape), dtype)
        self.const_loads.append((sb, dr))
        self._const_sb[name] = sb
        return sb

    def flush_consts(self):
        if not self.const_loads:
            return
        nc = self.nc
        loads = self.const_loads
        self.const_loads = []
        csem = self.sem("csem")
        n = len(loads)
        with nc.Block() as blk:
            @blk.sync
            def _(s):
                for sb, dr in loads:
                    s.dma_start(out=sb[...], in_=dr.ap()).then_inc(csem, 16)
                s.wait_ge(csem, 16 * n)
        self.barrier()

    def barrier(self):
        nc = self.nc
        self.bar_count += 5
        target = self.bar_count
        s = self.bar_sem
        with nc.Block() as blk:
            for en in ("sync", "gpsimd", "tensor", "vector", "scalar"):
                def mk(e, tgt=target):
                    e.sem_inc(s, 1)
                    e.wait_ge(s, tgt)
                getattr(blk, en)(mk)


def _bf16(a):
    import ml_dtypes
    return np.asarray(a, dtype=ml_dtypes.bfloat16)


class Chain:

    def __init__(self, sem):
        self.sem = sem
        self.n = 0

    def step(self, eng, inst):
        self.n += 1
        eng.drain(fusable=True).then_inc(self.sem, 1)
        return inst

    def sync(self, eng):
        eng.wait_ge(self.sem, self.n)


def emit_p0(pg):
    nc = pg.nc
    x = pg.io["x"]
    xn_sp = pg.io["xn_sp"]
    u0_sp = pg.io["u0_sp"]

    ident = pg.const("identbf", _bf16(np.eye(128, dtype=np.float32)), BF16)
    pg.flush_consts()

    chunks = [(0, 3)] + [(3 + 128 * c, 128) for c in range(NCH)]
    NCHK = len(chunks)

    x_tm = pg.sbuf("p0_x", [128, 2, DM], F32)
    xn_tm = pg.sbuf("p0_xn", [128, DM], BF16)
    u_tm = pg.sbuf("p0_u", [128, DM], BF16)
    junk = pg.sbuf("p0_junk", [128, DM], F32)
    st_xn = pg.sbuf("p0_sxn", [128, 2, NDC, 128], BF16)
    st_u = pg.sbuf("p0_su", [128, 2, NDC, 128], BF16)
    stats = pg.sbuf("p0_stats", [128, 8], F32)
    ps_tr = [pg.psum(f"p0_ps{i}", [128, 128], BF16) for i in range(4)]

    sx = [pg.sem("p0_sx0"), pg.sem("p0_sx1")]
    sv = pg.sem("p0_sv")
    sp = pg.sem("p0_sp")
    se = pg.sem("p0_se")
    ss = [pg.sem("p0_ss0"), pg.sem("p0_ss1")]
    cs = Chain(pg.sem("p0_cs"))
    cv = Chain(pg.sem("p0_cv"))

    with nc.Block() as blk:
        @blk.sync
        def _(s):
            for i, (off, w) in enumerate(chunks):
                if i >= 2:
                    s.wait_ge(sv, 2 * (i - 2) + 1)
                    s.wait_ge(cs.sem, 6 * (i - 2) + 1)
                s.dma_start(out=x_tm[0:w, i % 2, :], in_=x[off:off + w, :]
                            ).then_inc(sx[i % 2], 16)

        @blk.gpsimd
        def _(g):
            for i, (off, w) in enumerate(chunks):
                g.wait_ge(se, 32 * (i + 1))
                g.dma_start(out=xn_sp.ap()[:, :, off:off + w].rearrange("c p t -> p c t"),
                            in_=st_xn[:, i % 2, :, 0:w]).then_inc(ss[i % 2], 16)
                g.dma_start(out=u0_sp.ap()[:, :, off:off + w].rearrange("c p t -> p c t"),
                            in_=st_u[:, i % 2, :, 0:w]).then_inc(ss[i % 2], 16)
            g.wait_ge(ss[0], 32 * ((NCHK + 1) // 2))
            g.wait_ge(ss[1], 32 * (NCHK // 2))

        @blk.scalar
        def _(sc):
            for i, (off, w) in enumerate(chunks):
                xi = x_tm[0:w, i % 2, :]
                sc.wait_ge(sx[i % 2], 16 * (i // 2 + 1))
                if i >= 1:
                    sc.wait_ge(cv.sem, 8 * (i - 1) + 3)
                cs.step(sc, sc.activation(junk[0:w, :], xi, AF.Square,
                                      accum_out=stats[0:w, 0:1]))
                sc.wait_ge(cv.sem, 8 * i + 6)
                cs.step(sc, sc.activation(stats[0:w, 4:5], stats[0:w, 4:5], AF.Ln))
                cs.sync(sc)
                cs.step(sc, sc.activation(stats[0:w, 5:6], stats[0:w, 4:5], AF.Exp,
                                      scale=-0.5))
                sc.wait_ge(sv, 2 * i + 1)
                cs.step(sc, sc.activation(junk[0:w, :], xn_tm[0:w, :], AF.Square,
                                      accum_out=stats[0:w, 6:7]))
                sc.wait_ge(cv.sem, 8 * i + 8)
                cs.step(sc, sc.activation(stats[0:w, 6:7], stats[0:w, 6:7], AF.Ln))
                cs.sync(sc)
                cs.step(sc, sc.activation(stats[0:w, 7:8], stats[0:w, 6:7], AF.Exp,
                                      scale=-0.5))

        @blk.vector
        def _(v):
            for i, (off, w) in enumerate(chunks):
                xi = x_tm[0:w, i % 2, :]
                v.wait_ge(sx[i % 2], 16 * (i // 2 + 1))
                cv.step(v, v.reduce_sum(stats[0:w, 1:2], xi, axis=AX.X))
                v.wait_ge(cs.sem, 6 * i + 1)
                cv.sync(v)
                cv.step(v, v.tensor_scalar(stats[0:w, 2:3], stats[0:w, 1:2], 1.0 / DM, None, ALU.mult))
                cv.step(v, v.tensor_scalar(stats[0:w, 3:4], stats[0:w, 0:1], 1.0 / DM, None, ALU.mult))
                cv.sync(v)
                cv.step(v, v.tensor_tensor(stats[0:w, 4:5], stats[0:w, 2:3], stats[0:w, 2:3], ALU.mult))
                cv.sync(v)
                cv.step(v, v.tensor_tensor(stats[0:w, 4:5], stats[0:w, 3:4], stats[0:w, 4:5], ALU.subtract))
                cv.sync(v)
                cv.step(v, v.tensor_scalar(stats[0:w, 4:5], stats[0:w, 4:5], EPS, None,
                                        ALU.add))
                v.wait_ge(cs.sem, 6 * i + 3)
                if i >= 1:
                    v.wait_ge(sp, 32 * (i - 1) + 16)
                v.tensor_scalar(xn_tm[0:w, :], xi, stats[0:w, 2:3], stats[0:w, 5:6],
                                ALU.subtract, ALU.mult)
                v.drain(fusable=True).then_inc(sv, 1)
                v.wait_ge(cs.sem, 6 * i + 4)
                cv.step(v, v.tensor_scalar(stats[0:w, 6:7], stats[0:w, 6:7], 1.0 / DM, None, ALU.mult))
                cv.sync(v)
                cv.step(v, v.tensor_scalar(stats[0:w, 6:7], stats[0:w, 6:7], EPS, None,
                                        ALU.add))
                v.wait_ge(cs.sem, 6 * i + 6)
                if i >= 1:
                    v.wait_ge(sp, 32 * i)
                v.tensor_scalar(u_tm[0:w, :], xn_tm[0:w, :], stats[0:w, 7:8], None,
                                ALU.mult)
                v.drain(fusable=True).then_inc(sv, 1)
                if i >= 2:
                    v.wait_ge(ss[i % 2], 32 * (i // 2))
                for t in range(2 * NDC):
                    g = 32 * i + t + 1
                    v.wait_ge(sp, g)
                    src = ps_tr[t % 4]
                    dst = st_xn if t < NDC else st_u
                    dc = t % NDC
                    v.tensor_copy(dst[:, i % 2, dc, 0:w], src[:, 0:w])
                    v.drain(fusable=True).then_inc(se, 1)

        @blk.tensor
        def _(t):
            for i, (off, w) in enumerate(chunks):
                t.wait_ge(sv, 2 * i + 1)
                for dc in range(NDC):
                    g = 32 * i + dc + 1
                    if g > 4:
                        t.wait_ge(se, g - 4)
                    t.transpose(ps_tr[(g - 1) % 4][:, 0:w],
                                xn_tm[0:w, 128 * dc:128 * dc + 128],
                                ident[0:w, 0:w])
                    t.drain().then_inc(sp, 1)
                t.wait_ge(sv, 2 * i + 2)
                for dc in range(NDC):
                    g = 32 * i + NDC + dc + 1
                    t.wait_ge(se, g - 4)
                    t.transpose(ps_tr[(g - 1) % 4][:, 0:w],
                                u_tm[0:w, 128 * dc:128 * dc + 128],
                                ident[0:w, 0:w])
                    t.drain().then_inc(sp, 1)

    pg.barrier()


def emit_p1(pg):
    nc = pg.nc
    xn_sp, u0_sp, h1_sp = pg.io["xn_sp"], pg.io["u0_sp"], pg.io["h1_sp"]
    ipT, opT = pg.io["ipT"], pg.io["opT"]
    cwv, cbv, zbv, n1v, D64v = (pg.io[k] for k in
                                ("cw128", "cb128", "zb128", "n1w", "D64"))

    identbf = pg.const("identbf", _bf16(np.eye(128, dtype=np.float32)), BF16)
    identf = pg.const("identf", np.eye(128, dtype=np.float32), F32)
    sel = np.zeros((32, 32, 128), np.float32)
    for h in range(32):
        sel[h, h, :] = 1.0
    self_sel = pg.const("sel32f", sel, F32)
    maskbT = pg.const("maskbT",
                      np.where(np.arange(128)[:, None] <= np.arange(128)[None, :],
                               0.0, NEG).astype(np.float32), F32)
    diag32 = pg.const("diag32", np.eye(32, dtype=np.float32), F32)
    ones32x128 = pg.const("o32x128", np.ones((32, 128), np.float32), F32)
    ones32x64 = pg.const("o32x64", np.ones((32, 64), np.float32), F32)
    rep = np.zeros((32, 2048), np.float32)
    for h in range(32):
        rep[h, 64 * h:64 * (h + 1)] = 1.0
    repmask = pg.const("repmask", rep, F32)

    n1_sb = pg.sbuf("p1_n1", [128, NDC], F32)
    zb_sb = pg.sbuf("p1_zb", [128, NDC], F32)
    cb_sb = pg.sbuf("p1_cb", [128, 48], F32)
    cw_sb = pg.sbuf("p1_cw", [128, 48, 4], F32)
    dd_sb = pg.sbuf("p1_dd", [128, NDC], F32)

    u_q = pg.sbuf("p1_uq", [128, NDC, 259], BF16)
    wslab = pg.sbuf("p1_ws", [128, 2, NDC, 128], BF16)
    zst = pg.sbuf("p1_zst", [128, 2, 256], BF16)
    dt_q = pg.sbuf("p1_dt", [32, 256], F32)
    xbc = pg.sbuf("p1_xbc", [128, 48, 256], BF16)
    cacc = pg.sbuf("p1_cacc", [128, 256], F32)
    y_sb = pg.sbuf("p1_y", [128, NDC, TOK], BF16)
    xT = pg.sbuf("p1_xT", [128, NDC, 128], BF16)
    BT = pg.sbuf("p1_BT", [128, NDC, 128], BF16)
    tA = pg.sbuf("p1_tA", [32, 128], F32)
    tL = pg.sbuf("p1_tL", [32, 128], F32)
    tR = pg.sbuf("p1_tR", [32, 128], F32)
    alog = pg.sbuf("p1_alog", [32, 128], F32)
    cum = pg.sbuf("p1_cum", [32, 128], F32)
    cumT = pg.sbuf("p1_cumT", [128, 32], F32)
    cumlast = pg.sbuf("p1_cl", [32, 1], F32)
    runsum = pg.sbuf("p1_run", [32, 2], F32)
    pref = pg.sbuf("p1_pref", [32, NCH], F32)
    rhsd = pg.sbuf("p1_rhsd", [32, 32], F32)
    dte = pg.sbuf("p1_dte", [128, 32], F32)
    acolb = pg.sbuf("p1_acolb", [64, 32], F32)
    ecT_all = pg.sbuf("p1_ecT", [128, NCH, 32], F32)
    tS = pg.sbuf("p1_tS", [128, 2, 128], F32)
    tE2 = pg.sbuf("p1_tE2", [128, 2, 128], F32)
    St = pg.sbuf("p1_St", [128, 2, 128], BF16)
    Bd = pg.sbuf("p1_Bd", [128, 2, 64], BF16)
    H = pg.sbuf("p1_H", [64, NH * DS], F32)
    hl_st = pg.sbuf("p1_hlst", [64, 2, NH * DS], BF16)
    hl_ld = pg.sbuf("p1_hlld", [64, 2, NH * DS], BF16)
    Hin = pg.sbuf("p1_Hin", [64, NH * DS], F32)
    Hg = pg.sbuf("p1_Hg", [64, NH * DS], F32)
    Hg16 = pg.sbuf("p1_Hg16", [64, NH * DS], BF16)
    pack = pg.sbuf("p1_pack", [64, NH * DS], F32)
    rem_s = pg.sbuf("p1_rems", [64, NH * DS], F32)
    rem_a = pg.sbuf("p1_rema", [64, NH * DS], F32)
    lA = pg.sbuf("p1_lA", [32, 64], F32)
    ecr = pg.sbuf("p1_ecr", [32, 128], F32)
    prefx = pg.sbuf("p1_prefx", [32, NCH], F32)
    gm_sb = pg.sbuf("p1_gm", [64, 8], F32)
    EC = pg.sbuf("p1_EC", [64, 2, 128], BF16)
    Cch = pg.sbuf("p1_Cch", [128, 2, NDC, 128], BF16)
    zs_t = pg.sbuf("p1_zs", [128, 2, TOK], BF16)
    xn_m = pg.sbuf("p1_xnm", [128, 2, TOK], BF16)
    h1_st = pg.sbuf("p1_h1st", [128, 2, TOK], BF16)

    ps = [pg.psum(f"p1_ps{i}", [128, 512], F32) for i in range(6)]
    psT = [pg.psum(f"p1_psT{i}", [128, 128], BF16) for i in range(2)]

    C_sp = nc.dram_tensor("p1_C_sp", [NDC, 128, TOK], BF16, kind="Internal")
    zs_sp = nc.dram_tensor("p1_zs_sp", [NDC, 128, TOK], BF16, kind="Internal")
    hl_sp = nc.dram_tensor("p1_hl_sp", [NCH, 64, NH * DS], BF16, kind="Internal")
    cc_in = nc.dram_tensor("p1_ccin", [128, NH * DS], F32, kind="Internal")
    cc_out = nc.dram_tensor("p1_ccout", [512, NH * DS], F32, kind="Internal")

    if DBG_P1:
        dbg = {n: nc.dram_tensor("dbg_" + n, sh, dt2, kind="ExternalOutput")
               for n, sh, dt2 in [
                   ("xbc", [128, 48, 256], BF16), ("dt", [32, 256], F32),
                   ("cum", [32, 128], F32), ("cumT", [128, 32], F32),
                   ("dte", [128, 32], F32), ("acolb", [64, 32], F32),
                   ("y1", [128, NDC, TOK], BF16), ("H", [64, 2048], F32),
                   ("hin", [64, 2048], F32), ("y2", [128, NDC, TOK], BF16),
                   ("zs", [NDC, 128, TOK], BF16), ("ecr", [32, 128], F32),
                   ("hg", [64, 2048], F32)]}
    s_dbg = pg.sem("p1_dbg")
    s_cst = pg.sem("p1_cst")
    s_wl = [pg.sem("p1_wl0"), pg.sem("p1_wl1")]
    s_uq = pg.sem("p1_uq")
    s_un = pg.sem("p1_un")
    s_m1 = pg.sem("p1_m1")
    s_ez = pg.sem("p1_ez")
    s_ec = pg.sem("p1_ec")
    s_spl = pg.sem("p1_spl")
    s_cum = pg.sem("p1_cum")
    s_rd = pg.sem("p1_rd")
    s_pbc = pg.sem("p1_pbc")
    s_ptc = pg.sem("p1_ptc")
    s_ctv = pg.sem("p1_ctv")
    s_dte = pg.sem("p1_dte")
    s_acb = pg.sem("p1_acb")
    s_vtr = pg.sem("p1_vtr")
    s_ptr = pg.sem("p1_ptr")
    s_pcb = pg.sem("p1_pcb")
    s_px0 = pg.sem("p1_px0")
    s_py = pg.sem("p1_py")
    s_psm = pg.sem("p1_psm")
    s_dl = pg.sem("p1_dl")
    s_xe = pg.sem("p1_xe")
    s_st = pg.sem("p1_st")
    s_bd = pg.sem("p1_bd")
    s_hu = pg.sem("p1_hu")
    s_ya = pg.sem("p1_ya")
    s_hlc = pg.sem("p1_hlc")
    s_scd = pg.sem("p1_scd")
    s_csp = pg.sem("p1_csp")
    s_cc = pg.sem("p1_cc")
    s_pk = pg.sem("p1_pk")
    s_pkd = pg.sem("p1_pkd")
    s_rm = [pg.sem("p1_rm0"), pg.sem("p1_rm1"), pg.sem("p1_rm2")]
    s_hbc = pg.sem("p1_hbc")
    s_hg = pg.sem("p1_hg")
    s_hgc = pg.sem("p1_hgc")
    s_ptc2 = pg.sem("p1_ptc2")
    s_ecr = pg.sem("p1_ecr")
    s_pec = pg.sem("p1_pec")
    s_ecm = pg.sem("p1_ecm")
    s_py2 = pg.sem("p1_py2")
    s_ya2 = pg.sem("p1_ya2")
    s_cld = [pg.sem("p1_cld0"), pg.sem("p1_cld1")]
    s_hinz = pg.sem("p1_hinz")
    s_gz = pg.sem("p1_gz")
    s_gy = pg.sem("p1_gy")
    s_m2 = pg.sem("p1_m2")
    s_xnl = [pg.sem("p1_xnl0"), pg.sem("p1_xnl1")]
    s_h1e = pg.sem("p1_h1e")
    s_h1s = [pg.sem("p1_h1s0"), pg.sem("p1_h1s1")]
    s_zs = [pg.sem("p1_zs0"), pg.sem("p1_zs1")]
    s_zl = [pg.sem("p1_zl0"), pg.sem("p1_zl1")]
    s_hlw = pg.sem("p1_hlw")
    s_hls = [pg.sem("p1_hls0"), pg.sem("p1_hls1")]
    s_hld = [pg.sem("p1_hld0"), pg.sem("p1_hld1")]
    s_pkh = pg.sem("p1_pkh")
    s_dtp = pg.sem("p1_dtp")
    s_ect = pg.sem("p1_ect")
    s_aex = pg.sem("p1_aex")
    s_pfx = pg.sem("p1_pfx")
    s_la = pg.sem("p1_la")
    s_la2 = pg.sem("p1_la2")
    s_gm = pg.sem("p1_gm")
    cve = Chain(pg.sem("p1_cve"))
    cse = Chain(pg.sem("p1_cse"))

    pg.flush_consts()

    QW = 259
    NM = 65

    def mwidth(m):
        return 32 if m == 64 else 128

    with nc.Block() as blk:
        @blk.sync
        def _(s):
            for name, sb in (("n1w", n1_sb), ("zb128", zb_sb), ("cb128", cb_sb),
                             ("D64", dd_sb)):
                s.dma_start(out=sb[...], in_=pg.io[name]).then_inc(s_cst, 16)
            s.dma_start(out=cw_sb[...], in_=pg.io["cw128"]).then_inc(s_cst, 16)
            s.dma_start(out=gm_sb[...], in_=pg.io["gmask"]).then_inc(s_gm, 16)
            for Q in range(4):
                if Q >= 1:
                    s.wait_ge(s_m1, NM * Q)
                s.dma_start(out=u_q[:, :, :],
                            in_=u0_sp.ap()[:, :, 256 * Q:256 * Q + QW]
                            .rearrange("c p t -> p c t")).then_inc(s_uq, 16)
            for lc in range(NCH):
                s.wait_ge(s_cc, 1)
                if lc >= 2:
                    s.wait_ge(s_ya2, 16 * (lc - 1))
                    s.wait_ge(s_hg, lc - 1)
                s.dma_start(out=Cch[:, lc % 2, :, :],
                            in_=C_sp.ap()[:, :, 128 * lc:128 * (lc + 1)]
                            .rearrange("c p t -> p c t")).then_inc(s_cld[lc % 2], 16)
                s.dma_start(out=hl_ld[:, lc % 2, :],
                            in_=hl_sp.ap()[lc, :, :]).then_inc(s_hld[lc % 2], 16)
            for dc in range(NDC):
                s.wait_ge(s_ya2, 16 * NCH)
                if dc >= 2:
                    s.wait_ge(s_gy, dc - 1)
                s.dma_start(out=zs_t[:, dc % 2, :],
                            in_=zs_sp.ap()[dc, :, :]).then_inc(s_zl[dc % 2], 16)
            for m in range(NDC + 2):
                if m >= 2:
                    s.wait_ge(s_h1e, m - 1)
                if m < NDC:
                    s.dma_start(out=xn_m[:, m % 2, :],
                                in_=xn_sp.ap()[m, :, 3:TOKH]).then_inc(s_xnl[m % 2], 16)
                if m >= 2:
                    s.dma_start(out=h1_sp.ap()[m - 2, :, :],
                                in_=h1_st[:, m % 2, :]).then_inc(s_h1s[m % 2], 16)
            s.wait_ge(s_h1s[0], 16 * 8)
            s.wait_ge(s_h1s[1], 16 * 8)

        @blk.gpsimd
        def _(g):
            for Q in range(4):
                for m in range(NM):
                    gidx = NM * Q + m
                    if gidx >= 2:
                        g.wait_ge(s_m1, gidx - 1)
                    mw = mwidth(m)
                    g.dma_start(out=wslab[:, gidx % 2, :, 0:mw],
                                in_=ipT[:, 128 * m:128 * m + mw]
                                .rearrange("(c p) m -> p c m", p=128)
                                ).then_inc(s_wl[gidx % 2], 16)
                g.wait_ge(s_ec, 48 * (Q + 1))
                g.dma_start(out=C_sp.ap()[:, :, 256 * Q:256 * (Q + 1)]
                            .rearrange("c p t -> p c t"),
                            in_=xbc[:, 32:48, :]).then_inc(s_csp, 16)
                for cq in range(2):
                    lc = 2 * Q + cq
                    g.wait_ge(s_hlw, lc + 1)
                    g.dma_start(out=hl_sp.ap()[lc, :, :],
                                in_=hl_st[:, lc % 2, :]).then_inc(s_hls[lc % 2], 16)
            g.wait_ge(s_scd, NCH)
            g.dma_start(out=cc_in.ap()[0:64, :], in_=H[:, :]).then_inc(s_pkh, 16)
            g.wait_ge(s_pk, 1)
            g.dma_start(out=cc_in.ap()[64:128, :], in_=pack[:, :]).then_inc(s_pkd, 16)
            g.wait_ge(s_pkd, 16)
            g.wait_ge(s_pkh, 16)
            g.collective_compute(
                "AllGather", ALU.bypass,
                replica_groups=[[0, 1, 2, 3], [4, 5, 6, 7]],
                ins=[cc_in.ap().opt()], outs=[cc_out.ap().opt()],
            ).then_inc(s_cc, 1)
            g.wait_ge(s_cc, 1)
            for r in range(4):
                if r >= 1:
                    g.wait_ge(s_hinz, r)
                g.dma_start(out=rem_s[:, :], in_=cc_out.ap()[128 * r:128 * r + 64, :]
                            ).then_inc(s_rm[0], 16)
                g.dma_start(out=rem_a[:, :],
                            in_=cc_out.ap()[128 * r + 64:128 * r + 128, :]
                            ).then_inc(s_rm[1], 16)
            if DBG_P1:
                g.wait_ge(s_ya, 16 * NCH)
                g.wait_ge(s_scd, NCH)
                g.dma_start(out=dbg["y1"].ap(), in_=y_sb[...]).then_inc(s_dbg, 16)
                g.dma_start(out=dbg["xbc"].ap(), in_=xbc[...]).then_inc(s_dbg, 16)
                g.dma_start(out=dbg["dt"].ap(), in_=dt_q[...]).then_inc(s_dbg, 16)
                g.dma_start(out=dbg["cum"].ap(), in_=cum[...]).then_inc(s_dbg, 16)
                g.dma_start(out=dbg["cumT"].ap(), in_=cumT[...]).then_inc(s_dbg, 16)
                g.dma_start(out=dbg["dte"].ap(), in_=dte[...]).then_inc(s_dbg, 16)
                g.dma_start(out=dbg["acolb"].ap(), in_=acolb[...]).then_inc(s_dbg, 16)
                g.dma_start(out=dbg["H"].ap(), in_=H[...]).then_inc(s_dbg, 16)
                g.wait_ge(s_hinz, 4)
                g.dma_start(out=dbg["hin"].ap(), in_=Hin[...]).then_inc(s_dbg, 16)
                g.wait_ge(s_ya2, 16 * NCH)
                g.wait_ge(s_hg, NCH)
                g.dma_start(out=dbg["y2"].ap(), in_=y_sb[...]).then_inc(s_dbg, 16)
                g.dma_start(out=dbg["zs"].ap(), in_=zs_sp.ap()).then_inc(s_dbg, 16)
                g.dma_start(out=dbg["ecr"].ap(), in_=ecr[...]).then_inc(s_dbg, 16)
                g.dma_start(out=dbg["hg"].ap(), in_=Hg[...]).then_inc(s_dbg, 16)
                g.wait_ge(s_dbg, 16 * 13)
            for m in range(NDC):
                if m < 2:
                    g.wait_ge(s_m1, 4 * NM - 2 + m + 1)
                else:
                    g.wait_ge(s_m2, m - 1)
                g.dma_start(out=wslab[:, m % 2, :, :],
                            in_=opT[:, 128 * m:128 * (m + 1)]
                            .rearrange("(c p) m -> p c m", p=128)
                            ).then_inc(s_wl[m % 2], 16)


        @blk.tensor
        def _(t):
            for Q in range(4):
                t.wait_ge(s_un, 16 * (Q + 1))
                for m in range(NM):
                    gidx = NM * Q + m
                    mw = mwidth(m)
                    t.wait_ge(s_wl[gidx % 2], 16 * (gidx // 2 + 1))
                    if Q >= 1 and m < 4:
                        t.wait_ge(s_scd, 2 * Q)
                        t.wait_ge(s_ez, 17 * Q)
                        t.wait_ge(s_ec, 48 * Q)
                    elif m >= 4:
                        pm = m - 4
                        if pm <= 15:
                            t.wait_ge(s_ez, 17 * Q + pm + 1)
                        elif pm == 64:
                            pass
                        else:
                            t.wait_ge(s_ec, 48 * Q + pm - 15)
                    bank = ps[m % 4]
                    for k in range(NDC):
                        t.matmul(bank[0:mw, 0:QW], wslab[:, gidx % 2, k, 0:mw],
                                 u_q[:, k, :], start=(k == 0),
                                 stop=(k == NDC - 1))
                        if k == NDC - 1:
                            t.drain().then_inc(s_m1, 1)
                for cq in range(2):
                    lc = 2 * Q + cq
                    cqs = slice(128 * cq, 128 * cq + 128)
                    t.wait_ge(s_cum, lc + 1)
                    if lc >= 1:
                        t.wait_ge(s_ctv, lc)
                        t.wait_ge(s_ya, 16 * lc)
                        t.wait_ge(s_dte, lc)
                        t.wait_ge(s_acb, lc)
                    t.transpose(ps[4][:, 384:416], cum[0:32, :],
                                identf[0:32, 0:32])
                    t.drain().then_inc(s_ptc, 1)
                    t.wait_ge(s_rd, lc + 1)
                    t.matmul(ps[5][:, 384:416], ones32x128[:, :], rhsd[:, :],
                             start=True, stop=True)
                    t.drain().then_inc(s_pbc, 1)
                    t.wait_ge(s_ec, 48 * (Q + 1))
                    for tt in range(32):
                        g2 = 32 * lc + tt + 1
                        if g2 > 2:
                            t.wait_ge(s_vtr, g2 - 2)
                        t.transpose(psT[tt % 2][:, :], xbc[:, tt, cqs],
                                    identbf[:, :])
                        t.drain().then_inc(s_ptr, 1)
                    def cb_mm(h):
                        if h >= 2:
                            t.wait_ge(s_st, 32 * lc + h - 1)
                        elif lc >= 1:
                            t.wait_ge(s_st, 32 * lc)
                        t.matmul(ps[h % 2][:, 0:128],
                                 xbc[64 * (h % 2):64 * (h % 2) + 64, 16 + h // 2, cqs],
                                 xbc[64 * (h % 2):64 * (h % 2) + 64, 32 + h // 2, cqs],
                                 start=True, stop=True)
                        t.drain().then_inc(s_pcb, 1)

                    def x0_mm(h):
                        if h >= 1:
                            t.wait_ge(s_dl, 32 * lc + h)
                        elif lc >= 1:
                            t.wait_ge(s_dl, 32 * lc)
                        t.matmul(ps[2][:, 0:128], self_sel[:, h, :], cum[:, :],
                                 start=True, stop=True)
                        t.drain().then_inc(s_px0, 1)

                    cb_mm(0)
                    x0_mm(0)
                    cb_mm(1)
                    for h in range(32):
                        dc = h // 2
                        t.wait_ge(s_st, 32 * lc + h + 1)
                        t.wait_ge(s_vtr, 32 * lc + dc + 1)
                        if h % 2 == 0 and 16 * lc + dc - 1 > 0:
                            t.wait_ge(s_ya, 16 * lc + dc - 1)
                        t.matmul(ps[4 + dc % 2][64 * (h % 2):64 * (h % 2) + 64, 0:128],
                                 xT[:, dc, 64 * (h % 2):64 * (h % 2) + 64],
                                 St[:, h % 2, :], start=True, stop=True)
                        t.drain().then_inc(s_py, 1)
                        t.wait_ge(s_bd, 32 * lc + h + 1)
                        if h >= 1:
                            t.wait_ge(s_hu, 32 * lc + h)
                        elif lc >= 1:
                            t.wait_ge(s_hu, 32 * lc)
                        t.matmul(ps[3][0:64, 0:64], Bd[:, h % 2, :],
                                 xT[:, dc, 64 * (h % 2):64 * (h % 2) + 64],
                                 start=True, stop=True)
                        t.drain().then_inc(s_psm, 1)
                        if h + 2 < 32:
                            cb_mm(h + 2)
                        if h + 1 < 32:
                            x0_mm(h + 1)
            t.wait_ge(s_la, 1)
            t.wait_ge(s_scd, NCH)
            for j in range(4):
                t.matmul(ps[j][0:64, 0:512], lA[:, :], repmask[:, 512 * j:512 * (j + 1)],
                         start=True, stop=True)
                t.drain().then_inc(s_hbc, 1)
            for lc in range(NCH):
                t.wait_ge(s_la2, lc + 1)
                if lc == 0:
                    t.wait_ge(s_pk, 1)
                else:
                    t.wait_ge(s_hg, lc)
                for j in range(4):
                    t.matmul(ps[j][0:64, 0:512], lA[:, :],
                             repmask[:, 512 * j:512 * (j + 1)],
                             start=True, stop=True)
                    t.drain().then_inc(s_hbc, 1)
                t.wait_ge(s_ect, lc + 1)
                if lc >= 1:
                    t.wait_ge(s_ecr, lc)
                    t.wait_ge(s_ya2, 16 * lc)
                t.transpose(ps[4][0:32, 256:384], ecT_all[:, lc, :],
                            identf[:, :])
                t.drain().then_inc(s_ptc2, 1)
                for h in range(32):
                    dc = h // 2
                    if h >= 2:
                        t.wait_ge(s_ecm, 32 * lc + h - 1)
                    else:
                        t.wait_ge(s_hg, lc + 1)
                        t.wait_ge(s_ecr, lc + 1)
                    t.matmul(ps[h % 2][0:64, 0:128], self_sel[:, h, 0:64],
                             ecr[:, :], start=True, stop=True)
                    t.drain().then_inc(s_pec, 1)
                    t.wait_ge(s_ecm, 32 * lc + h + 1)
                    t.wait_ge(s_hgc, lc + 1)
                    if h % 2 == 0 and 16 * lc + dc - 1 > 0:
                        t.wait_ge(s_ya2, 16 * lc + dc - 1)
                    t.matmul(ps[4 + dc % 2][64 * (h % 2):64 * (h % 2) + 64, 0:128],
                             Hg16[:, 64 * h:64 * h + 64], EC[:, h % 2, :],
                             start=True, stop=True)
                    t.drain().then_inc(s_py2, 1)
            t.wait_ge(s_gy, 16)
            t.wait_ge(s_hg, NCH)
            t.wait_ge(s_ecm, 32 * NCH)
            for m in range(NDC):
                gidx = 4 * NM + m
                t.wait_ge(s_wl[gidx % 2], 16 * (gidx // 2 + 1))
                if m >= 2:
                    t.wait_ge(s_h1e, m - 1)
                b0, b1 = ps[2 * (m % 2)], ps[2 * (m % 2) + 1]
                for k in range(NDC):
                    t.matmul(b0[:, 0:512], wslab[:, gidx % 2, k, :],
                             y_sb[:, k, 0:512], start=(k == 0), stop=(k == NDC - 1))
                    t.matmul(b1[:, 0:512], wslab[:, gidx % 2, k, :],
                             y_sb[:, k, 512:1024], start=(k == 0), stop=(k == NDC - 1))
                    if k == NDC - 1:
                        t.drain().then_inc(s_m2, 1)

        @blk.vector
        def _(v):
            v.wait_ge(s_cst, 16 * 5)
            v.memset(H[:, :], 0.0)
            v.memset(Hin[:, :], 0.0)
            v.memset(runsum[:, 0:1], 0.0)
            for Q in range(4):
                v.wait_ge(s_uq, 16 * (Q + 1))
                for dc in range(NDC):
                    v.tensor_scalar(u_q[:, dc, :], u_q[:, dc, :], n1_sb[:, dc:dc + 1],
                                    None, ALU.mult)
                    v.drain(fusable=True).then_inc(s_un, 1)
                if Q >= 1:
                    v.wait_ge(s_csp, 16 * Q)
                for m in range(16, 64):
                    dc = m - 16
                    v.wait_ge(s_m1, NM * Q + m + 1)
                    bank = ps[m % 4]
                    v.tensor_scalar(cacc[:, :], bank[:, 0:256], cw_sb[:, dc, 0:1],
                                    cb_sb[:, dc:dc + 1], ALU.mult, ALU.add)
                    for k2 in (1, 2):
                        v.scalar_tensor_tensor(cacc[:, :], bank[:, k2:k2 + 256],
                                               cw_sb[:, dc, k2:k2 + 1], cacc[:, :],
                                               ALU.mult, ALU.add)
                    v.scalar_tensor_tensor(xbc[:, dc, :], bank[:, 3:259],
                                           cw_sb[:, dc, 3:4], cacc[:, :],
                                           ALU.mult, ALU.add)
                    v.drain(fusable=True).then_inc(s_ec, 1)
                for cq in range(2):
                    lc = 2 * Q + cq
                    cqs = slice(128 * cq, 128 * cq + 128)
                    qcols = slice(256 * Q + 128 * cq, 256 * Q + 128 * cq + 128)
                    v.wait_ge(s_ez, 17 * (Q + 1))
                    v.tensor_scalar(tR[:, :], dt_q[0:32, cqs], 0.0, None, ALU.max)
                    v.wait_ge(s_spl, lc + 1)
                    v.scalar_tensor_tensor(alog[:, :], tR[:, :], -1.0, tL[:, :],
                                           ALU.mult, ALU.subtract)
                    if lc >= 1:
                        v.wait_ge(s_px0, 32 * lc)
                    v.tensor_tensor_scan(cum[:, :], alog[:, :], alog[:, :], 0.0,
                                         ALU.add, ALU.bypass)
                    v.drain(fusable=True).then_inc(s_cum, 1)
                    cve.step(v, v.tensor_copy(cumlast[:, :], cum[:, 127:128]))
                    cve.sync(v)
                    cve.step(v, v.tensor_copy(pref[:, lc:lc + 1], runsum[:, 0:1]))
                    cve.sync(v)
                    cve.step(v, v.tensor_tensor(runsum[:, 0:1], runsum[:, 0:1],
                                             cumlast[:, :], ALU.add))
                    cve.sync(v)
                    if lc >= 1:
                        v.wait_ge(s_pbc, lc)
                    v.tensor_scalar(rhsd[:, :], diag32[:, :], cumlast[:, :],
                                    None, ALU.mult)
                    v.drain(fusable=True).then_inc(s_rd, 1)
                    v.wait_ge(s_ptc, lc + 1)
                    cve.step(v, v.tensor_copy(cumT[:, :], ps[4][:, 384:416]))
                    cve.sync(v)
                    v.sem_inc(s_ctv, 1)
                    v.wait_ge(s_pbc, lc + 1)
                    v.tensor_tensor(dte[:, :], ps[5][:, 384:416], cumT[:, :],
                                    ALU.subtract).then_inc(s_dtp, 1)
                    if lc >= 2:
                        v.wait_ge(s_hls[lc % 2], 16 * (lc // 2))
                    v.tensor_copy(hl_st[:, lc % 2, :], H[:, :])
                    v.drain(fusable=True).then_inc(s_hlw, 1)
                    for tt in range(32):
                        v.wait_ge(s_ptr, 32 * lc + tt + 1)
                        dst = xT if tt < 16 else BT
                        v.tensor_copy(dst[:, tt % 16, :], psT[tt % 2][:, :])
                        v.drain(fusable=True).then_inc(s_vtr, 1)
                    v.wait_ge(s_dte, lc + 1)
                    v.wait_ge(s_acb, lc + 1)
                    for h in range(32):
                        dc = h // 2
                        v.wait_ge(s_px0, 32 * lc + h + 1)
                        if 32 * lc + h - 1 > 0:
                            v.wait_ge(s_xe, 32 * lc + h - 1)
                        v.tensor_scalar(tS[:, h % 2, :], ps[2][:, 0:128],
                                        cumT[:, h:h + 1], None, ALU.subtract)
                        v.tensor_tensor(tS[:, h % 2, :], tS[:, h % 2, :],
                                        maskbT[:, :], ALU.add).then_inc(s_dl, 1)
                        if 32 * lc + h - 1 > 0:
                            v.wait_ge(s_psm, 32 * lc + h - 1)
                        v.tensor_scalar(Bd[:, h % 2, :],
                                        BT[:, dc, 64 * (h % 2):64 * (h % 2) + 64],
                                        dte[:, h:h + 1], None, ALU.mult)
                        v.drain(fusable=True).then_inc(s_bd, 1)
                        v.wait_ge(s_xe, 32 * lc + h + 1)
                        if 32 * lc + h - 1 > 0:
                            v.wait_ge(s_py, 32 * lc + h - 1)
                        v.tensor_tensor(St[:, h % 2, :], tE2[:, h % 2, :],
                                        ps[h % 2][:, 0:128], ALU.mult)
                        v.drain(fusable=True).then_inc(s_st, 1)
                        v.wait_ge(s_psm, 32 * lc + h + 1)
                        v.scalar_tensor_tensor(H[0:64, 64 * h:64 * h + 64],
                                               H[0:64, 64 * h:64 * h + 64],
                                               acolb[:, h:h + 1],
                                               ps[3][0:64, 0:64],
                                               ALU.mult, ALU.add).then_inc(s_hu, 1)
                        if h % 2 == 1:
                            v.wait_ge(s_py, 32 * lc + h + 1)
                            v.scalar_tensor_tensor(y_sb[:, dc, qcols],
                                                   xbc[:, dc, cqs],
                                                   dd_sb[:, dc:dc + 1],
                                                   ps[4 + dc % 2][:, 0:128],
                                                   ALU.mult, ALU.add
                                                   ).then_inc(s_ya, 1)
                    v.sem_inc(s_scd, 1)
            v.wait_ge(s_aex, 1)
            v.tensor_scalar(lA[:, :], ones32x64[:, :], runsum[:, 1:2], None,
                            ALU.mult)
            v.drain(fusable=True).then_inc(s_la, 1)
            v.wait_ge(s_hbc, 4)
            for j in range(4):
                v.tensor_copy(pack[:, 512 * j:512 * (j + 1)],
                              ps[j][0:64, 0:512])
                if j == 3:
                    v.drain(fusable=True).then_inc(s_pk, 1)
            v.wait_ge(s_gm, 16)
            for r in range(4):
                v.wait_ge(s_rm[0], 16 * (r + 1))
                v.wait_ge(s_rm[1], 16 * (r + 1))
                v.tensor_scalar(Hg[:, :], rem_a[:, :],
                                gm_sb[:, r:r + 1], gm_sb[:, 4 + r:5 + r],
                                ALU.mult, ALU.add)
                v.tensor_tensor(Hin[:, :], Hin[:, :], Hg[:, :], ALU.mult)
                v.tensor_scalar(Hg[:, :], rem_s[:, :],
                                gm_sb[:, r:r + 1], None, ALU.mult)
                v.tensor_tensor(Hin[:, :], Hin[:, :], Hg[:, :], ALU.add
                                ).then_inc(s_hinz, 1)
            if DBG_P1:
                v.wait_ge(s_dbg, 16 * 8)
            for lc in range(NCH):
                v.wait_ge(s_pfx, lc + 1)
                if lc >= 1:
                    v.wait_ge(s_hbc, 4 + 4 * lc)
                v.tensor_scalar(lA[:, :], ones32x64[:, :], prefx[:, lc:lc + 1],
                                None, ALU.mult)
                v.drain(fusable=True).then_inc(s_la2, 1)
                v.wait_ge(s_hbc, 4 + 4 * (lc + 1))
                for j in range(4):
                    v.tensor_tensor(Hg[:, 512 * j:512 * (j + 1)],
                                    Hin[:, 512 * j:512 * (j + 1)],
                                    ps[j][0:64, 0:512], ALU.mult)
                v.wait_ge(s_hld[lc % 2], 16 * (lc // 2 + 1))
                v.tensor_tensor(Hg[:, :], Hg[:, :], hl_ld[:, lc % 2, :], ALU.add
                                ).then_inc(s_hg, 1)
                v.tensor_copy(Hg16[:, :], Hg[:, :])
                v.drain(fusable=True).then_inc(s_hgc, 1)
                v.wait_ge(s_ptc2, lc + 1)
                v.tensor_copy(ecr[:, :], ps[4][0:32, 256:384])
                v.drain(fusable=True).then_inc(s_ecr, 1)
                v.wait_ge(s_cld[lc % 2], 16 * (lc // 2 + 1))
                for h in range(32):
                    dc = h // 2
                    v.wait_ge(s_pec, 32 * lc + h + 1)
                    if 32 * lc + h - 1 > 0:
                        v.wait_ge(s_py2, 32 * lc + h - 1)
                    v.tensor_tensor(EC[:, h % 2, :],
                                    Cch[64 * (h % 2):64 * (h % 2) + 64, lc % 2,
                                        dc, :],
                                    ps[h % 2][0:64, 0:128], ALU.mult)
                    v.drain(fusable=True).then_inc(s_ecm, 1)
                    if h % 2 == 1:
                        v.wait_ge(s_py2, 32 * lc + h + 1)
                        lcol = slice(128 * lc, 128 * lc + 128)
                        v.tensor_tensor(y_sb[:, dc, lcol], y_sb[:, dc, lcol],
                                        ps[4 + dc % 2][:, 0:128], ALU.add
                                        ).then_inc(s_ya2, 1)
            if DBG_P1:
                v.wait_ge(s_dbg, 16 * 8)
            for dc in range(NDC):
                v.wait_ge(s_zl[dc % 2], 16 * (dc // 2 + 1))
                v.tensor_tensor(y_sb[:, dc, :], y_sb[:, dc, :], zs_t[:, dc % 2, :],
                                ALU.mult)
                v.drain(fusable=True).then_inc(s_gy, 1)
            for m in range(NDC):
                v.wait_ge(s_m2, m + 1)
                v.wait_ge(s_xnl[m % 2], 16 * (m // 2 + 1))
                if m >= 2:
                    v.wait_ge(s_h1s[m % 2], 16 * (m // 2))
                b0, b1 = ps[2 * (m % 2)], ps[2 * (m % 2) + 1]
                v.tensor_tensor(h1_st[:, m % 2, 0:512], b0[:, 0:512],
                                xn_m[:, m % 2, 0:512], ALU.add)
                v.tensor_tensor(h1_st[:, m % 2, 512:1024], b1[:, 0:512],
                                xn_m[:, m % 2, 512:1024], ALU.add)
                v.drain(fusable=True).then_inc(s_h1e, 1)

        @blk.scalar
        def _(sc):
            for Q in range(4):
                for m in range(16):
                    sc.wait_ge(s_m1, NM * Q + m + 1)
                    gi = 16 * Q + m
                    if gi >= 2:
                        sc.wait_ge(s_zs[gi % 2], 16 * (gi // 2))
                    sc.activation(zst[:, gi % 2, :], ps[m % 4][:, 3:259], AF.Silu,
                                  bias=zb_sb[:, m:m + 1])
                    sc.drain(fusable=True).then_inc(s_ez, 1)
                    sc.dma_start(out=zs_sp.ap()[m, :, 256 * Q:256 * (Q + 1)],
                                 in_=zst[:, gi % 2, :]).then_inc(s_zs[gi % 2], 16)
                sc.wait_ge(s_m1, NM * (Q + 1))
                if Q >= 1:
                    sc.wait_ge(s_cum, 2 * Q)
                sc.copy(dt_q[0:32, :], ps[0][0:32, 3:259])
                sc.drain(fusable=True).then_inc(s_ez, 1)
                for cq in range(2):
                    lc = 2 * Q + cq
                    sc.wait_ge(s_ez, 17 * (Q + 1))
                    if lc >= 1:
                        sc.wait_ge(s_cum, lc)
                    cse.step(sc, sc.activation(tA[:, :], dt_q[0:32, 128 * cq:128 * cq + 128],
                                           AF.Abs))
                    cse.sync(sc)
                    cse.step(sc, sc.activation(tA[:, :], tA[:, :], AF.Exp, scale=-1.0))
                    cse.sync(sc)
                    sc.activation(tL[:, :], tA[:, :], AF.Ln, bias=1.0
                                  ).then_inc(s_spl, 1)
                    sc.wait_ge(s_dtp, lc + 1)
                    sc.activation(dte[:, :], dte[:, :], AF.Exp).then_inc(s_dte, 1)
                    sc.wait_ge(s_pbc, lc + 1)
                    sc.activation(acolb[:, :], ps[5][0:64, 384:416], AF.Exp
                                  ).then_inc(s_acb, 1)
                    sc.wait_ge(s_ctv, lc + 1)
                    sc.activation(ecT_all[:, lc, :], cumT[:, :], AF.Exp)
                    sc.drain(fusable=True).then_inc(s_ect, 1)
                    for h in range(32):
                        sc.wait_ge(s_dl, 32 * lc + h + 1)
                        if 32 * lc + h - 1 > 0:
                            sc.wait_ge(s_st, 32 * lc + h - 1)
                        sc.activation(tE2[:, h % 2, :], tS[:, h % 2, :], AF.Exp
                                      ).then_inc(s_xe, 1)
            sc.wait_ge(s_scd, NCH)
            sc.activation(runsum[:, 1:2], runsum[:, 0:1], AF.Exp).then_inc(s_aex, 1)
            for lc in range(NCH):
                sc.activation(prefx[:, lc:lc + 1], pref[:, lc:lc + 1], AF.Exp
                              ).then_inc(s_pfx, 1)

    pg.barrier()


def emit_p2(pg):
    nc = pg.nc
    h1_sp, s_sp = pg.io["h1_sp"], pg.io["s_sp"]
    gwT, uwT, dwT = pg.io["gwT"], pg.io["uwT"], pg.io["dwT"]

    onescol = pg.const("ones128c", _bf16(np.ones((128, 1), np.float32)), BF16)
    onesrow = pg.const("ones1x128f", np.ones((1, 128), np.float32), F32)
    pg.flush_consts()

    n2_sb = pg.sbuf("p2_n2", [128, NDC], F32)
    h1 = pg.sbuf("p2_h1", [128, NDC, TOK], BF16)
    sq = pg.sbuf("p2_sq", [128, 2, TOK], BF16)
    rstd = pg.sbuf("p2_rstd", [1, TOK], F32)
    v_h = pg.sbuf("p2_v", [128, NDC, 512], BF16)
    a_h = pg.sbuf("p2_a", [128, 64, 512], BF16)
    gs = pg.sbuf("p2_gs", [128, 2, 512], BF16)
    wsl = pg.sbuf("p2_wsl", [128, 2, NDC, 128], BF16)
    wsl2 = pg.sbuf("p2_wsl2", [128, 2, 64, 128], BF16)
    sst = pg.sbuf("p2_sst", [128, 2, 512], BF16)

    psr = [pg.psum(f"p2_ps{i}", [128, 512], F32) for i in range(4)]
    psA = pg.psum("p2_psA", [128, 512], F32)
    psB = pg.psum("p2_psB", [128, 512], F32)

    s_pr = pg.sem("p2_pr")
    s_h1 = pg.sem("p2_h1")
    s_sq = pg.sem("p2_sq")
    s_pq = pg.sem("p2_pq")
    s_eps = pg.sem("p2_eps")
    s_pst = pg.sem("p2_pst")
    s_bc = pg.sem("p2_bc")
    s_vh = pg.sem("p2_vh")
    s_wl = [pg.sem("p2_wl0"), pg.sem("p2_wl1")]
    s_w2 = [pg.sem("p2_w20"), pg.sem("p2_w21")]
    s_mm = pg.sem("p2_mm")
    s_sil = pg.sem("p2_sil")
    s_am = pg.sem("p2_am")
    s_se = pg.sem("p2_se")
    s_ss = [pg.sem("p2_ss0"), pg.sem("p2_ss1")]
    s_sr = pg.sem("p2_sr2")

    NM2 = 64

    with nc.Block() as blk:
        @blk.sync
        def _(s):
            s.dma_start(out=n2_sb[...], in_=pg.io["n2w"]).then_inc(s_pr, 16)
            s.dma_start(out=h1[...],
                        in_=h1_sp.ap().rearrange("m p t -> p m t")).then_inc(s_h1, 16)
            for th in range(2):
                for m2 in range(NDC):
                    gi = NDC * th + m2
                    s.wait_ge(s_se, gi + 1)
                    s.dma_start(out=s_sp.ap()[m2, :, 512 * th:512 * (th + 1)],
                                in_=sst[:, gi % 2, :]).then_inc(s_ss[gi % 2], 16)
            s.wait_ge(s_ss[0], 16 * NDC)
            s.wait_ge(s_ss[1], 16 * NDC)

        @blk.gpsimd
        def _(g):
            for th in range(2):
                for m in range(NM2):
                    for j, wt in ((0, gwT), (1, uwT)):
                        si = th * 128 + 2 * m + j
                        if si >= 2:
                            pg_cnt = (si - 2 + 1) if si - 2 < 128 else (144 + si - 2 - 128 + 1)
                            g.wait_ge(s_mm, pg_cnt)
                        g.dma_start(out=wsl[:, si % 2, :, :],
                                    in_=wt[:, 128 * m:128 * (m + 1)]
                                    .rearrange("(c p) m -> p c m", p=128)
                                    ).then_inc(s_wl[si % 2], 16)
                for m2 in range(NDC):
                    gi2 = th * NDC + m2
                    if gi2 >= 2:
                        g.wait_ge(s_mm, th * 144 + 128 + m2 - 1)
                    g.dma_start(out=wsl2[:, gi2 % 2, :, :],
                                in_=dwT[:, 128 * m2:128 * (m2 + 1)]
                                .rearrange("(c p) m -> p c m", p=128)
                                ).then_inc(s_w2[gi2 % 2], 16)

        @blk.scalar
        def _(sc):
            sc.wait_ge(s_h1, 16)
            for m in range(NDC):
                if m >= 2:
                    sc.wait_ge(s_pq, m - 1)
                sc.activation(sq[:, m % 2, :], h1[:, m, :], AF.Square)
                sc.drain(fusable=True).then_inc(s_sq, 1)
            sc.wait_ge(s_eps, 2)
            sc.activation(rstd[0:1, 0:512], psA[0:1, 0:512], AF.Ln)
            sc.activation(rstd[0:1, 512:1024], psB[0:1, 0:512], AF.Ln)
            sc.activation(rstd[0:1, 0:1024], rstd[0:1, 0:1024], AF.Exp,
                          scale=-0.5)
            sc.drain(fusable=True).then_inc(s_sr, 1)
            for th in range(2):
                for m in range(NM2):
                    gm = th * NM2 + m
                    sc.wait_ge(s_mm, th * 144 + 2 * m + 1)
                    if gm >= 2:
                        sc.wait_ge(s_am, gm - 1)
                    sc.activation(gs[:, gm % 2, :], psr[(2 * m) % 4][:, :], AF.Silu)
                    sc.drain(fusable=True).then_inc(s_sil, 1)

        @blk.vector
        def _(v):
            v.wait_ge(s_pr, 16)
            v.wait_ge(s_h1, 16)
            v.wait_ge(s_pst, 1)
            v.tensor_scalar(psA[0:1, 0:512], psA[0:1, 0:512],
                            1.0 / DM, EPS, ALU.mult, ALU.add)
            v.drain(fusable=True).then_inc(s_eps, 1)
            v.tensor_scalar(psB[0:1, 0:512], psB[0:1, 0:512],
                            1.0 / DM, EPS, ALU.mult, ALU.add)
            v.drain(fusable=True).then_inc(s_eps, 1)
            for th in range(2):
                ths = slice(512 * th, 512 * (th + 1))
                v.wait_ge(s_bc, th + 1)
                for m in range(NDC):
                    v.scalar_tensor_tensor(v_h[:, m, :], h1[:, m, ths],
                                           n2_sb[:, m:m + 1],
                                           psA[:, 0:512] if th == 0 else psB[:, 0:512],
                                           ALU.mult, ALU.mult)
                    v.drain(fusable=True).then_inc(s_vh, 1)
                for m in range(NM2):
                    gm = th * NM2 + m
                    v.wait_ge(s_sil, gm + 1)
                    v.wait_ge(s_mm, th * 144 + 2 * m + 2)
                    v.tensor_tensor(a_h[:, m, :], gs[:, gm % 2, :],
                                    psr[(2 * m + 1) % 4][:, :], ALU.mult)
                    v.drain(fusable=True).then_inc(s_am, 1)
                for m2 in range(NDC):
                    gi = th * NDC + m2
                    v.wait_ge(s_mm, th * 144 + 128 + m2 + 1)
                    if gi >= 2:
                        v.wait_ge(s_ss[gi % 2], 16 * (gi // 2))
                    v.tensor_tensor(sst[:, gi % 2, :], psr[m2 % 4][:, :],
                                    h1[:, m2, ths], ALU.add)
                    v.drain(fusable=True).then_inc(s_se, 1)

        @blk.tensor
        def _(t):
            for m in range(NDC):
                t.wait_ge(s_sq, m + 1)
                t.matmul(psA[0:1, 0:512], onescol[:, :], sq[:, m % 2, 0:512],
                         start=(m == 0), stop=(m == NDC - 1))
                t.matmul(psB[0:1, 0:512], onescol[:, :], sq[:, m % 2, 512:1024],
                         start=(m == 0), stop=(m == NDC - 1))
                t.drain().then_inc(s_pq, 1)
            t.drain().then_inc(s_pst, 1)
            for th in range(2):
                t.wait_ge(s_sr, 1)
                if th == 1:
                    t.wait_ge(s_vh, 16)
                pb = psA if th == 0 else psB
                t.matmul(pb[:, 0:512], onesrow[:, :],
                         rstd[0:1, 512 * th:512 * (th + 1)], start=True, stop=True)
                t.drain().then_inc(s_bc, 1)
                t.wait_ge(s_vh, 16 * (th + 1))
                for m in range(NM2):
                    g0 = th * 144 + 2 * m
                    for j in range(2):
                        gi = g0 + j
                        si = th * 128 + 2 * m + j
                        t.wait_ge(s_wl[si % 2], 16 * (si // 2 + 1))
                        bank = psr[(2 * m + j) % 4]
                        if 2 * m + j >= 4:
                            pm = 2 * m + j - 4
                            if pm % 2 == 0:
                                t.wait_ge(s_sil, th * NM2 + pm // 2 + 1)
                            else:
                                t.wait_ge(s_am, th * NM2 + pm // 2 + 1)
                        elif th == 1 and 2 * m + j < 4:
                            t.wait_ge(s_se, th * NDC)
                        for k in range(NDC):
                            t.matmul(bank[:, :], wsl[:, gi % 2, k, :],
                                     v_h[:, k, :], start=(k == 0),
                                     stop=(k == NDC - 1))
                        t.drain().then_inc(s_mm, 1)
                for m2 in range(NDC):
                    gi2 = th * NDC + m2
                    t.wait_ge(s_w2[gi2 % 2], 16 * (gi2 // 2 + 1))
                    bank = psr[m2 % 4]
                    t.wait_ge(s_am, NM2 * (th + 1))
                    if gi2 >= 4:
                        t.wait_ge(s_se, gi2 - 3)
                    for k in range(NM2):
                        t.matmul(bank[:, :], wsl2[:, gi2 % 2, k, :],
                                 a_h[:, k, :], start=(k == 0), stop=(k == NM2 - 1))
                    t.drain().then_inc(s_mm, 1)

    pg.barrier()


def emit_p3(pg):
    nc = pg.nc
    xn_sp, s_sp, x1_sp = pg.io["xn_sp"], pg.io["s_sp"], pg.io["x1_sp"]
    x_in = pg.io["x"]
    fupT, fgT, fdT = pg.io["fupT"], pg.io["fgT"], pg.io["fdT"]

    def hadamard(n):
        h = np.array([[1.0]], np.float32)
        while h.shape[0] < n:
            h = np.block([[h, h], [h, -h]])
        return h

    H128 = pg.const("h128bf", _bf16(hadamard(128)), BF16)
    identbf = pg.const("identbf", _bf16(np.eye(128, dtype=np.float32)), BF16)
    pg.flush_consts()

    sug_sb = pg.sbuf("p3_sug", [128, NDC], F32)
    wprod = pg.sbuf("p3_wprod", [128, 64], F32)
    wsd_sb = pg.sbuf("p3_wsd", [128, NDC], F32)
    sdp_sb = pg.sbuf("p3_sdp", [128, 64], F32)
    wsu_sb = pg.sbuf("p3_wsu", [128, 64], F32)
    wsg_sb = pg.sbuf("p3_wsg", [128, 64], F32)
    xf = pg.sbuf("p3_xf", [128, NDC, TOK], BF16)
    s_sb = pg.sbuf("p3_s", [128, NDC, TOK], BF16)
    tmpb = pg.sbuf("p3_tmp", [128, TOK], BF16)
    a_h = pg.sbuf("p3_a", [128, 64, 512], BF16)
    tmp5 = pg.sbuf("p3_tmp5", [128, 512], BF16)
    gs = pg.sbuf("p3_gs", [128, 2, 512], BF16)
    gsd = pg.sbuf("p3_gsd", [128, 2, 512], BF16)
    wsl = pg.sbuf("p3_wsl", [128, 2, NDC, 128], BF16)
    wsl2 = pg.sbuf("p3_wsl2", [128, 2, 32, 128], BF16)
    x_ld = pg.sbuf("p3_xld", [128, 2, DM], F32)
    x1_st = pg.sbuf("p3_x1st", [128, 2, DM], F32)

    psr = [pg.psum(f"p3_ps{i}", [128, 512], F32) for i in range(4)]
    psT = [pg.psum(f"p3_psT{i}", [128, 128], BF16) for i in range(2)]

    s_pr = pg.sem("p3_pr")
    s_ld = pg.sem("p3_ld")
    s_su = pg.sem("p3_su")
    s_bf = pg.sem("p3_bf")
    s_fwm = pg.sem("p3_fwm")
    s_fa = pg.sem("p3_fa")
    s_wv = pg.sem("p3_wv")
    s_wl = [pg.sem("p3_wl0"), pg.sem("p3_wl1")]
    s_w2 = [pg.sem("p3_w20"), pg.sem("p3_w21")]
    s_mm = pg.sem("p3_mm")
    s_sil = pg.sem("p3_sil")
    s_am = pg.sem("p3_am")
    s_bf8 = pg.sem("p3_bf8")
    s_fw8 = pg.sem("p3_fw8")
    s_fa8 = pg.sem("p3_fa8")
    s_dsl = pg.sem("p3_dsl")
    s_sg = pg.sem("p3_sg")
    s_ptr = pg.sem("p3_ptr")
    s_cmb = pg.sem("p3_cmb")
    s_xl = [pg.sem("p3_xl0"), pg.sem("p3_xl1")]
    s_xs = [pg.sem("p3_xs0"), pg.sem("p3_xs1")]

    NM2 = 64
    RS2 = float(1.0 / np.sqrt(2048.0))
    RS8 = float(1.0 / np.sqrt(8192.0))

    with nc.Block() as blk:
        @blk.sync
        def _(s):
            for name, sb in (("sug", sug_sb), ("wsd", wsd_sb), ("sdp", sdp_sb),
                             ("wsu", wsu_sb), ("wsg", wsg_sb)):
                s.dma_start(out=sb[...], in_=pg.io[name]).then_inc(s_pr, 16)
            s.dma_start(out=xf[...],
                        in_=xn_sp.ap()[:, :, 3:TOKH].rearrange("m p t -> p m t")
                        ).then_inc(s_ld, 16)
            s.dma_start(out=s_sb[...],
                        in_=s_sp.ap().rearrange("m p t -> p m t")).then_inc(s_ld, 16)
            for tc in range(NCH + 2):
                if tc >= 2:
                    s.wait_ge(s_cmb, 16 * (tc - 1))
                if tc < NCH:
                    s.dma_start(out=x_ld[:, tc % 2, :],
                                in_=x_in[3 + 128 * tc:3 + 128 * (tc + 1), :]
                                ).then_inc(s_xl[tc % 2], 16)
                if tc >= 2:
                    s.dma_start(out=x1_sp.ap()[128 * (tc - 2):128 * (tc - 1), :],
                                in_=x1_st[:, tc % 2, :]).then_inc(s_xs[tc % 2], 16)
            s.wait_ge(s_xs[0], 16 * 4)
            s.wait_ge(s_xs[1], 16 * 4)

        @blk.gpsimd
        def _(g):
            for th in range(2):
                for m in range(NM2):
                    for j, wt in ((0, fgT), (1, fupT)):
                        si = th * 128 + 2 * m + j
                        gi = th * 144 + 2 * m + j
                        if si >= 2:
                            pgi = si - 2
                            pcnt = (pgi + 1) if pgi < 128 else (144 + pgi - 128 + 1)
                            g.wait_ge(s_mm, pcnt)
                        g.dma_start(out=wsl[:, si % 2, :, :],
                                    in_=wt[:, 128 * m:128 * (m + 1)]
                                    .rearrange("(c p) m -> p c m", p=128)
                                    ).then_inc(s_wl[si % 2], 16)
                for m2 in range(NDC):
                    for kh in range(2):
                        si2 = th * 32 + 2 * m2 + kh
                        if si2 >= 2:
                            pgi = si2 - 2
                            pth, prem = pgi // 32, pgi % 32
                            pcnt = pth * 144 + 128 + prem // 2 + 1
                            g.wait_ge(s_mm, pcnt)
                        g.dma_start(out=wsl2[:, si2 % 2, :, :],
                                    in_=fdT[4096 * kh:4096 * (kh + 1),
                                            128 * m2:128 * (m2 + 1)]
                                    .rearrange("(c p) m -> p c m", p=128)
                                    ).then_inc(s_w2[si2 % 2], 16)

        @blk.scalar
        def _(sc):
            for th in range(2):
                for m in range(NM2):
                    gm = th * NM2 + m
                    sc.wait_ge(s_mm, th * 144 + 2 * m + 1)
                    if gm >= 2:
                        sc.wait_ge(s_am, gm - 1)
                    sc.activation(gs[:, gm % 2, :], psr[(2 * m) % 4][:, :], AF.Silu,
                                  scale=wsg_sb[:, m:m + 1])
                    sc.drain(fusable=True).then_inc(s_sil, 1)
                for m2 in range(NDC):
                    gi = th * NDC + m2
                    sc.wait_ge(s_mm, th * 144 + 128 + m2 + 1)
                    if gi >= 2:
                        sc.wait_ge(s_sg, gi - 1)
                    sc.activation(gsd[:, gi % 2, :], psr[m2 % 4][:, :], AF.Silu,
                                  scale=wsd_sb[:, m2:m2 + 1])
                    sc.drain(fusable=True).then_inc(s_dsl, 1)

        @blk.vector
        def _(v):
            v.wait_ge(s_pr, 16 * 5)
            v.wait_ge(s_ld, 32)
            v.tensor_tensor(wprod[:, :], wsu_sb[:, :], sdp_sb[:, :], ALU.mult)
            v.drain(fusable=True).then_inc(s_wv, 1)
            for m in range(NDC):
                v.tensor_scalar(xf[:, m, :], xf[:, m, :], sug_sb[:, m:m + 1],
                                None, ALU.mult)
                v.drain(fusable=True).then_inc(s_su, 1)
            st = 8
            while st >= 1:
                for blk0 in range(0, NDC, 2 * st):
                    for c in range(blk0, blk0 + st):
                        a, b = xf[:, c, :], xf[:, c + st, :]
                        v.tensor_tensor(tmpb[:, :], a, b, ALU.subtract)
                        v.tensor_tensor(a, a, b, ALU.add)
                        v.tensor_copy(b, tmpb[:, :])
                st //= 2
            v.drain(fusable=True).then_inc(s_bf, 1)
            for c in range(NDC):
                v.wait_ge(s_fwm, c + 1)
                v.tensor_scalar(xf[:, c, 0:512], psr[(2 * c) % 4][:, :], RS2,
                                None, ALU.mult)
                v.tensor_scalar(xf[:, c, 512:1024], psr[(2 * c + 1) % 4][:, :], RS2,
                                None, ALU.mult)
                v.drain(fusable=True).then_inc(s_fa, 1)
            for th in range(2):
                for m in range(NM2):
                    gm = th * NM2 + m
                    v.wait_ge(s_sil, gm + 1)
                    v.wait_ge(s_mm, th * 144 + 2 * m + 2)
                    v.scalar_tensor_tensor(a_h[:, m, :], psr[(2 * m + 1) % 4][:, :],
                                           wprod[:, m:m + 1], gs[:, gm % 2, :],
                                           ALU.mult, ALU.mult)
                    v.drain(fusable=True).then_inc(s_am, 1)
                st = 32
                while st >= 1:
                    for blk0 in range(0, NM2, 2 * st):
                        for c in range(blk0, blk0 + st):
                            a, b = a_h[:, c, :], a_h[:, c + st, :]
                            v.tensor_tensor(tmp5[:, :], a, b, ALU.subtract)
                            v.tensor_tensor(a, a, b, ALU.add)
                            v.tensor_copy(b, tmp5[:, :])
                    st //= 2
                v.drain(fusable=True).then_inc(s_bf8, 1)
                for c in range(NM2):
                    v.wait_ge(s_fw8, th * NM2 + c + 1)
                    v.tensor_scalar(a_h[:, c, :], psr[c % 4][:, :], RS8,
                                    None, ALU.mult)
                    v.drain(fusable=True).then_inc(s_fa8, 1)
                for m2 in range(NDC):
                    gi = th * NDC + m2
                    ths = slice(512 * th, 512 * (th + 1))
                    v.wait_ge(s_dsl, gi + 1)
                    v.tensor_tensor(s_sb[:, m2, ths], s_sb[:, m2, ths],
                                    gsd[:, gi % 2, :], ALU.mult)
                    v.drain(fusable=True).then_inc(s_sg, 1)
            for tc in range(NCH):
                for dc in range(NDC):
                    gtr = 16 * tc + dc
                    v.wait_ge(s_ptr, gtr + 1)
                    v.wait_ge(s_xl[tc % 2], 16 * (tc // 2 + 1))
                    if tc >= 2 and dc == 0:
                        v.wait_ge(s_xs[tc % 2], 16 * (tc // 2))
                    v.tensor_tensor(x1_st[:, tc % 2, 128 * dc:128 * (dc + 1)],
                                    psT[gtr % 2][:, :],
                                    x_ld[:, tc % 2, 128 * dc:128 * (dc + 1)],
                                    ALU.add)
                    v.drain(fusable=True).then_inc(s_cmb, 1)

        @blk.tensor
        def _(t):
            t.wait_ge(s_bf, 1)
            for c in range(NDC):
                if c >= 2:
                    t.wait_ge(s_fa, c - 1)
                t.matmul(psr[(2 * c) % 4][:, :], H128[:, :], xf[:, c, 0:512],
                         start=True, stop=True)
                t.matmul(psr[(2 * c + 1) % 4][:, :], H128[:, :], xf[:, c, 512:1024],
                         start=True, stop=True)
                t.drain().then_inc(s_fwm, 1)
            for th in range(2):
                t.wait_ge(s_fa, NDC)
                t.wait_ge(s_wv, 1)
                for m in range(NM2):
                    for j in range(2):
                        si = th * 128 + 2 * m + j
                        t.wait_ge(s_wl[si % 2], 16 * (si // 2 + 1))
                        bank = psr[(2 * m + j) % 4]
                        if 2 * m + j >= 4:
                            pm = 2 * m + j - 4
                            if pm % 2 == 0:
                                t.wait_ge(s_sil, th * NM2 + pm // 2 + 1)
                            else:
                                t.wait_ge(s_am, th * NM2 + pm // 2 + 1)
                        elif th == 1:
                            t.wait_ge(s_dsl, NDC)
                        for k in range(NDC):
                            ths = slice(512 * th, 512 * (th + 1))
                            t.matmul(bank[:, :], wsl[:, si % 2, k, :],
                                     xf[:, k, ths], start=(k == 0),
                                     stop=(k == NDC - 1))
                        t.drain().then_inc(s_mm, 1)
                t.wait_ge(s_bf8, th + 1)
                for c in range(NM2):
                    if c >= 4:
                        t.wait_ge(s_fa8, th * NM2 + c - 3)
                    t.matmul(psr[c % 4][:, :], H128[:, :], a_h[:, c, :],
                             start=True, stop=True)
                    t.drain().then_inc(s_fw8, 1)
                t.wait_ge(s_fa8, NM2 * (th + 1))
                for m2 in range(NDC):
                    gi2 = th * NDC + m2
                    bank = psr[m2 % 4]
                    if gi2 >= 4:
                        t.wait_ge(s_dsl, gi2 - 3)
                    for kh in range(2):
                        si2 = th * 32 + 2 * m2 + kh
                        t.wait_ge(s_w2[si2 % 2], 16 * (si2 // 2 + 1))
                        for k in range(32):
                            t.matmul(bank[:, :], wsl2[:, si2 % 2, k, :],
                                     a_h[:, 32 * kh + k, :],
                                     start=(kh == 0 and k == 0),
                                     stop=(kh == 1 and k == 31))
                    t.drain().then_inc(s_mm, 1)
            t.wait_ge(s_sg, 2 * NDC)
            for tc in range(NCH):
                for dc in range(NDC):
                    gtr = 16 * tc + dc
                    if gtr >= 2:
                        t.wait_ge(s_cmb, gtr - 1)
                    t.transpose(psT[gtr % 2][:, :],
                                s_sb[:, dc, 128 * tc:128 * (tc + 1)],
                                identbf[:, :])
                    t.drain().then_inc(s_ptr, 1)

    pg.barrier()


def emit_p4(pg):
    nc = pg.nc
    x1_sp, xn1_sp, s2_sp = pg.io["x1_sp"], pg.io["xn1_sp"], pg.io["s2_sp"]

    identbf = pg.const("identbf", _bf16(np.eye(128, dtype=np.float32)), BF16)
    identf = pg.const("identf", np.eye(128, dtype=np.float32), F32)
    onesrow = pg.const("ones1x128f", np.ones((1, 128), np.float32), F32)
    pg.flush_consts()

    x1_ld = pg.sbuf("p4_x1", [128, 2, DM], F32)
    xn1T = pg.sbuf("p4_xT", [128, NCH, DM], BF16)
    xn1f = pg.sbuf("p4_xf", [128, NDC, TOK], BF16)
    junk = pg.sbuf("p4_junk", [128, DM], F32)
    stats = pg.sbuf("p4_stats", [128, 8], F32)
    est = pg.sbuf("p4_est", [128, DM], F32)
    ebuf = pg.sbuf("p4_ebuf", [128, 2, 256], F32)
    expT = pg.sbuf("p4_expT", [128, 2, 256], BF16)
    mbc = pg.sbuf("p4_mbc", [128, DM], F32)
    m_all = pg.sbuf("p4_mall", [128, NDC], F32)
    rz_all = pg.sbuf("p4_rz", [128, NDC], F32)
    mrow = pg.sbuf("p4_mrow", [1, DM], F32)
    e_st = pg.sbuf("p4_est2", [128, 2, DM], F32)
    s2st = pg.sbuf("p4_s2st", [128, 2, TOK], BF16)

    psT = [pg.psum(f"p4_psT{i}", [128, 128], BF16) for i in range(2)]
    psO = [pg.psum(f"p4_psO{i}", [128, TOK], F32) for i in range(2)]
    psS = pg.psum("p4_psS", [128, 512], F32)
    psR = pg.psum("p4_psR", [1, 128], F32)

    E_loc = nc.dram_tensor("p4_Eloc", [DM, DM], F32, kind="Internal")
    E_sum = nc.dram_tensor("p4_Esum", [DM, DM], F32, kind="Internal")

    s_xl = [pg.sem("p4_xl0"), pg.sem("p4_xl1")]
    cs = Chain(pg.sem("p4_cs"))
    cv = Chain(pg.sem("p4_cv"))
    s_sv = pg.sem("p4_sv")
    s_ptr = pg.sem("p4_ptr")
    s_vtr = pg.sem("p4_vtr")
    s_fms = [pg.sem("p4_fms0"), pg.sem("p4_fms1")]
    s_pe = pg.sem("p4_pe")
    s_ee = pg.sem("p4_ee")
    s_es = [pg.sem("p4_es0"), pg.sem("p4_es1")]
    s_cc = pg.sem("p4_cc")
    s_sl = pg.sem("p4_sl")
    s_sx = pg.sem("p4_sx")
    s_sm = pg.sem("p4_sm")
    s_mt = pg.sem("p4_mt")
    s_mr = pg.sem("p4_mr")
    s_mb = pg.sem("p4_mb")
    s_mbe = pg.sem("p4_mbe")
    s_el = [pg.sem("p4_el0"), pg.sem("p4_el1")]
    s_esb = pg.sem("p4_esb")
    s_ex = pg.sem("p4_ex")
    s_pa = pg.sem("p4_pa")
    s_paj = pg.sem("p4_paj")
    s_ae = pg.sem("p4_ae")
    s_ss = [pg.sem("p4_ss0"), pg.sem("p4_ss1")]

    with nc.Block() as blk:
        @blk.sync
        def _(s):
            for tc in range(NCH + 2):
                if tc < NCH:
                    if tc >= 2:
                        s.wait_ge(s_sv, tc - 1)
                    s.dma_start(out=x1_ld[:, tc % 2, :],
                                in_=x1_sp.ap()[128 * tc:128 * (tc + 1), :]
                                ).then_inc(s_xl[tc % 2], 16)
                if tc >= 2:
                    s.wait_ge(s_vtr, 16 * (tc - 1))
                    s.dma_start(out=xn1_sp.ap()[:, :, 128 * (tc - 2):128 * (tc - 1)]
                                .rearrange("c p t -> p c t"),
                                in_=xn1f[:, :, 128 * (tc - 2):128 * (tc - 1)]
                                ).then_inc(s_fms[tc % 2], 16)
            for i in range(NDC):
                s.wait_ge(s_cc, 1)
                if i >= 1:
                    s.wait_ge(s_sx, i)
                s.dma_start(out=est[:, :], in_=E_sum.ap()[128 * i:128 * (i + 1), :]
                            ).then_inc(s_sl, 16)
            for g in range(8):
                for J in range(NDC):
                    li = NDC * g + J
                    if li >= 2:
                        s.wait_ge(s_ex, li - 1)
                    s.dma_start(out=ebuf[:, li % 2, :],
                                in_=E_sum.ap()[128 * J:128 * (J + 1),
                                               256 * g:256 * g + 256]
                                ).then_inc(s_el[li % 2], 16)
                if g >= 1:
                    for il in range(2):
                        gi = 2 * (g - 1) + il
                        s.wait_ge(s_ae, gi + 1)
                        s.dma_start(out=s2_sp.ap()[gi, :, :],
                                    in_=s2st[:, gi % 2, :]).then_inc(s_ss[gi % 2], 16)
            for il in range(2):
                gi = 14 + il
                s.wait_ge(s_ae, gi + 1)
                s.dma_start(out=s2_sp.ap()[gi, :, :],
                            in_=s2st[:, gi % 2, :]).then_inc(s_ss[gi % 2], 16)
            s.wait_ge(s_ss[0], 16 * 8)
            s.wait_ge(s_ss[1], 16 * 8)

        @blk.gpsimd
        def _(g):
            for m in range(NDC):
                g.wait_ge(s_ee, 4 * (m + 1))
                g.dma_start(out=E_loc.ap()[128 * m:128 * (m + 1), :],
                            in_=e_st[:, m % 2, :]).then_inc(s_es[m % 2], 16)
            g.wait_ge(s_es[0], 16 * 8)
            g.wait_ge(s_es[1], 16 * 8)
            g.collective_compute(
                "AllReduce", ALU.add,
                replica_groups=[[0, 1, 2, 3], [4, 5, 6, 7]],
                ins=[E_loc.ap().opt()], outs=[E_sum.ap().opt()],
            ).then_inc(s_cc, 1)

        @blk.scalar
        def _(sc):
            for tc in range(NCH):
                xi = x1_ld[:, tc % 2, :]
                sc.wait_ge(s_xl[tc % 2], 16 * (tc // 2 + 1))
                if tc >= 1:
                    sc.wait_ge(cv.sem, 6 * (tc - 1) + 3)
                cs.step(sc, sc.activation(junk[:, :], xi, AF.Square,
                                          accum_out=stats[:, 0:1]))
                sc.wait_ge(cv.sem, 6 * tc + 5)
                cs.step(sc, sc.activation(stats[:, 4:5], stats[:, 4:5], AF.Ln))
                cs.sync(sc)
                cs.step(sc, sc.activation(stats[:, 5:6], stats[:, 4:5], AF.Exp,
                                          scale=-0.5))
                cs.step(sc, sc.activation(junk[:, 0:1], stats[:, 0:1], AF.Copy))
            for i in range(NDC):
                sc.wait_ge(s_sm, i + 1)
                sc.activation(junk[:, :], est[:, :], AF.Exp,
                              bias=stats[:, 6:7], accum_out=stats[:, 7:8])
                sc.drain(fusable=True).then_inc(s_sx, 1)
            for g in range(8):
                for J in range(NDC):
                    li = NDC * g + J
                    sc.wait_ge(s_esb, li + 1)
                    if li >= 2:
                        sc.wait_ge(s_paj, li - 1)
                    sc.activation(expT[:, li % 2, :], ebuf[:, li % 2, :], AF.Exp)
                    sc.drain(fusable=True).then_inc(s_ex, 1)

        @blk.vector
        def _(v):
            for tc in range(NCH):
                xi = x1_ld[:, tc % 2, :]
                v.wait_ge(s_xl[tc % 2], 16 * (tc // 2 + 1))
                cv.step(v, v.reduce_sum(stats[:, 1:2], xi, axis=AX.X))
                v.wait_ge(cs.sem, 4 * tc + 1)
                cv.sync(v)
                cv.step(v, v.tensor_scalar(stats[:, 2:3], stats[:, 1:2], 1.0 / DM, None, ALU.mult))
                cv.step(v, v.tensor_scalar(stats[:, 3:4], stats[:, 0:1], 1.0 / DM, None, ALU.mult))
                cv.sync(v)
                cv.step(v, v.tensor_tensor(stats[:, 4:5], stats[:, 2:3], stats[:, 2:3], ALU.mult))
                cv.sync(v)
                cv.step(v, v.tensor_tensor(stats[:, 4:5], stats[:, 3:4], stats[:, 4:5], ALU.subtract))
                cv.sync(v)
                cv.step(v, v.tensor_scalar(stats[:, 4:5], stats[:, 4:5], EPS, None,
                                           ALU.add))
                v.wait_ge(cs.sem, 4 * tc + 3)
                if tc >= 1:
                    v.wait_ge(s_ptr, 16 * tc)
                v.tensor_scalar(xn1T[:, tc, :], xi, stats[:, 2:3], stats[:, 5:6],
                                ALU.subtract, ALU.mult)
                v.drain(fusable=True).then_inc(s_sv, 1)
                for dc in range(NDC):
                    gtr = 16 * tc + dc
                    v.wait_ge(s_ptr, gtr + 1)
                    if tc >= 2 and dc == 0:
                        v.wait_ge(s_fms[tc % 2], 16 * (tc // 2))
                    v.tensor_copy(xn1f[:, dc, 128 * tc:128 * (tc + 1)],
                                  psT[gtr % 2][:, :])
                    v.drain(fusable=True).then_inc(s_vtr, 1)
            for m in range(NDC):
                for jb in range(4):
                    gi = 4 * m + jb
                    v.wait_ge(s_pe, gi + 1)
                    if m >= 2 and jb == 0:
                        v.wait_ge(s_es[m % 2], 16 * (m // 2))
                    v.tensor_copy(e_st[:, m % 2, 512 * jb:512 * (jb + 1)],
                                  psO[gi % 2][:, 0:512])
                    v.drain(fusable=True).then_inc(s_ee, 1)
            for i in range(NDC):
                v.wait_ge(s_sl, 16 * (i + 1))
                cv.step(v, v.reduce_max(stats[:, 6:7], est[:, :], axis=AX.X))
                cv.sync(v)
                cv.step(v, v.tensor_scalar(stats[:, 6:7], stats[:, 6:7], -1.0,
                                           None, ALU.mult))
                cv.sync(v)
                cv.step(v, v.tensor_copy(m_all[:, i:i + 1], stats[:, 6:7]))
                v.drain(fusable=True).then_inc(s_sm, 1)
                v.wait_ge(s_sx, i + 1)
                cv.step(v, v.reciprocal(stats[:, 7:8], stats[:, 7:8]))
                cv.sync(v)
                cv.step(v, v.tensor_copy(rz_all[:, i:i + 1], stats[:, 7:8]))
            for i in range(NDC):
                v.wait_ge(s_mt, i + 1)
                cv.step(v, v.tensor_copy(mrow[0:1, 128 * i:128 * (i + 1)],
                                         psR[0:1, :]))
                v.drain(fusable=True).then_inc(s_mr, 1)
            for j in range(4):
                v.wait_ge(s_mb, j + 1)
                v.tensor_copy(mbc[:, 512 * j:512 * (j + 1)], psS[:, 0:512])
                v.drain(fusable=True).then_inc(s_mbe, 1)
            for g in range(8):
                for J in range(NDC):
                    li = NDC * g + J
                    v.wait_ge(s_el[li % 2], 16 * (li // 2 + 1))
                    if li >= 2:
                        v.wait_ge(s_ex, li - 1)
                    v.tensor_tensor(ebuf[:, li % 2, :], ebuf[:, li % 2, :],
                                    mbc[:, 256 * g:256 * g + 256], ALU.add)
                    v.drain(fusable=True).then_inc(s_esb, 1)
                for il in range(2):
                    gi = 2 * g + il
                    v.wait_ge(s_pa, gi + 1)
                    if gi >= 2:
                        v.wait_ge(s_ss[gi % 2], 16 * (gi // 2))
                    v.tensor_scalar(s2st[:, gi % 2, :], psO[il][:, :],
                                    rz_all[:, gi:gi + 1], None, ALU.mult)
                    v.drain(fusable=True).then_inc(s_ae, 1)

        @blk.tensor
        def _(t):
            for tc in range(NCH):
                t.wait_ge(s_sv, tc + 1)
                for dc in range(NDC):
                    gtr = 16 * tc + dc
                    if gtr >= 2:
                        t.wait_ge(s_vtr, gtr - 1)
                    t.transpose(psT[gtr % 2][:, :],
                                xn1T[:, tc, 128 * dc:128 * (dc + 1)],
                                identbf[:, :])
                    t.drain().then_inc(s_ptr, 1)
            for m in range(NDC):
                for jb in range(4):
                    gi = 4 * m + jb
                    if gi >= 2:
                        t.wait_ge(s_ee, gi - 1)
                    for tc in range(NCH):
                        t.matmul(psO[gi % 2][:, 0:512],
                                 xn1T[:, tc, 128 * m:128 * (m + 1)],
                                 xn1T[:, tc, 512 * jb:512 * (jb + 1)],
                                 start=(tc == 0), stop=(tc == NCH - 1))
                    t.drain().then_inc(s_pe, 1)
            for i in range(NDC):
                t.wait_ge(s_sm, i + 1)
                if i >= 1:
                    t.wait_ge(s_mr, i)
                t.transpose(psR[0:1, :], m_all[:, i:i + 1], identf[:, :])
                t.drain().then_inc(s_mt, 1)
            for j in range(4):
                t.wait_ge(s_mr, NDC)
                if j >= 1:
                    t.wait_ge(s_mbe, j)
                t.matmul(psS[:, 0:512], onesrow[:, :],
                         mrow[0:1, 512 * j:512 * (j + 1)], start=True, stop=True)
                t.drain().then_inc(s_mb, 1)
            t.wait_ge(s_mbe, 4)
            for g in range(8):
                for il in range(2):
                    gi = 2 * g + il
                    if gi >= 2:
                        t.wait_ge(s_ae, gi - 1)
                for J in range(NDC):
                    li = NDC * g + J
                    t.wait_ge(s_ex, li + 1)
                    for il in range(2):
                        for nb in range(2):
                            t.matmul(psO[il][:, 512 * nb:512 * (nb + 1)],
                                     expT[:, li % 2, 128 * il:128 * (il + 1)],
                                     xn1f[:, J, 512 * nb:512 * (nb + 1)],
                                     start=(J == 0), stop=(J == NDC - 1))
                    t.drain().then_inc(s_paj, 1)
                t.drain().then_inc(s_pa, 2)

    pg.barrier()


def emit_p5(pg):
    nc = pg.nc
    xn1_sp, x1_sp, s2_sp = pg.io["xn1_sp"], pg.io["x1_sp"], pg.io["s2_sp"]
    out_t = pg.io["out"]
    mgT, muT, mdT = pg.io["mgT"], pg.io["muT"], pg.io["mdT"]
    sgT, suT, sdT = pg.io["sgT"], pg.io["suT"], pg.io["sdT"]

    identbf = pg.const("identbf", _bf16(np.eye(128, dtype=np.float32)), BF16)
    identf = pg.const("identf", np.eye(128, dtype=np.float32), F32)
    sel16 = np.zeros((16, 16, 128), np.float32)
    for e in range(16):
        sel16[e, e, :] = 1.0
    sel16c = pg.const("sel16f", sel16, F32)
    pg.flush_consts()

    BIG = 1e30
    xn1f = pg.sbuf("p5_xn", [128, NDC, TOK], BF16)
    s2_sb = pg.sbuf("p5_s2", [128, NDC, TOK], BF16)
    gw_sb = pg.sbuf("p5_gw", [128, NDC, 16], BF16)
    eb_sb = pg.sbuf("p5_eb", [16, 1], F32)
    scoF = pg.sbuf("p5_sco", [16, TOK], F32)
    biF = pg.sbuf("p5_bi", [16, TOK], F32)
    scT = pg.sbuf("p5_scT", [128, NCH, 16], F32)
    biT = pg.sbuf("p5_biT", [128, NCH, 16], F32)
    wk = pg.sbuf("p5_wk", [128, 64], F32)
    msk0 = pg.sbuf("p5_msk", [128, 16], F32)
    cwTe = pg.sbuf("p5_cwT", [16, NCH, 128], F32)
    cwb = pg.sbuf("p5_cwb", [128, 2, 512], BF16)
    aE = pg.sbuf("p5_aE", [128, 2, 8, 512], BF16)
    a1 = pg.sbuf("p5_a1", [128, 512], BF16)
    gsE = pg.sbuf("p5_gsE", [128, 2, 512], BF16)
    g2 = pg.sbuf("p5_g2", [128, NDC, 512], F32)
    gg = pg.sbuf("p5_gg", [128, 2, 512], BF16)
    wsl = pg.sbuf("p5_wsl", [128, 2, NDC, 128], BF16)
    wsl2 = pg.sbuf("p5_wsl2", [128, 2, 8, 128], BF16)
    x1_ld = pg.sbuf("p5_x1", [128, 2, DM], F32)
    o_st = pg.sbuf("p5_ost", [128, 2, DM], F32)

    pga = pg.psum("p5_pga", [128, TOK], F32)
    pua = pg.psum("p5_pua", [128, TOK], F32)
    pdn = pg.psum("p5_pdn", [128, TOK], F32)
    psm = pg.psum("p5_psm", [128, 512], F32)
    psT = pg.psum("p5_psT", [128, 128], BF16)
    psTf = None

    s_ld = pg.sem("p5_ld")
    s_gw = pg.sem("p5_gw")
    s_lg = pg.sem("p5_lg")
    s_sig = pg.sem("p5_sig")
    s_bi = pg.sem("p5_bi")
    s_rt = pg.sem("p5_rt")
    s_rte = pg.sem("p5_rte")
    s_cw = pg.sem("p5_cw")
    s_cwt = pg.sem("p5_cwt")
    s_cwe = pg.sem("p5_cwe")
    s_wl = [pg.sem("p5_wl0"), pg.sem("p5_wl1")]
    s_w2 = [pg.sem("p5_w20"), pg.sem("p5_w21")]
    s_mm = pg.sem("p5_mm")
    s_md = pg.sem("p5_md")
    s_cb = pg.sem("p5_cb")
    s_cbe = pg.sem("p5_cbe")
    s_sil = pg.sem("p5_sil")
    s_a1 = pg.sem("p5_a1")
    s_am = pg.sem("p5_am")
    s_ga = pg.sem("p5_ga")
    s_fs = pg.sem("p5_fs")
    s_sg = pg.sem("p5_sg")
    s_ptr = pg.sem("p5_ptr")
    s_cmb = pg.sem("p5_cmb")
    s_xl = [pg.sem("p5_xl0"), pg.sem("p5_xl1")]
    s_os = [pg.sem("p5_os0"), pg.sem("p5_os1")]
    cv = Chain(pg.sem("p5_cv"))

    GU_H = 16 + 16 * 16
    DN_H = 16 + 8 * 16

    def gu_cnt(th, ei, fc, j):
        base = th * GU_H
        if ei < 0:
            return base + 2 * fc + j + 1
        return base + 16 + ei * 16 + 2 * fc + j + 1

    def dn_cnt(th, pi, m2):
        base = th * DN_H
        if pi < 0:
            return base + m2 + 1
        return base + 16 + pi * 16 + m2 + 1

    with nc.Block() as blk:
        @blk.sync
        def _(s):
            s.dma_start(out=eb_sb[...], in_=pg.io["eb16"]).then_inc(s_ld, 16)
            s.dma_start(out=xn1f[...],
                        in_=xn1_sp.ap().rearrange("m p t -> p m t")).then_inc(s_ld, 16)
            s.dma_start(out=s2_sb[...],
                        in_=s2_sp.ap().rearrange("m p t -> p m t")).then_inc(s_ld, 16)
            for tc in range(NCH + 2):
                if tc < NCH:
                    if tc >= 2:
                        s.wait_ge(s_cmb, 16 * (tc - 1))
                    s.dma_start(out=x1_ld[:, tc % 2, :],
                                in_=x1_sp.ap()[128 * tc:128 * (tc + 1), :]
                                ).then_inc(s_xl[tc % 2], 16)
                if tc >= 2:
                    s.wait_ge(s_cmb, 16 * (tc - 1))
                    s.dma_start(out=out_t[128 * (tc - 2):128 * (tc - 1), :],
                                in_=o_st[:, tc % 2, :]).then_inc(s_os[tc % 2], 16)
            s.wait_ge(s_os[0], 16 * 4)
            s.wait_ge(s_os[1], 16 * 4)

        @blk.gpsimd
        def _(g):
            g.dma_start(out=gw_sb[...],
                        in_=pg.io["gw1T"].rearrange("(c p) e -> p c e", p=128)
                        ).then_inc(s_gw, 16)
            def gu_slab_consumer(si):
                thp, r = divmod(si, 272)
                eip = r // 16 - 1
                fcp, jp = (r % 16) // 2, (r % 16) % 2
                return gu_cnt(thp, eip, fcp, jp)

            def dn_slab_consumer(si2):
                thp, r = divmod(si2, 272)
                if r < 16:
                    return dn_cnt(thp, -1, r)
                r2 = r - 16
                return dn_cnt(thp, r2 // 32, (r2 % 32) // 2)

            def emit_gu_slabs(g, th, ei):
                for fc in range(8):
                    for j in range(2):
                        si = th * 272 + (ei + 1) * 16 + 2 * fc + j
                        if ei < 0:
                            wt = (sgT if j == 0 else suT)[
                                :, 128 * fc:128 * (fc + 1)]
                        else:
                            wt = (mgT if j == 0 else muT)[
                                ei, :, 128 * fc:128 * (fc + 1)]
                        if si >= 2:
                            g.wait_ge(s_mm, gu_slab_consumer(si - 2))
                        g.dma_start(out=wsl[:, si % 2, :, :],
                                    in_=wt.rearrange("(c p) m -> p c m", p=128)
                                    ).then_inc(s_wl[si % 2], 16)

            def emit_dn_slabs(g, th, pi):
                for m2 in range(NDC):
                    for eo in range(2 if pi >= 0 else 1):
                        if pi < 0:
                            si2 = th * 272 + m2
                            wt = sdT[:, 128 * m2:128 * (m2 + 1)]
                        else:
                            si2 = th * 272 + 16 + 32 * pi + 2 * m2 + eo
                            wt = mdT[2 * pi + eo, :, 128 * m2:128 * (m2 + 1)]
                        if si2 >= 2:
                            g.wait_ge(s_md, dn_slab_consumer(si2 - 2))
                        g.dma_start(out=wsl2[:, si2 % 2, :, :],
                                    in_=wt.rearrange("(c p) m -> p c m", p=128)
                                    ).then_inc(s_w2[si2 % 2], 16)

            for th in range(2):
                emit_gu_slabs(g, th, -1)
                emit_dn_slabs(g, th, -1)
                for pi in range(8):
                    emit_gu_slabs(g, th, 2 * pi)
                    emit_gu_slabs(g, th, 2 * pi + 1)
                    emit_dn_slabs(g, th, pi)

        @blk.scalar
        def _(sc):
            sc.wait_ge(s_lg, 1)
            sc.activation(scoF[:, 0:TOK], pga[0:16, 0:TOK], AF.Sigmoid)
            sc.drain(fusable=True).then_inc(s_sig, 1)
            for th in range(2):
                for ei in range(-1, 16):
                    for fc in range(8):
                        gid = th * 136 + (ei + 1) * 8 + fc
                        sc.wait_ge(s_mm, gu_cnt(th, ei, fc, 0))
                        if gid >= 2:
                            sc.wait_ge(s_a1, gid - 1)
                        sc.activation(gsE[:, gid % 2, :], pga[:, 0:512]
                                      if fc % 2 == 0 else pga[:, 512:1024], AF.Silu)
                        sc.drain(fusable=True).then_inc(s_sil, 1)
                for m2 in range(NDC):
                    fid = th * NDC + m2
                    sc.wait_ge(s_ga, 144 * (th + 1))
                    if fid >= 2:
                        sc.wait_ge(s_sg, fid - 1)
                    sc.activation(gg[:, fid % 2, :], g2[:, m2, :], AF.Silu)
                    sc.drain(fusable=True).then_inc(s_fs, 1)

        @blk.vector
        def _(v):
            v.wait_ge(s_ld, 48)
            v.wait_ge(s_sig, 1)
            v.tensor_scalar(biF[:, :], scoF[:, :], eb_sb[:, 0:1], None, ALU.add)
            v.drain(fusable=True).then_inc(s_bi, 1)
            for tc in range(NCH):
                for j in range(2):
                    gi = 2 * tc + j
                    v.wait_ge(s_rt, gi + 1)
                    dst = scT if j == 0 else biT
                    v.tensor_copy(dst[:, tc, :], psm[:, 0:16])
                    v.drain(fusable=True).then_inc(s_rte, 1)
            for tc in range(NCH):
                v.wait_ge(s_rte, 2 * tc + 2)
                if tc >= 1:
                    v.wait_ge(s_cwt, tc)
                b3 = biT[:, tc, :].rearrange("p (g s) -> p g s", g=4)
                W = wk
                cv.step(v, v.tensor_tensor(W[:, 0:4], b3[:, :, 0], b3[:, :, 1], ALU.max))
                cv.step(v, v.tensor_tensor(W[:, 4:8], b3[:, :, 0], b3[:, :, 1], ALU.min))
                cv.step(v, v.tensor_tensor(W[:, 8:12], b3[:, :, 2], b3[:, :, 3], ALU.max))
                cv.step(v, v.tensor_tensor(W[:, 12:16], b3[:, :, 2], b3[:, :, 3], ALU.min))
                cv.sync(v)
                cv.step(v, v.tensor_tensor(W[:, 16:20], W[:, 0:4], W[:, 8:12], ALU.max))
                cv.step(v, v.tensor_tensor(W[:, 20:24], W[:, 0:4], W[:, 8:12], ALU.min))
                cv.step(v, v.tensor_tensor(W[:, 24:28], W[:, 4:8], W[:, 12:16], ALU.max))
                cv.sync(v)
                cv.step(v, v.tensor_tensor(W[:, 28:32], W[:, 20:24], W[:, 24:28], ALU.max))
                cv.sync(v)
                cv.step(v, v.tensor_tensor(W[:, 32:36], W[:, 16:20], W[:, 28:32], ALU.add))
                cv.sync(v)
                cv.step(v, v.reduce_max(W[:, 36:37], W[:, 32:36], axis=AX.X))
                cv.sync(v)
                cv.step(v, v.tensor_scalar(W[:, 40:44], W[:, 32:36], W[:, 36:37],
                                           None, ALU.is_equal))
                cv.sync(v)
                cv.step(v, v.tensor_scalar(W[:, 40:44], W[:, 40:44], BIG, None, ALU.mult))
                cv.sync(v)
                cv.step(v, v.tensor_tensor(W[:, 44:48], W[:, 32:36], W[:, 40:44], ALU.subtract))
                cv.sync(v)
                cv.step(v, v.reduce_max(W[:, 36:37], W[:, 44:48], axis=AX.X))
                cv.sync(v)
                cv.step(v, v.tensor_scalar(W[:, 48:52], W[:, 32:36], W[:, 36:37],
                                           None, ALU.is_ge))
                cv.sync(v)
                m0 = msk0[:, :].rearrange("p (g s) -> p g s", g=4)
                for sslot in range(4):
                    cv.step(v, v.tensor_scalar(m0[:, :, sslot], W[:, 48:52], 1.0,
                                               None, ALU.subtract))
                cv.sync(v)
                cv.step(v, v.tensor_scalar(msk0[:, :], msk0[:, :], BIG, None, ALU.mult))
                cv.sync(v)
                cv.step(v, v.tensor_tensor(msk0[:, :], biT[:, tc, :], msk0[:, :], ALU.add))
                cv.sync(v)
                cv.step(v, v.tensor_copy(W[:, 48:64], msk0[:, :]))
                cv.sync(v)
                for it in range(3):
                    cv.step(v, v.reduce_max(W[:, 36:37], W[:, 48:64], axis=AX.X))
                    cv.sync(v)
                    cv.step(v, v.tensor_scalar(W[:, 0:16], W[:, 48:64], W[:, 36:37],
                                               None, ALU.is_equal))
                    cv.sync(v)
                    cv.step(v, v.tensor_scalar(W[:, 0:16], W[:, 0:16], BIG, None, ALU.mult))
                    cv.sync(v)
                    cv.step(v, v.tensor_tensor(W[:, 48:64], W[:, 48:64], W[:, 0:16],
                                               ALU.subtract))
                    cv.sync(v)
                cv.step(v, v.reduce_max(W[:, 36:37], W[:, 48:64], axis=AX.X))
                cv.sync(v)
                cv.step(v, v.tensor_scalar(W[:, 0:16], msk0[:, :], W[:, 36:37],
                                           None, ALU.is_ge))
                cv.sync(v)
                cv.step(v, v.tensor_tensor(W[:, 16:32], scT[:, tc, :], W[:, 0:16],
                                           ALU.mult))
                cv.sync(v)
                cv.step(v, v.reduce_sum(W[:, 36:37], W[:, 16:32], axis=AX.X))
                cv.sync(v)
                cv.step(v, v.tensor_scalar(W[:, 36:37], W[:, 36:37], 1e-20, None, ALU.add))
                cv.sync(v)
                cv.step(v, v.reciprocal(W[:, 37:38], W[:, 36:37]))
                cv.sync(v)
                cv.step(v, v.tensor_scalar(W[:, 16:32], W[:, 16:32], W[:, 37:38],
                                           2.5, ALU.mult, ALU.mult))
                v.drain(fusable=True).then_inc(s_cw, 1)
                v.wait_ge(s_cwt, tc + 1)
                v.tensor_copy(cwTe[:, tc, :], psm[0:16, 16:144])
                v.drain(fusable=True).then_inc(s_cwe, 1)
            for th in range(2):
                ths = slice(512 * th, 512 * (th + 1))
                for ei in range(-1, 16):
                    if ei >= 0:
                        v.wait_ge(s_md, th * 144 + 16 + (ei // 2) * 16)
                    if ei >= 0:
                        v.wait_ge(s_cb, th * 16 + ei + 1)
                        v.tensor_copy(cwb[:, ei % 2, :], psm[:, 0:512])
                        v.drain(fusable=True).then_inc(s_cbe, 1)
                    for fc in range(8):
                        gid = th * 136 + (ei + 1) * 8 + fc
                        v.wait_ge(s_sil, gid + 1)
                        v.wait_ge(s_mm, gu_cnt(th, ei, fc, 1))
                        v.tensor_tensor(a1[:, :], gsE[:, gid % 2, :],
                                        pua[:, 0:512] if fc % 2 == 0
                                        else pua[:, 512:1024], ALU.mult)
                        v.drain(fusable=True).then_inc(s_a1, 1)
                        if ei < 0:
                            v.tensor_copy(aE[:, 0, fc, :], a1[:, :])
                        else:
                            v.tensor_tensor(aE[:, ei % 2, fc, :], a1[:, :],
                                            cwb[:, ei % 2, :], ALU.mult)
                        v.drain(fusable=True).then_inc(s_am, 1)
                    if ei == -1:
                        for m2 in range(NDC):
                            v.wait_ge(s_md, dn_cnt(th, -1, m2))
                            v.tensor_copy(g2[:, m2, :], pdn[:, 0:512]
                                          if m2 % 2 == 0 else pdn[:, 512:1024])
                            v.drain(fusable=True).then_inc(s_ga, 1)
                    elif ei % 2 == 1:
                        pi = ei // 2
                        for m2 in range(NDC):
                            v.wait_ge(s_md, dn_cnt(th, pi, m2))
                            v.tensor_tensor(g2[:, m2, :], g2[:, m2, :],
                                            pdn[:, 0:512] if m2 % 2 == 0
                                            else pdn[:, 512:1024], ALU.add)
                            v.drain(fusable=True).then_inc(s_ga, 1)
                for m2 in range(NDC):
                    fid = th * NDC + m2
                    v.wait_ge(s_fs, fid + 1)
                    v.tensor_tensor(s2_sb[:, m2, ths], s2_sb[:, m2, ths],
                                    gg[:, fid % 2, :], ALU.mult)
                    v.drain(fusable=True).then_inc(s_sg, 1)
            for tc in range(NCH):
                for dc in range(NDC):
                    gtr = 16 * tc + dc
                    v.wait_ge(s_ptr, gtr + 1)
                    v.wait_ge(s_xl[tc % 2], 16 * (tc // 2 + 1))
                    if tc >= 2 and dc == 0:
                        v.wait_ge(s_os[tc % 2], 16 * (tc // 2))
                    v.tensor_tensor(o_st[:, tc % 2, 128 * dc:128 * (dc + 1)],
                                    psT[:, :], x1_ld[:, tc % 2, 128 * dc:128 * (dc + 1)],
                                    ALU.add)
                    v.drain(fusable=True).then_inc(s_cmb, 1)

        @blk.tensor
        def _(t):
            t.wait_ge(s_gw, 16)
            t.wait_ge(s_ld, 32)
            for k in range(NDC):
                for nb in range(2):
                    t.matmul(pga[0:16, 512 * nb:512 * (nb + 1)], gw_sb[:, k, :],
                             xn1f[:, k, 512 * nb:512 * (nb + 1)],
                             start=(k == 0), stop=(k == NDC - 1))
            t.drain().then_inc(s_lg, 1)
            for tc in range(NCH):
                for j in range(2):
                    gi = 2 * tc + j
                    src = scoF if j == 0 else biF
                    if j == 1:
                        t.wait_ge(s_bi, 1)
                    else:
                        t.wait_ge(s_sig, 1)
                    if gi >= 1:
                        t.wait_ge(s_rte, gi)
                    t.transpose(psm[0:128, 0:16], src[:, 128 * tc:128 * (tc + 1)],
                                identf[0:16, 0:16])
                    t.drain().then_inc(s_rt, 1)
            for tc in range(NCH):
                t.wait_ge(s_cw, tc + 1)
                if tc >= 1:
                    t.wait_ge(s_cwe, tc)
                t.transpose(psm[0:16, 16:144], wk[:, 16:32], identf[:, :])
                t.drain().then_inc(s_cwt, 1)
            for th in range(2):
                ths = slice(512 * th, 512 * (th + 1))
                for ei in range(-1, 16):
                    if ei >= 0:
                        t.wait_ge(s_cwe, NCH)
                        if th * 16 + ei >= 2:
                            t.wait_ge(s_cbe, th * 16 + ei - 1)
                        cwv = cwTe[:, :, :].rearrange("e c t -> e (c t)")
                        t.matmul(psm[:, 0:512], sel16c[:, ei, :], cwv[:, ths],
                                 start=True, stop=True)
                        t.drain().then_inc(s_cb, 1)
                    for fc in range(8):
                        for j in range(2):
                            si = th * 272 + (ei + 1) * 16 + 2 * fc + j
                            t.wait_ge(s_wl[si % 2], 16 * (si // 2 + 1))
                            gid = th * 136 + (ei + 1) * 8 + fc
                            bank = pga if j == 0 else pua
                            half = slice(0, 512) if fc % 2 == 0 else slice(512, 1024)
                            if gid >= 2:
                                if j == 0:
                                    t.wait_ge(s_sil, gid - 1)
                                else:
                                    t.wait_ge(s_a1, gid - 1)
                            elif th == 0 and ei == -1 and fc < 2 and j == 0:
                                t.wait_ge(s_sig, 1)
                            for k in range(NDC):
                                t.matmul(bank[:, half], wsl[:, si % 2, k, :],
                                         xn1f[:, k, ths], start=(k == 0),
                                         stop=(k == NDC - 1))
                            t.drain().then_inc(s_mm, 1)
                    if ei == -1:
                        for m2 in range(NDC):
                            si2 = th * 272 + m2
                            t.wait_ge(s_w2[si2 % 2], 16 * (si2 // 2 + 1))
                            t.wait_ge(s_am, th * 136 + 8)
                            half = slice(0, 512) if m2 % 2 == 0 else slice(512, 1024)
                            gseq = th * 144 + m2
                            if gseq >= 2:
                                t.wait_ge(s_ga, gseq - 1)
                            for k in range(8):
                                t.matmul(pdn[:, half], wsl2[:, si2 % 2, k, :],
                                         aE[:, 0, k, :], start=(k == 0),
                                         stop=(k == 7))
                            t.drain().then_inc(s_md, 1)
                    elif ei % 2 == 1:
                        pi = ei // 2
                        for m2 in range(NDC):
                            si2 = th * 272 + 16 + 32 * pi + 2 * m2
                            t.wait_ge(s_am, th * 136 + (ei + 2) * 8)
                            half = slice(0, 512) if m2 % 2 == 0 else slice(512, 1024)
                            gseq = th * 144 + 16 + pi * 16 + m2
                            if gseq >= 2:
                                t.wait_ge(s_ga, gseq - 1)
                            for eo in range(2):
                                sii = si2 + eo
                                t.wait_ge(s_w2[sii % 2], 16 * (sii // 2 + 1))
                                for k in range(8):
                                    t.matmul(pdn[:, half], wsl2[:, sii % 2, k, :],
                                             aE[:, eo, k, :],
                                             start=(eo == 0 and k == 0),
                                             stop=(eo == 1 and k == 7))
                            t.drain().then_inc(s_md, 1)
            t.wait_ge(s_sg, 2 * NDC)
            for tc in range(NCH):
                for dc in range(NDC):
                    gtr = 16 * tc + dc
                    if gtr >= 1:
                        t.wait_ge(s_cmb, gtr)
                    t.transpose(psT[:, :], s2_sb[:, dc, 128 * tc:128 * (tc + 1)],
                                identbf[:, :])
                    t.drain().then_inc(s_ptr, 1)

    pg.barrier()


def _col128(v):
    v = np.ascontiguousarray(v, dtype=np.float32)
    return np.ascontiguousarray(v.reshape(-1, 128).T)


def host_layout(inputs):
    T = lambda a: np.ascontiguousarray(np.asarray(a, dtype=np.float32).T)
    w = {}
    w["ipT"] = T(inputs["L0_attn_in_proj_w"])
    w["opT"] = T(inputs["L0_attn_out_proj_w"])
    w["gwT"] = T(inputs["L0_attn_gate_w"])
    w["uwT"] = T(inputs["L0_attn_up_w"])
    w["dwT"] = T(inputs["L0_attn_down_w"])
    w["fupT"] = T(inputs["L0_ffn_W_up"])
    w["fgT"] = T(inputs["L0_ffn_W_gate"])
    w["fdT"] = T(inputs["L0_ffn_W_down"])
    w["mgT"] = np.ascontiguousarray(np.asarray(inputs["L1_ffn_gate_proj"], dtype=np.float32).transpose(0, 2, 1))
    w["muT"] = np.ascontiguousarray(np.asarray(inputs["L1_ffn_up_proj"], dtype=np.float32).transpose(0, 2, 1))
    w["mdT"] = np.ascontiguousarray(np.asarray(inputs["L1_ffn_down_proj"], dtype=np.float32).transpose(0, 2, 1))
    w["sgT"] = T(inputs["L1_ffn_shared_gate"])
    w["suT"] = T(inputs["L1_ffn_shared_up"])
    w["sdT"] = T(inputs["L1_ffn_shared_down"])
    w["gw1T"] = T(inputs["L1_ffn_gate_weight"])
    w["eb16"] = np.ascontiguousarray(np.asarray(inputs["L1_ffn_expert_bias"], dtype=np.float32).reshape(16, 1))
    w["zb128"] = _col128(inputs["L0_attn_z_bias"])
    w["cb128"] = _col128(inputs["L0_attn_conv_b"])
    cw = np.ascontiguousarray(np.asarray(inputs["L0_attn_conv_w"], dtype=np.float32))
    w["cw128"] = np.ascontiguousarray(cw.reshape(48, 128, 4).transpose(1, 0, 2))
    w["n1w"] = _col128(inputs["L0_attn_norm1_w"])
    w["n2w"] = _col128(inputs["L0_attn_norm2_w"])
    w["sug"] = _col128(inputs["L0_ffn_SU_ug"])
    w["sdp"] = _col128(inputs["L0_ffn_SU_dp"])
    wug = np.asarray(inputs["L0_ffn_Wscale_ug"], dtype=np.float32)
    w["wsu"] = _col128(wug[:DFF])
    w["wsg"] = _col128(wug[DFF:])
    w["wsd"] = _col128(inputs["L0_ffn_Wscale_dp"])
    D = np.asarray(inputs["L0_attn_D"], dtype=np.float32)
    w["D64"] = _col128(np.repeat(D, HD))
    xf = np.asarray(inputs["L0_attn_hidden_states"], dtype=np.float32)
    xs = []
    for core in range(8):
        b, q = core // 4, core % 4
        lo = TOK * q
        if q == 0:
            sl = np.concatenate([np.zeros((3, DM), np.float32), xf[b, 0:TOK + 0]], axis=0)
        else:
            sl = xf[b, lo - 3:lo + TOK]
        xs.append(np.ascontiguousarray(sl))
    return w, xs


EMITTERS = {"p0": emit_p0, "p1": emit_p1, "p2": emit_p2, "p3": emit_p3, "p4": emit_p4, "p5": emit_p5}


def build_program(phases):
    pg = Pg()
    nc = pg.nc
    produced = {BOUND[b][2]: None for b in BOUND}
    inside = set(phases)
    if any("x" in CONSUMES[p] for p in phases):
        pg.io["x"] = nc.dram_tensor("x", [TOKH, DM], F32, kind="ExternalInput").ap()
    if "p1" in phases:
        pg.io["gmask"] = nc.dram_tensor("gmask", [64, 8], F32, kind="ExternalInput").ap()
    if "p5" in phases:
        pg.io["out"] = nc.dram_tensor("out", [TOK, DM], F32, kind="ExternalOutput").ap()
    for name, (shape, dt, producer) in BOUND.items():
        used = any(name in CONSUMES[p] for p in phases)
        made = producer in inside
        if not (used or made):
            continue
        if made and used:
            kind = "Internal"
        elif made:
            kind = "ExternalOutput"
        else:
            kind = "ExternalInput"
        pg.io[name] = nc.dram_tensor(name, shape, dt, kind=kind)
    for p in phases:
        for name, shape in WEIGHTS[p].items():
            if name not in pg.io:
                pg.io[name] = nc.dram_tensor(name, shape, F32, kind="ExternalInput").ap()
    for p in phases:
        pg.begin_phase()
        EMITTERS[p](pg)
        pg.end_phase()
    return pg


def _run(pg, in_map, n_cores=8):
    res = run_bass_kernel_spmd(pg.nc, in_map, core_ids=list(range(n_cores)))
    return res


def kernel(**inputs):
    w, xs = host_layout(inputs)
    if FUSED:
        groups = [PHASE_ORDER]
    else:
        groups = [[p] for p in PHASE_ORDER]

    state = [dict() for _ in range(8)]
    out = None
    for phases in groups:
        pg = build_program(phases)
        in_maps = []
        for core in range(8):
            m = {}
            if any("x" in CONSUMES[p] for p in phases):
                m["x"] = xs[core]
            if "p1" in phases:
                q = core % 4
                g = np.zeros((64, 8), np.float32)
                for r in range(4):
                    g[:, r] = 1.0 if r < q else 0.0
                    g[:, 4 + r] = 0.0 if r < q else 1.0
                m["gmask"] = g
            for p in phases:
                for name in WEIGHTS[p]:
                    m[name] = w[name]
            for name in BOUND:
                if name in pg.io and any(name in CONSUMES[p] for p in phases) \
                        and BOUND[name][2] not in phases:
                    m[name] = state[core][name]
            in_maps.append(m)
        res = _run(pg, in_maps)
        for core in range(8):
            r = res.results[core]
            for name in BOUND:
                if name in r:
                    state[core][name] = r[name]
            if "out" in r:
                if out is None:
                    out = np.zeros((2, 4096, DM), np.float32)
                b, q = core // 4, core % 4
                out[b, TOK * q:TOK * (q + 1)] = r["out"]
        pg.stack.close()
    return out


# version 05 (final)
# speedup vs baseline: 1.4524x; 1.3543x over previous
import contextlib
import numpy as np
import concourse.bass as bass
import concourse.mybir as mybir
from concourse.bass_utils import run_bass_kernel_spmd

F32 = mybir.dt.float32
BF16 = mybir.dt.bfloat16
AF = mybir.ActivationFunctionType
ALU = mybir.AluOpType
AX = mybir.AxisListType

DM = 2048
DIP = 8224
CONVD = 6144
NH = 32
DS = 64
HD = 64
DFF = 8192
EPS = 1e-5
TOK = 1024
TOKH = TOK + 3
NCH = 8
NDC = DM // 128
NEG = -1e30

FUSED = True
DBG_P1 = False

BOUND = {
    "xn_sp":  ([NDC, 128, TOKH], BF16, "p0"),
    "u0_sp":  ([NDC, 128, TOKH], BF16, "p0"),
    "h1_sp":  ([NDC, 128, TOK], BF16, "p1"),
    "s_sp":   ([NDC, 128, TOK], BF16, "p2"),
    "x1_sp":  ([TOK, DM], F32, "p3"),
    "xn1_sp": ([NDC, 128, TOK], BF16, "p4"),
    "s2_sp":  ([NDC, 128, TOK], BF16, "p4"),
}

CONSUMES = {
    "p0": ["x"],
    "p1": ["xn_sp", "u0_sp"],
    "p2": ["h1_sp"],
    "p3": ["xn_sp", "s_sp", "x"],
    "p4": ["x1_sp"],
    "p5": ["xn1_sp", "x1_sp", "s2_sp"],
}

WEIGHTS = {
    "p0": {},
    "p1": {"ipT": [DM, DIP], "opT": [DM, DM], "cw128": [128, 48, 4],
           "cb128": [128, 48], "zb128": [128, NDC], "n1w": [128, NDC],
           "D64": [128, NDC]},
    "p2": {"gwT": [DM, DFF], "uwT": [DM, DFF], "dwT": [DFF, DM],
           "n2w": [128, NDC]},
    "p3": {"fupT": [DM, DFF], "fgT": [DM, DFF], "fdT": [DFF, DM],
           "sug": [128, NDC], "sdp": [128, 64], "wsu": [128, 64],
           "wsg": [128, 64], "wsd": [128, NDC]},
    "p4": {},
    "p5": {"mgT": [16, DM, 1024], "muT": [16, DM, 1024], "mdT": [16, 1024, DM],
           "sgT": [DM, 1024], "suT": [DM, 1024], "sdT": [1024, DM],
           "gw1T": [DM, 16], "eb16": [16, 1]},
}

PHASE_ORDER = ["p0", "p1", "p2", "p3", "p4", "p5"]


class Pg:

    def __init__(self, detect_races=False):
        self.nc = bass.Bass("TRN2", target_bir_lowering=False,
                            detect_race_conditions=detect_races)
        self.stack = contextlib.ExitStack()
        self._const_sb = {}
        self._sem_n = 0
        self._phase_stack = None
        self._phase_sems = []
        self.bar_sem = self.nc.alloc_semaphore("barrier_sem")
        self.bar_count = 0
        self.io = {}
        self.const_loads = []

    def begin_phase(self):
        self._phase_stack = contextlib.ExitStack()
        self._phase_sems = []
        self._const_sb = {}

    def end_phase(self):
        self.barrier()
        if self._phase_sems:
            with self.nc.Block() as blk:
                sems = list(self._phase_sems)

                @blk.gpsimd
                def _(g):
                    pass
            self.nc.clear_and_free_semaphores(sems)
            self.barrier()
        self._phase_sems = []
        if self._phase_stack is not None:
            self._phase_stack.close()
            self._phase_stack = None

    def sem(self, name):
        self._sem_n += 1
        h = self.nc.alloc_semaphore(f"{name}_{self._sem_n}")
        self._phase_sems.append(h)
        return h

    def sbuf(self, name, shape, dtype):
        st = self._phase_stack if self._phase_stack is not None else self.stack
        return st.enter_context(self.nc.sbuf_tensor(name, shape, dtype))

    def psum(self, name, shape, dtype=F32):
        st = self._phase_stack if self._phase_stack is not None else self.stack
        return st.enter_context(self.nc.psum_tensor(name, shape, dtype))

    _const_seq = 0

    def const(self, name, arr, dtype):
        if name in self._const_sb:
            return self._const_sb[name]
        arr = np.ascontiguousarray(arr)
        Pg._const_seq += 1
        uname = f"{name}_{Pg._const_seq}"
        dr = self.nc.inline_tensor(arr, uname)
        sb = self.sbuf("c_" + uname, list(arr.shape), dtype)
        self.const_loads.append((sb, dr))
        self._const_sb[name] = sb
        return sb

    def flush_consts(self):
        if not self.const_loads:
            return
        nc = self.nc
        loads = self.const_loads
        self.const_loads = []
        csem = self.sem("csem")
        n = len(loads)
        with nc.Block() as blk:
            @blk.sync
            def _(s):
                for sb, dr in loads:
                    s.dma_start(out=sb[...], in_=dr.ap()).then_inc(csem, 16)
                s.wait_ge(csem, 16 * n)
        self.barrier()

    def barrier(self):
        nc = self.nc
        self.bar_count += 5
        target = self.bar_count
        s = self.bar_sem
        with nc.Block() as blk:
            for en in ("sync", "gpsimd", "tensor", "vector", "scalar"):
                def mk(e, tgt=target):
                    e.sem_inc(s, 1)
                    e.wait_ge(s, tgt)
                getattr(blk, en)(mk)


def _bf16(a):
    import ml_dtypes
    return np.asarray(a, dtype=ml_dtypes.bfloat16)


class Chain:

    def __init__(self, sem):
        self.sem = sem
        self.n = 0

    def step(self, eng, inst):
        self.n += 1
        eng.drain(fusable=True).then_inc(self.sem, 1)
        return inst

    def sync(self, eng):
        eng.wait_ge(self.sem, self.n)


def emit_p0(pg):
    nc = pg.nc
    x = pg.io["x"]
    xn_sp = pg.io["xn_sp"]
    u0_sp = pg.io["u0_sp"]

    ident = pg.const("identbf", _bf16(np.eye(128, dtype=np.float32)), BF16)
    pg.flush_consts()

    chunks = [(0, 3)] + [(3 + 128 * c, 128) for c in range(NCH)]
    NCHK = len(chunks)

    x_tm = pg.sbuf("p0_x", [128, 2, DM], F32)
    xn_tm = pg.sbuf("p0_xn", [128, DM], BF16)
    u_tm = pg.sbuf("p0_u", [128, DM], BF16)
    junk = pg.sbuf("p0_junk", [128, DM], F32)
    st_xn = pg.sbuf("p0_sxn", [128, 2, NDC, 128], BF16)
    st_u = pg.sbuf("p0_su", [128, 2, NDC, 128], BF16)
    stats = pg.sbuf("p0_stats", [128, 8], F32)
    ps_tr = [pg.psum(f"p0_ps{i}", [128, 128], BF16) for i in range(4)]

    sx = [pg.sem("p0_sx0"), pg.sem("p0_sx1")]
    sv = pg.sem("p0_sv")
    sp = pg.sem("p0_sp")
    se = pg.sem("p0_se")
    ss = [pg.sem("p0_ss0"), pg.sem("p0_ss1")]
    cs = Chain(pg.sem("p0_cs"))
    cv = Chain(pg.sem("p0_cv"))

    with nc.Block() as blk:
        @blk.sync
        def _(s):
            for i, (off, w) in enumerate(chunks):
                if i >= 2:
                    s.wait_ge(sv, 2 * (i - 2) + 1)
                    s.wait_ge(cs.sem, 6 * (i - 2) + 1)
                s.dma_start(out=x_tm[0:w, i % 2, :], in_=x[off:off + w, :]
                            ).then_inc(sx[i % 2], 16)

        @blk.gpsimd
        def _(g):
            for i, (off, w) in enumerate(chunks):
                g.wait_ge(se, 32 * (i + 1))
                g.dma_start(out=xn_sp.ap()[:, :, off:off + w].rearrange("c p t -> p c t"),
                            in_=st_xn[:, i % 2, :, 0:w]).then_inc(ss[i % 2], 16)
                g.dma_start(out=u0_sp.ap()[:, :, off:off + w].rearrange("c p t -> p c t"),
                            in_=st_u[:, i % 2, :, 0:w]).then_inc(ss[i % 2], 16)
            g.wait_ge(ss[0], 32 * ((NCHK + 1) // 2))
            g.wait_ge(ss[1], 32 * (NCHK // 2))

        @blk.scalar
        def _(sc):
            for i, (off, w) in enumerate(chunks):
                xi = x_tm[0:w, i % 2, :]
                sc.wait_ge(sx[i % 2], 16 * (i // 2 + 1))
                if i >= 1:
                    sc.wait_ge(cv.sem, 8 * (i - 1) + 3)
                cs.step(sc, sc.activation(junk[0:w, :], xi, AF.Square,
                                      accum_out=stats[0:w, 0:1]))
                sc.wait_ge(cv.sem, 8 * i + 6)
                cs.step(sc, sc.activation(stats[0:w, 4:5], stats[0:w, 4:5], AF.Ln))
                cs.sync(sc)
                cs.step(sc, sc.activation(stats[0:w, 5:6], stats[0:w, 4:5], AF.Exp,
                                      scale=-0.5))
                sc.wait_ge(sv, 2 * i + 1)
                cs.step(sc, sc.activation(junk[0:w, :], xn_tm[0:w, :], AF.Square,
                                      accum_out=stats[0:w, 6:7]))
                sc.wait_ge(cv.sem, 8 * i + 8)
                cs.step(sc, sc.activation(stats[0:w, 6:7], stats[0:w, 6:7], AF.Ln))
                cs.sync(sc)
                cs.step(sc, sc.activation(stats[0:w, 7:8], stats[0:w, 6:7], AF.Exp,
                                      scale=-0.5))

        @blk.vector
        def _(v):
            for i, (off, w) in enumerate(chunks):
                xi = x_tm[0:w, i % 2, :]
                v.wait_ge(sx[i % 2], 16 * (i // 2 + 1))
                cv.step(v, v.reduce_sum(stats[0:w, 1:2], xi, axis=AX.X))
                v.wait_ge(cs.sem, 6 * i + 1)
                cv.sync(v)
                cv.step(v, v.tensor_scalar(stats[0:w, 2:3], stats[0:w, 1:2], 1.0 / DM, None, ALU.mult))
                cv.step(v, v.tensor_scalar(stats[0:w, 3:4], stats[0:w, 0:1], 1.0 / DM, None, ALU.mult))
                cv.sync(v)
                cv.step(v, v.tensor_tensor(stats[0:w, 4:5], stats[0:w, 2:3], stats[0:w, 2:3], ALU.mult))
                cv.sync(v)
                cv.step(v, v.tensor_tensor(stats[0:w, 4:5], stats[0:w, 3:4], stats[0:w, 4:5], ALU.subtract))
                cv.sync(v)
                cv.step(v, v.tensor_scalar(stats[0:w, 4:5], stats[0:w, 4:5], EPS, None,
                                        ALU.add))
                v.wait_ge(cs.sem, 6 * i + 3)
                if i >= 1:
                    v.wait_ge(sp, 32 * (i - 1) + 16)
                v.tensor_scalar(xn_tm[0:w, :], xi, stats[0:w, 2:3], stats[0:w, 5:6],
                                ALU.subtract, ALU.mult)
                v.drain(fusable=True).then_inc(sv, 1)
                v.wait_ge(cs.sem, 6 * i + 4)
                cv.step(v, v.tensor_scalar(stats[0:w, 6:7], stats[0:w, 6:7], 1.0 / DM, None, ALU.mult))
                cv.sync(v)
                cv.step(v, v.tensor_scalar(stats[0:w, 6:7], stats[0:w, 6:7], EPS, None,
                                        ALU.add))
                v.wait_ge(cs.sem, 6 * i + 6)
                if i >= 1:
                    v.wait_ge(sp, 32 * i)
                v.tensor_scalar(u_tm[0:w, :], xn_tm[0:w, :], stats[0:w, 7:8], None,
                                ALU.mult)
                v.drain(fusable=True).then_inc(sv, 1)
                if i >= 2:
                    v.wait_ge(ss[i % 2], 32 * (i // 2))
                for t in range(2 * NDC):
                    g = 32 * i + t + 1
                    v.wait_ge(sp, g)
                    src = ps_tr[t % 4]
                    dst = st_xn if t < NDC else st_u
                    dc = t % NDC
                    v.tensor_copy(dst[:, i % 2, dc, 0:w], src[:, 0:w])
                    v.drain(fusable=True).then_inc(se, 1)

        @blk.tensor
        def _(t):
            for i, (off, w) in enumerate(chunks):
                t.wait_ge(sv, 2 * i + 1)
                for dc in range(NDC):
                    g = 32 * i + dc + 1
                    if g > 4:
                        t.wait_ge(se, g - 4)
                    t.transpose(ps_tr[(g - 1) % 4][:, 0:w],
                                xn_tm[0:w, 128 * dc:128 * dc + 128],
                                ident[0:w, 0:w])
                    t.drain().then_inc(sp, 1)
                t.wait_ge(sv, 2 * i + 2)
                for dc in range(NDC):
                    g = 32 * i + NDC + dc + 1
                    t.wait_ge(se, g - 4)
                    t.transpose(ps_tr[(g - 1) % 4][:, 0:w],
                                u_tm[0:w, 128 * dc:128 * dc + 128],
                                ident[0:w, 0:w])
                    t.drain().then_inc(sp, 1)

    pg.barrier()


def emit_p1(pg):
    nc = pg.nc
    xn_sp, u0_sp, h1_sp = pg.io["xn_sp"], pg.io["u0_sp"], pg.io["h1_sp"]
    ipT, opT = pg.io["ipT"], pg.io["opT"]
    cwv, cbv, zbv, n1v, D64v = (pg.io[k] for k in
                                ("cw128", "cb128", "zb128", "n1w", "D64"))

    identbf = pg.const("identbf", _bf16(np.eye(128, dtype=np.float32)), BF16)
    identf = pg.const("identf", np.eye(128, dtype=np.float32), F32)
    sel = np.zeros((32, 32, 128), np.float32)
    for h in range(32):
        sel[h, h, :] = 1.0
    self_sel = pg.const("sel32f", sel, F32)
    maskbT = pg.const("maskbT",
                      np.where(np.arange(128)[:, None] <= np.arange(128)[None, :],
                               0.0, NEG).astype(np.float32), F32)
    diag32 = pg.const("diag32", np.eye(32, dtype=np.float32), F32)
    ones32x128 = pg.const("o32x128", np.ones((32, 128), np.float32), F32)
    ones32x64 = pg.const("o32x64", np.ones((32, 64), np.float32), F32)
    rep = np.zeros((32, 2048), np.float32)
    for h in range(32):
        rep[h, 64 * h:64 * (h + 1)] = 1.0
    repmask = pg.const("repmask", rep, F32)

    n1_sb = pg.sbuf("p1_n1", [128, NDC], F32)
    zb_sb = pg.sbuf("p1_zb", [128, NDC], F32)
    cb_sb = pg.sbuf("p1_cb", [128, 48], F32)
    cw_sb = pg.sbuf("p1_cw", [128, 48, 4], F32)
    dd_sb = pg.sbuf("p1_dd", [128, NDC], F32)

    u_q = pg.sbuf("p1_uq", [128, NDC, 259], BF16)
    wslab = pg.sbuf("p1_ws", [128, 2, NDC, 128], BF16)
    zst = pg.sbuf("p1_zst", [128, 2, 256], BF16)
    dt_q = pg.sbuf("p1_dt", [32, 256], F32)
    xbc = pg.sbuf("p1_xbc", [128, 48, 256], BF16)
    cacc = pg.sbuf("p1_cacc", [128, 256], F32)
    y_sb = pg.sbuf("p1_y", [128, NDC, TOK], BF16)
    xT = pg.sbuf("p1_xT", [128, NDC, 128], BF16)
    BT = pg.sbuf("p1_BT", [128, NDC, 128], BF16)
    tA = pg.sbuf("p1_tA", [32, 128], F32)
    tL = pg.sbuf("p1_tL", [32, 128], F32)
    tR = pg.sbuf("p1_tR", [32, 128], F32)
    alog = pg.sbuf("p1_alog", [32, 128], F32)
    cum = pg.sbuf("p1_cum", [32, 128], F32)
    cumT = pg.sbuf("p1_cumT", [128, 32], F32)
    cumlast = pg.sbuf("p1_cl", [32, 1], F32)
    runsum = pg.sbuf("p1_run", [32, 2], F32)
    pref = pg.sbuf("p1_pref", [32, NCH], F32)
    rhsd = pg.sbuf("p1_rhsd", [32, 32], F32)
    dte = pg.sbuf("p1_dte", [128, 32], F32)
    acolb = pg.sbuf("p1_acolb", [64, 32], F32)
    ecT_all = pg.sbuf("p1_ecT", [128, NCH, 32], F32)
    tS = pg.sbuf("p1_tS", [128, 2, 128], F32)
    tE2 = pg.sbuf("p1_tE2", [128, 2, 128], F32)
    St = pg.sbuf("p1_St", [128, 2, 128], BF16)
    Bd = pg.sbuf("p1_Bd", [128, 2, 64], BF16)
    H = pg.sbuf("p1_H", [64, NH * DS], F32)
    hl_st = pg.sbuf("p1_hlst", [64, 2, NH * DS], BF16)
    hl_ld = pg.sbuf("p1_hlld", [64, 2, NH * DS], BF16)
    Hin = pg.sbuf("p1_Hin", [64, NH * DS], F32)
    Hg = pg.sbuf("p1_Hg", [64, NH * DS], F32)
    Hg16 = pg.sbuf("p1_Hg16", [64, NH * DS], BF16)
    pack = pg.sbuf("p1_pack", [64, NH * DS], F32)
    rem_s = pg.sbuf("p1_rems", [64, NH * DS], F32)
    rem_a = pg.sbuf("p1_rema", [64, NH * DS], F32)
    lA = pg.sbuf("p1_lA", [32, 64], F32)
    ecr = pg.sbuf("p1_ecr", [32, 128], F32)
    prefx = pg.sbuf("p1_prefx", [32, NCH], F32)
    gm_sb = pg.sbuf("p1_gm", [64, 8], F32)
    EC = pg.sbuf("p1_EC", [64, 2, 128], BF16)
    Cch = pg.sbuf("p1_Cch", [128, 2, NDC, 128], BF16)
    zs_t = pg.sbuf("p1_zs", [128, 2, TOK], BF16)
    xn_m = pg.sbuf("p1_xnm", [128, 2, TOK], BF16)
    h1_st = pg.sbuf("p1_h1st", [128, 2, TOK], BF16)

    ps = [pg.psum(f"p1_ps{i}", [128, 512], F32) for i in range(6)]
    psT = [pg.psum(f"p1_psT{i}", [128, 128], BF16) for i in range(2)]

    C_sp = nc.dram_tensor("p1_C_sp", [NDC, 128, TOK], BF16, kind="Internal")
    zs_sp = nc.dram_tensor("p1_zs_sp", [NDC, 128, TOK], BF16, kind="Internal")
    hl_sp = nc.dram_tensor("p1_hl_sp", [NCH, 64, NH * DS], BF16, kind="Internal")
    cc_in = nc.dram_tensor("p1_ccin", [128, NH * DS], F32, kind="Internal")
    cc_out = nc.dram_tensor("p1_ccout", [512, NH * DS], F32, kind="Internal")

    if DBG_P1:
        dbg = {n: nc.dram_tensor("dbg_" + n, sh, dt2, kind="ExternalOutput")
               for n, sh, dt2 in [
                   ("xbc", [128, 48, 256], BF16), ("dt", [32, 256], F32),
                   ("cum", [32, 128], F32), ("cumT", [128, 32], F32),
                   ("dte", [128, 32], F32), ("acolb", [64, 32], F32),
                   ("y1", [128, NDC, TOK], BF16), ("H", [64, 2048], F32),
                   ("hin", [64, 2048], F32), ("y2", [128, NDC, TOK], BF16),
                   ("zs", [NDC, 128, TOK], BF16), ("ecr", [32, 128], F32),
                   ("hg", [64, 2048], F32)]}
    s_dbg = pg.sem("p1_dbg")
    s_cst = pg.sem("p1_cst")
    s_wl = [pg.sem("p1_wl0"), pg.sem("p1_wl1")]
    s_uq = pg.sem("p1_uq")
    s_un = pg.sem("p1_un")
    s_m1 = pg.sem("p1_m1")
    s_ez = pg.sem("p1_ez")
    s_ec = pg.sem("p1_ec")
    s_spl = pg.sem("p1_spl")
    s_cum = pg.sem("p1_cum")
    s_rd = pg.sem("p1_rd")
    s_pbc = pg.sem("p1_pbc")
    s_ptc = pg.sem("p1_ptc")
    s_ctv = pg.sem("p1_ctv")
    s_dte = pg.sem("p1_dte")
    s_acb = pg.sem("p1_acb")
    s_vtr = pg.sem("p1_vtr")
    s_ptr = pg.sem("p1_ptr")
    s_pcb = pg.sem("p1_pcb")
    s_px0 = pg.sem("p1_px0")
    s_py = pg.sem("p1_py")
    s_psm = pg.sem("p1_psm")
    s_dl = pg.sem("p1_dl")
    s_xe = pg.sem("p1_xe")
    s_st = pg.sem("p1_st")
    s_bd = pg.sem("p1_bd")
    s_hu = pg.sem("p1_hu")
    s_ya = pg.sem("p1_ya")
    s_hlc = pg.sem("p1_hlc")
    s_scd = pg.sem("p1_scd")
    s_csp = pg.sem("p1_csp")
    s_cc = pg.sem("p1_cc")
    s_pk = pg.sem("p1_pk")
    s_pkd = pg.sem("p1_pkd")
    s_rm = [pg.sem("p1_rm0"), pg.sem("p1_rm1"), pg.sem("p1_rm2")]
    s_hbc = pg.sem("p1_hbc")
    s_hg = pg.sem("p1_hg")
    s_hgc = pg.sem("p1_hgc")
    s_ptc2 = pg.sem("p1_ptc2")
    s_ecr = pg.sem("p1_ecr")
    s_pec = pg.sem("p1_pec")
    s_ecm = pg.sem("p1_ecm")
    s_py2 = pg.sem("p1_py2")
    s_ya2 = pg.sem("p1_ya2")
    s_cld = [pg.sem("p1_cld0"), pg.sem("p1_cld1")]
    s_hinz = pg.sem("p1_hinz")
    s_gz = pg.sem("p1_gz")
    s_gy = pg.sem("p1_gy")
    s_m2 = pg.sem("p1_m2")
    s_xnl = [pg.sem("p1_xnl0"), pg.sem("p1_xnl1")]
    s_h1e = pg.sem("p1_h1e")
    s_h1s = [pg.sem("p1_h1s0"), pg.sem("p1_h1s1")]
    s_zs = [pg.sem("p1_zs0"), pg.sem("p1_zs1")]
    s_zl = [pg.sem("p1_zl0"), pg.sem("p1_zl1")]
    s_hlw = pg.sem("p1_hlw")
    s_hls = [pg.sem("p1_hls0"), pg.sem("p1_hls1")]
    s_hld = [pg.sem("p1_hld0"), pg.sem("p1_hld1")]
    s_pkh = pg.sem("p1_pkh")
    s_dtp = pg.sem("p1_dtp")
    s_ect = pg.sem("p1_ect")
    s_aex = pg.sem("p1_aex")
    s_pfx = pg.sem("p1_pfx")
    s_la = pg.sem("p1_la")
    s_la2 = pg.sem("p1_la2")
    s_gm = pg.sem("p1_gm")
    cve = Chain(pg.sem("p1_cve"))
    cse = Chain(pg.sem("p1_cse"))

    pg.flush_consts()

    QW = 259
    NM = 65

    def mwidth(m):
        return 32 if m == 64 else 128

    with nc.Block() as blk:
        @blk.sync
        def _(s):
            for name, sb in (("n1w", n1_sb), ("zb128", zb_sb), ("cb128", cb_sb),
                             ("D64", dd_sb)):
                s.dma_start(out=sb[...], in_=pg.io[name]).then_inc(s_cst, 16)
            s.dma_start(out=cw_sb[...], in_=pg.io["cw128"]).then_inc(s_cst, 16)
            s.dma_start(out=gm_sb[...], in_=pg.io["gmask"]).then_inc(s_gm, 16)
            for Q in range(4):
                if Q >= 1:
                    s.wait_ge(s_m1, NM * Q)
                s.dma_start(out=u_q[:, :, :],
                            in_=u0_sp.ap()[:, :, 256 * Q:256 * Q + QW]
                            .rearrange("c p t -> p c t")).then_inc(s_uq, 16)
            for lc in range(NCH):
                s.wait_ge(s_cc, 1)
                if lc >= 2:
                    s.wait_ge(s_ya2, 16 * (lc - 1))
                    s.wait_ge(s_hg, lc - 1)
                s.dma_start(out=Cch[:, lc % 2, :, :],
                            in_=C_sp.ap()[:, :, 128 * lc:128 * (lc + 1)]
                            .rearrange("c p t -> p c t")).then_inc(s_cld[lc % 2], 16)
                s.dma_start(out=hl_ld[:, lc % 2, :],
                            in_=hl_sp.ap()[lc, :, :]).then_inc(s_hld[lc % 2], 16)
            for dc in range(NDC):
                s.wait_ge(s_ya2, 16 * NCH)
                if dc >= 2:
                    s.wait_ge(s_gy, dc - 1)
                s.dma_start(out=zs_t[:, dc % 2, :],
                            in_=zs_sp.ap()[dc, :, :]).then_inc(s_zl[dc % 2], 16)
            for m in range(NDC + 2):
                if m >= 2:
                    s.wait_ge(s_h1e, m - 1)
                if m < NDC:
                    s.dma_start(out=xn_m[:, m % 2, :],
                                in_=xn_sp.ap()[m, :, 3:TOKH]).then_inc(s_xnl[m % 2], 16)
                if m >= 2:
                    s.dma_start(out=h1_sp.ap()[m - 2, :, :],
                                in_=h1_st[:, m % 2, :]).then_inc(s_h1s[m % 2], 16)
            s.wait_ge(s_h1s[0], 16 * 8)
            s.wait_ge(s_h1s[1], 16 * 8)

        @blk.gpsimd
        def _(g):
            for Q in range(4):
                for m in range(NM):
                    gidx = NM * Q + m
                    if gidx >= 2:
                        g.wait_ge(s_m1, gidx - 1)
                    mw = mwidth(m)
                    g.dma_start(out=wslab[:, gidx % 2, :, 0:mw],
                                in_=ipT[:, 128 * m:128 * m + mw]
                                .rearrange("(c p) m -> p c m", p=128)
                                ).then_inc(s_wl[gidx % 2], 16)
                g.wait_ge(s_ec, 48 * (Q + 1))
                g.dma_start(out=C_sp.ap()[:, :, 256 * Q:256 * (Q + 1)]
                            .rearrange("c p t -> p c t"),
                            in_=xbc[:, 32:48, :]).then_inc(s_csp, 16)
                for cq in range(2):
                    lc = 2 * Q + cq
                    g.wait_ge(s_hlw, lc + 1)
                    g.dma_start(out=hl_sp.ap()[lc, :, :],
                                in_=hl_st[:, lc % 2, :]).then_inc(s_hls[lc % 2], 16)
            g.wait_ge(s_scd, NCH)
            g.dma_start(out=cc_in.ap()[0:64, :], in_=H[:, :]).then_inc(s_pkh, 16)
            g.wait_ge(s_pk, 1)
            g.dma_start(out=cc_in.ap()[64:128, :], in_=pack[:, :]).then_inc(s_pkd, 16)
            g.wait_ge(s_pkd, 16)
            g.wait_ge(s_pkh, 16)
            g.collective_compute(
                "AllGather", ALU.bypass,
                replica_groups=[[0, 1, 2, 3], [4, 5, 6, 7]],
                ins=[cc_in.ap().opt()], outs=[cc_out.ap().opt()],
            ).then_inc(s_cc, 1)
            g.wait_ge(s_cc, 1)
            for r in range(4):
                if r >= 1:
                    g.wait_ge(s_hinz, r)
                g.dma_start(out=rem_s[:, :], in_=cc_out.ap()[128 * r:128 * r + 64, :]
                            ).then_inc(s_rm[0], 16)
                g.dma_start(out=rem_a[:, :],
                            in_=cc_out.ap()[128 * r + 64:128 * r + 128, :]
                            ).then_inc(s_rm[1], 16)
            if DBG_P1:
                g.wait_ge(s_ya, 16 * NCH)
                g.wait_ge(s_scd, NCH)
                g.dma_start(out=dbg["y1"].ap(), in_=y_sb[...]).then_inc(s_dbg, 16)
                g.dma_start(out=dbg["xbc"].ap(), in_=xbc[...]).then_inc(s_dbg, 16)
                g.dma_start(out=dbg["dt"].ap(), in_=dt_q[...]).then_inc(s_dbg, 16)
                g.dma_start(out=dbg["cum"].ap(), in_=cum[...]).then_inc(s_dbg, 16)
                g.dma_start(out=dbg["cumT"].ap(), in_=cumT[...]).then_inc(s_dbg, 16)
                g.dma_start(out=dbg["dte"].ap(), in_=dte[...]).then_inc(s_dbg, 16)
                g.dma_start(out=dbg["acolb"].ap(), in_=acolb[...]).then_inc(s_dbg, 16)
                g.dma_start(out=dbg["H"].ap(), in_=H[...]).then_inc(s_dbg, 16)
                g.wait_ge(s_hinz, 4)
                g.dma_start(out=dbg["hin"].ap(), in_=Hin[...]).then_inc(s_dbg, 16)
                g.wait_ge(s_ya2, 16 * NCH)
                g.wait_ge(s_hg, NCH)
                g.dma_start(out=dbg["y2"].ap(), in_=y_sb[...]).then_inc(s_dbg, 16)
                g.dma_start(out=dbg["zs"].ap(), in_=zs_sp.ap()).then_inc(s_dbg, 16)
                g.dma_start(out=dbg["ecr"].ap(), in_=ecr[...]).then_inc(s_dbg, 16)
                g.dma_start(out=dbg["hg"].ap(), in_=Hg[...]).then_inc(s_dbg, 16)
                g.wait_ge(s_dbg, 16 * 13)
            for m in range(NDC):
                if m < 2:
                    g.wait_ge(s_m1, 4 * NM - 2 + m + 1)
                else:
                    g.wait_ge(s_m2, m - 1)
                g.dma_start(out=wslab[:, m % 2, :, :],
                            in_=opT[:, 128 * m:128 * (m + 1)]
                            .rearrange("(c p) m -> p c m", p=128)
                            ).then_inc(s_wl[m % 2], 16)


        @blk.tensor
        def _(t):
            for Q in range(4):
                t.wait_ge(s_un, 16 * (Q + 1))
                for m in range(NM):
                    gidx = NM * Q + m
                    mw = mwidth(m)
                    t.wait_ge(s_wl[gidx % 2], 16 * (gidx // 2 + 1))
                    if Q >= 1 and m < 4:
                        t.wait_ge(s_scd, 2 * Q)
                        t.wait_ge(s_ez, 17 * Q)
                        t.wait_ge(s_ec, 48 * Q)
                    elif m >= 4:
                        pm = m - 4
                        if pm <= 15:
                            t.wait_ge(s_ez, 17 * Q + pm + 1)
                        elif pm == 64:
                            pass
                        else:
                            t.wait_ge(s_ec, 48 * Q + pm - 15)
                    bank = ps[m % 4]
                    for k in range(NDC):
                        t.matmul(bank[0:mw, 0:QW], wslab[:, gidx % 2, k, 0:mw],
                                 u_q[:, k, :], start=(k == 0),
                                 stop=(k == NDC - 1))
                        if k == NDC - 1:
                            t.drain().then_inc(s_m1, 1)
                for cq in range(2):
                    lc = 2 * Q + cq
                    cqs = slice(128 * cq, 128 * cq + 128)
                    t.wait_ge(s_cum, lc + 1)
                    if lc >= 1:
                        t.wait_ge(s_ctv, lc)
                        t.wait_ge(s_ya, 16 * lc)
                        t.wait_ge(s_dte, lc)
                        t.wait_ge(s_acb, lc)
                    t.transpose(ps[4][:, 384:416], cum[0:32, :],
                                identf[0:32, 0:32])
                    t.drain().then_inc(s_ptc, 1)
                    t.wait_ge(s_rd, lc + 1)
                    t.matmul(ps[5][:, 384:416], ones32x128[:, :], rhsd[:, :],
                             start=True, stop=True)
                    t.drain().then_inc(s_pbc, 1)
                    t.wait_ge(s_ec, 48 * (Q + 1))
                    for tt in range(32):
                        g2 = 32 * lc + tt + 1
                        if g2 > 2:
                            t.wait_ge(s_vtr, g2 - 2)
                        t.transpose(psT[tt % 2][:, :], xbc[:, tt, cqs],
                                    identbf[:, :])
                        t.drain().then_inc(s_ptr, 1)
                    def cb_mm(h):
                        if h >= 2:
                            t.wait_ge(s_st, 32 * lc + h - 1)
                        elif lc >= 1:
                            t.wait_ge(s_st, 32 * lc)
                        t.matmul(ps[h % 2][:, 0:128],
                                 xbc[64 * (h % 2):64 * (h % 2) + 64, 16 + h // 2, cqs],
                                 xbc[64 * (h % 2):64 * (h % 2) + 64, 32 + h // 2, cqs],
                                 start=True, stop=True)
                        t.drain().then_inc(s_pcb, 1)

                    def x0_mm(h):
                        if h >= 1:
                            t.wait_ge(s_dl, 32 * lc + h)
                        elif lc >= 1:
                            t.wait_ge(s_dl, 32 * lc)
                        t.matmul(ps[2][:, 0:128], self_sel[:, h, :], cum[:, :],
                                 start=True, stop=True)
                        t.drain().then_inc(s_px0, 1)

                    cb_mm(0)
                    x0_mm(0)
                    cb_mm(1)
                    for h in range(32):
                        dc = h // 2
                        t.wait_ge(s_st, 32 * lc + h + 1)
                        t.wait_ge(s_vtr, 32 * lc + dc + 1)
                        if h % 2 == 0 and 16 * lc + dc - 1 > 0:
                            t.wait_ge(s_ya, 16 * lc + dc - 1)
                        t.matmul(ps[4 + dc % 2][64 * (h % 2):64 * (h % 2) + 64, 0:128],
                                 xT[:, dc, 64 * (h % 2):64 * (h % 2) + 64],
                                 St[:, h % 2, :], start=True, stop=True)
                        t.drain().then_inc(s_py, 1)
                        t.wait_ge(s_bd, 32 * lc + h + 1)
                        if h >= 1:
                            t.wait_ge(s_hu, 32 * lc + h)
                        elif lc >= 1:
                            t.wait_ge(s_hu, 32 * lc)
                        t.matmul(ps[3][0:64, 0:64], Bd[:, h % 2, :],
                                 xT[:, dc, 64 * (h % 2):64 * (h % 2) + 64],
                                 start=True, stop=True)
                        t.drain().then_inc(s_psm, 1)
                        if h + 2 < 32:
                            cb_mm(h + 2)
                        if h + 1 < 32:
                            x0_mm(h + 1)
            t.wait_ge(s_la, 1)
            t.wait_ge(s_scd, NCH)
            for j in range(4):
                t.matmul(ps[j][0:64, 0:512], lA[:, :], repmask[:, 512 * j:512 * (j + 1)],
                         start=True, stop=True)
                t.drain().then_inc(s_hbc, 1)
            for lc in range(NCH):
                t.wait_ge(s_la2, lc + 1)
                if lc == 0:
                    t.wait_ge(s_pk, 1)
                else:
                    t.wait_ge(s_hg, lc)
                for j in range(4):
                    t.matmul(ps[j][0:64, 0:512], lA[:, :],
                             repmask[:, 512 * j:512 * (j + 1)],
                             start=True, stop=True)
                    t.drain().then_inc(s_hbc, 1)
                t.wait_ge(s_ect, lc + 1)
                if lc >= 1:
                    t.wait_ge(s_ecr, lc)
                    t.wait_ge(s_ya2, 16 * lc)
                t.transpose(ps[4][0:32, 256:384], ecT_all[:, lc, :],
                            identf[:, :])
                t.drain().then_inc(s_ptc2, 1)
                for h in range(32):
                    dc = h // 2
                    if h >= 2:
                        t.wait_ge(s_ecm, 32 * lc + h - 1)
                    else:
                        t.wait_ge(s_hg, lc + 1)
                        t.wait_ge(s_ecr, lc + 1)
                    t.matmul(ps[h % 2][0:64, 0:128], self_sel[:, h, 0:64],
                             ecr[:, :], start=True, stop=True)
                    t.drain().then_inc(s_pec, 1)
                    t.wait_ge(s_ecm, 32 * lc + h + 1)
                    t.wait_ge(s_hgc, lc + 1)
                    if h % 2 == 0 and 16 * lc + dc - 1 > 0:
                        t.wait_ge(s_ya2, 16 * lc + dc - 1)
                    t.matmul(ps[4 + dc % 2][64 * (h % 2):64 * (h % 2) + 64, 0:128],
                             Hg16[:, 64 * h:64 * h + 64], EC[:, h % 2, :],
                             start=True, stop=True)
                    t.drain().then_inc(s_py2, 1)
            t.wait_ge(s_gy, 16)
            t.wait_ge(s_hg, NCH)
            t.wait_ge(s_ecm, 32 * NCH)
            for m in range(NDC):
                gidx = 4 * NM + m
                t.wait_ge(s_wl[gidx % 2], 16 * (gidx // 2 + 1))
                if m >= 2:
                    t.wait_ge(s_h1e, m - 1)
                b0, b1 = ps[2 * (m % 2)], ps[2 * (m % 2) + 1]
                for k in range(NDC):
                    t.matmul(b0[:, 0:512], wslab[:, gidx % 2, k, :],
                             y_sb[:, k, 0:512], start=(k == 0), stop=(k == NDC - 1))
                    t.matmul(b1[:, 0:512], wslab[:, gidx % 2, k, :],
                             y_sb[:, k, 512:1024], start=(k == 0), stop=(k == NDC - 1))
                    if k == NDC - 1:
                        t.drain().then_inc(s_m2, 1)

        @blk.vector
        def _(v):
            v.wait_ge(s_cst, 16 * 5)
            v.memset(H[:, :], 0.0)
            v.memset(Hin[:, :], 0.0)
            v.memset(runsum[:, 0:1], 0.0)
            for Q in range(4):
                v.wait_ge(s_uq, 16 * (Q + 1))
                for dc in range(NDC):
                    v.tensor_scalar(u_q[:, dc, :], u_q[:, dc, :], n1_sb[:, dc:dc + 1],
                                    None, ALU.mult)
                    v.drain(fusable=True).then_inc(s_un, 1)
                if Q >= 1:
                    v.wait_ge(s_csp, 16 * Q)
                for m in range(16, 64):
                    dc = m - 16
                    v.wait_ge(s_m1, NM * Q + m + 1)
                    bank = ps[m % 4]
                    v.tensor_scalar(cacc[:, :], bank[:, 0:256], cw_sb[:, dc, 0:1],
                                    cb_sb[:, dc:dc + 1], ALU.mult, ALU.add)
                    for k2 in (1, 2):
                        v.scalar_tensor_tensor(cacc[:, :], bank[:, k2:k2 + 256],
                                               cw_sb[:, dc, k2:k2 + 1], cacc[:, :],
                                               ALU.mult, ALU.add)
                    v.scalar_tensor_tensor(xbc[:, dc, :], bank[:, 3:259],
                                           cw_sb[:, dc, 3:4], cacc[:, :],
                                           ALU.mult, ALU.add)
                    v.drain(fusable=True).then_inc(s_ec, 1)
                for cq in range(2):
                    lc = 2 * Q + cq
                    cqs = slice(128 * cq, 128 * cq + 128)
                    qcols = slice(256 * Q + 128 * cq, 256 * Q + 128 * cq + 128)
                    v.wait_ge(s_ez, 17 * (Q + 1))
                    v.tensor_scalar(tR[:, :], dt_q[0:32, cqs], 0.0, None, ALU.max)
                    v.wait_ge(s_spl, lc + 1)
                    v.scalar_tensor_tensor(alog[:, :], tR[:, :], -1.0, tL[:, :],
                                           ALU.mult, ALU.subtract)
                    if lc >= 1:
                        v.wait_ge(s_px0, 32 * lc)
                    v.tensor_tensor_scan(cum[:, :], alog[:, :], alog[:, :], 0.0,
                                         ALU.add, ALU.bypass)
                    v.drain(fusable=True).then_inc(s_cum, 1)
                    cve.step(v, v.tensor_copy(cumlast[:, :], cum[:, 127:128]))
                    cve.sync(v)
                    cve.step(v, v.tensor_copy(pref[:, lc:lc + 1], runsum[:, 0:1]))
                    cve.sync(v)
                    cve.step(v, v.tensor_tensor(runsum[:, 0:1], runsum[:, 0:1],
                                             cumlast[:, :], ALU.add))
                    cve.sync(v)
                    if lc >= 1:
                        v.wait_ge(s_pbc, lc)
                    v.tensor_scalar(rhsd[:, :], diag32[:, :], cumlast[:, :],
                                    None, ALU.mult)
                    v.drain(fusable=True).then_inc(s_rd, 1)
                    v.wait_ge(s_ptc, lc + 1)
                    cve.step(v, v.tensor_copy(cumT[:, :], ps[4][:, 384:416]))
                    cve.sync(v)
                    v.sem_inc(s_ctv, 1)
                    v.wait_ge(s_pbc, lc + 1)
                    v.tensor_tensor(dte[:, :], ps[5][:, 384:416], cumT[:, :],
                                    ALU.subtract).then_inc(s_dtp, 1)
                    if lc >= 2:
                        v.wait_ge(s_hls[lc % 2], 16 * (lc // 2))
                    v.tensor_copy(hl_st[:, lc % 2, :], H[:, :])
                    v.drain(fusable=True).then_inc(s_hlw, 1)
                    for tt in range(32):
                        v.wait_ge(s_ptr, 32 * lc + tt + 1)
                        dst = xT if tt < 16 else BT
                        v.tensor_copy(dst[:, tt % 16, :], psT[tt % 2][:, :])
                        v.drain(fusable=True).then_inc(s_vtr, 1)
                    v.wait_ge(s_dte, lc + 1)
                    v.wait_ge(s_acb, lc + 1)
                    for h in range(32):
                        dc = h // 2
                        v.wait_ge(s_px0, 32 * lc + h + 1)
                        if 32 * lc + h - 1 > 0:
                            v.wait_ge(s_xe, 32 * lc + h - 1)
                        v.tensor_scalar(tS[:, h % 2, :], ps[2][:, 0:128],
                                        cumT[:, h:h + 1], None, ALU.subtract)
                        v.tensor_tensor(tS[:, h % 2, :], tS[:, h % 2, :],
                                        maskbT[:, :], ALU.add).then_inc(s_dl, 1)
                        if 32 * lc + h - 1 > 0:
                            v.wait_ge(s_psm, 32 * lc + h - 1)
                        v.tensor_scalar(Bd[:, h % 2, :],
                                        BT[:, dc, 64 * (h % 2):64 * (h % 2) + 64],
                                        dte[:, h:h + 1], None, ALU.mult)
                        v.drain(fusable=True).then_inc(s_bd, 1)
                        v.wait_ge(s_xe, 32 * lc + h + 1)
                        if 32 * lc + h - 1 > 0:
                            v.wait_ge(s_py, 32 * lc + h - 1)
                        v.tensor_tensor(St[:, h % 2, :], tE2[:, h % 2, :],
                                        ps[h % 2][:, 0:128], ALU.mult)
                        v.drain(fusable=True).then_inc(s_st, 1)
                        v.wait_ge(s_psm, 32 * lc + h + 1)
                        v.scalar_tensor_tensor(H[0:64, 64 * h:64 * h + 64],
                                               H[0:64, 64 * h:64 * h + 64],
                                               acolb[:, h:h + 1],
                                               ps[3][0:64, 0:64],
                                               ALU.mult, ALU.add).then_inc(s_hu, 1)
                        if h % 2 == 1:
                            v.wait_ge(s_py, 32 * lc + h + 1)
                            v.scalar_tensor_tensor(y_sb[:, dc, qcols],
                                                   xbc[:, dc, cqs],
                                                   dd_sb[:, dc:dc + 1],
                                                   ps[4 + dc % 2][:, 0:128],
                                                   ALU.mult, ALU.add
                                                   ).then_inc(s_ya, 1)
                    v.sem_inc(s_scd, 1)
            v.wait_ge(s_aex, 1)
            v.tensor_scalar(lA[:, :], ones32x64[:, :], runsum[:, 1:2], None,
                            ALU.mult)
            v.drain(fusable=True).then_inc(s_la, 1)
            v.wait_ge(s_hbc, 4)
            for j in range(4):
                v.tensor_copy(pack[:, 512 * j:512 * (j + 1)],
                              ps[j][0:64, 0:512])
                if j == 3:
                    v.drain(fusable=True).then_inc(s_pk, 1)
            v.wait_ge(s_gm, 16)
            for r in range(4):
                v.wait_ge(s_rm[0], 16 * (r + 1))
                v.wait_ge(s_rm[1], 16 * (r + 1))
                v.tensor_scalar(Hg[:, :], rem_a[:, :],
                                gm_sb[:, r:r + 1], gm_sb[:, 4 + r:5 + r],
                                ALU.mult, ALU.add)
                v.tensor_tensor(Hin[:, :], Hin[:, :], Hg[:, :], ALU.mult)
                v.tensor_scalar(Hg[:, :], rem_s[:, :],
                                gm_sb[:, r:r + 1], None, ALU.mult)
                v.tensor_tensor(Hin[:, :], Hin[:, :], Hg[:, :], ALU.add
                                ).then_inc(s_hinz, 1)
            if DBG_P1:
                v.wait_ge(s_dbg, 16 * 8)
            for lc in range(NCH):
                v.wait_ge(s_pfx, lc + 1)
                if lc >= 1:
                    v.wait_ge(s_hbc, 4 + 4 * lc)
                v.tensor_scalar(lA[:, :], ones32x64[:, :], prefx[:, lc:lc + 1],
                                None, ALU.mult)
                v.drain(fusable=True).then_inc(s_la2, 1)
                v.wait_ge(s_hbc, 4 + 4 * (lc + 1))
                for j in range(4):
                    v.tensor_tensor(Hg[:, 512 * j:512 * (j + 1)],
                                    Hin[:, 512 * j:512 * (j + 1)],
                                    ps[j][0:64, 0:512], ALU.mult)
                v.wait_ge(s_hld[lc % 2], 16 * (lc // 2 + 1))
                v.tensor_tensor(Hg[:, :], Hg[:, :], hl_ld[:, lc % 2, :], ALU.add
                                ).then_inc(s_hg, 1)
                v.tensor_copy(Hg16[:, :], Hg[:, :])
                v.drain(fusable=True).then_inc(s_hgc, 1)
                v.wait_ge(s_ptc2, lc + 1)
                v.tensor_copy(ecr[:, :], ps[4][0:32, 256:384])
                v.drain(fusable=True).then_inc(s_ecr, 1)
                v.wait_ge(s_cld[lc % 2], 16 * (lc // 2 + 1))
                for h in range(32):
                    dc = h // 2
                    v.wait_ge(s_pec, 32 * lc + h + 1)
                    if 32 * lc + h - 1 > 0:
                        v.wait_ge(s_py2, 32 * lc + h - 1)
                    v.tensor_tensor(EC[:, h % 2, :],
                                    Cch[64 * (h % 2):64 * (h % 2) + 64, lc % 2,
                                        dc, :],
                                    ps[h % 2][0:64, 0:128], ALU.mult)
                    v.drain(fusable=True).then_inc(s_ecm, 1)
                    if h % 2 == 1:
                        v.wait_ge(s_py2, 32 * lc + h + 1)
                        lcol = slice(128 * lc, 128 * lc + 128)
                        v.tensor_tensor(y_sb[:, dc, lcol], y_sb[:, dc, lcol],
                                        ps[4 + dc % 2][:, 0:128], ALU.add
                                        ).then_inc(s_ya2, 1)
            if DBG_P1:
                v.wait_ge(s_dbg, 16 * 8)
            for dc in range(NDC):
                v.wait_ge(s_zl[dc % 2], 16 * (dc // 2 + 1))
                v.tensor_tensor(y_sb[:, dc, :], y_sb[:, dc, :], zs_t[:, dc % 2, :],
                                ALU.mult)
                v.drain(fusable=True).then_inc(s_gy, 1)
            for m in range(NDC):
                v.wait_ge(s_m2, m + 1)
                v.wait_ge(s_xnl[m % 2], 16 * (m // 2 + 1))
                if m >= 2:
                    v.wait_ge(s_h1s[m % 2], 16 * (m // 2))
                b0, b1 = ps[2 * (m % 2)], ps[2 * (m % 2) + 1]
                v.tensor_tensor(h1_st[:, m % 2, 0:512], b0[:, 0:512],
                                xn_m[:, m % 2, 0:512], ALU.add)
                v.tensor_tensor(h1_st[:, m % 2, 512:1024], b1[:, 0:512],
                                xn_m[:, m % 2, 512:1024], ALU.add)
                v.drain(fusable=True).then_inc(s_h1e, 1)

        @blk.scalar
        def _(sc):
            for Q in range(4):
                for m in range(16):
                    sc.wait_ge(s_m1, NM * Q + m + 1)
                    gi = 16 * Q + m
                    if gi >= 2:
                        sc.wait_ge(s_zs[gi % 2], 16 * (gi // 2))
                    sc.activation(zst[:, gi % 2, :], ps[m % 4][:, 3:259], AF.Silu,
                                  bias=zb_sb[:, m:m + 1])
                    sc.drain(fusable=True).then_inc(s_ez, 1)
                    sc.dma_start(out=zs_sp.ap()[m, :, 256 * Q:256 * (Q + 1)],
                                 in_=zst[:, gi % 2, :]).then_inc(s_zs[gi % 2], 16)
                sc.wait_ge(s_m1, NM * (Q + 1))
                if Q >= 1:
                    sc.wait_ge(s_cum, 2 * Q)
                sc.copy(dt_q[0:32, :], ps[0][0:32, 3:259])
                sc.drain(fusable=True).then_inc(s_ez, 1)
                for cq in range(2):
                    lc = 2 * Q + cq
                    sc.wait_ge(s_ez, 17 * (Q + 1))
                    if lc >= 1:
                        sc.wait_ge(s_cum, lc)
                    cse.step(sc, sc.activation(tA[:, :], dt_q[0:32, 128 * cq:128 * cq + 128],
                                           AF.Abs))
                    cse.sync(sc)
                    cse.step(sc, sc.activation(tA[:, :], tA[:, :], AF.Exp, scale=-1.0))
                    cse.sync(sc)
                    sc.activation(tL[:, :], tA[:, :], AF.Ln, bias=1.0
                                  ).then_inc(s_spl, 1)
                    sc.wait_ge(s_dtp, lc + 1)
                    sc.activation(dte[:, :], dte[:, :], AF.Exp).then_inc(s_dte, 1)
                    sc.wait_ge(s_pbc, lc + 1)
                    sc.activation(acolb[:, :], ps[5][0:64, 384:416], AF.Exp
                                  ).then_inc(s_acb, 1)
                    sc.wait_ge(s_ctv, lc + 1)
                    sc.activation(ecT_all[:, lc, :], cumT[:, :], AF.Exp)
                    sc.drain(fusable=True).then_inc(s_ect, 1)
                    for h in range(32):
                        sc.wait_ge(s_dl, 32 * lc + h + 1)
                        if 32 * lc + h - 1 > 0:
                            sc.wait_ge(s_st, 32 * lc + h - 1)
                        sc.activation(tE2[:, h % 2, :], tS[:, h % 2, :], AF.Exp
                                      ).then_inc(s_xe, 1)
            sc.wait_ge(s_scd, NCH)
            sc.activation(runsum[:, 1:2], runsum[:, 0:1], AF.Exp).then_inc(s_aex, 1)
            for lc in range(NCH):
                sc.activation(prefx[:, lc:lc + 1], pref[:, lc:lc + 1], AF.Exp
                              ).then_inc(s_pfx, 1)

    pg.barrier()


def emit_p2(pg):
    nc = pg.nc
    h1_sp, s_sp = pg.io["h1_sp"], pg.io["s_sp"]
    gwT, uwT, dwT = pg.io["gwT"], pg.io["uwT"], pg.io["dwT"]

    onescol = pg.const("ones128c", _bf16(np.ones((128, 1), np.float32)), BF16)
    onesrow = pg.const("ones1x128f", np.ones((1, 128), np.float32), F32)
    pg.flush_consts()

    NM2 = 64
    n2_sb = pg.sbuf("p2_n2", [128, NDC], F32)
    h1t = pg.sbuf("p2_h1t", [128, 2, TOK], BF16)
    rstd = pg.sbuf("p2_rstd", [1, TOK], F32)
    v_f = pg.sbuf("p2_v", [128, NDC, TOK], BF16)
    a_f = pg.sbuf("p2_a", [128, NM2, TOK], BF16)
    gs = pg.sbuf("p2_gs", [128, 1, TOK], BF16)
    wsl = pg.sbuf("p2_wsl", [128, 4, NDC, 128], BF16)
    wsl2 = pg.sbuf("p2_wsl2", [128, 4, NDC, 128], BF16)
    sst = pg.sbuf("p2_sst", [128, 2, TOK], BF16)

    pga = pg.psum("p2_pga", [128, TOK], F32)
    pua = pg.psum("p2_pua", [128, TOK], F32)
    pdn = pg.psum("p2_pdn", [128, TOK], F32)
    psB = pg.psum("p2_psB", [128, TOK], F32)

    s_pr = pg.sem("p2_pr")
    s_h1a = [pg.sem("p2_h1a0"), pg.sem("p2_h1a1")]
    s_h1b = [pg.sem("p2_h1b0"), pg.sem("p2_h1b1")]
    s_h1c = [pg.sem("p2_h1c0"), pg.sem("p2_h1c1")]
    s_sq = pg.sem("p2_sq")
    s_pq = pg.sem("p2_pq")
    s_eps = pg.sem("p2_eps")
    s_sr = pg.sem("p2_sr")
    s_bc = pg.sem("p2_bc")
    s_vh = pg.sem("p2_vh")
    s_wl = [pg.sem("p2_wl0"), pg.sem("p2_wl1")]
    s_w2 = [pg.sem("p2_w20"), pg.sem("p2_w21")]
    s_mm = pg.sem("p2_mm")
    s_md = pg.sem("p2_md")
    s_sil = pg.sem("p2_sil")
    s_am = pg.sem("p2_am")
    s_se = pg.sem("p2_se")
    s_ss = [pg.sem("p2_ss0"), pg.sem("p2_ss1")]

    with nc.Block() as blk:
        @blk.sync
        def _(s):
            s.dma_start(out=n2_sb[...], in_=pg.io["n2w"]).then_inc(s_pr, 16)
            for m in range(NDC):
                if m >= 2:
                    s.wait_ge(s_sq, m - 1)
                s.dma_start(out=h1t[:, m % 2, :], in_=h1_sp.ap()[m, :, :]
                            ).then_inc(s_h1a[m % 2], 16)
            for m in range(NDC):
                if m >= 2:
                    s.wait_ge(s_vh, m - 1)
                else:
                    s.wait_ge(s_sq, NDC - 1 + m)
                s.dma_start(out=h1t[:, m % 2, :], in_=h1_sp.ap()[m, :, :]
                            ).then_inc(s_h1b[m % 2], 16)
            for m2 in range(NDC + 2):
                if m2 < NDC:
                    if m2 >= 2:
                        s.wait_ge(s_se, m2 - 1)
                    else:
                        s.wait_ge(s_vh, NDC - 1 + m2)
                    s.dma_start(out=h1t[:, m2 % 2, :], in_=h1_sp.ap()[m2, :, :]
                                ).then_inc(s_h1c[m2 % 2], 16)
                if m2 >= 2:
                    s.wait_ge(s_se, m2 - 1)
                    s.dma_start(out=s_sp.ap()[m2 - 2, :, :],
                                in_=sst[:, m2 % 2, :]).then_inc(s_ss[m2 % 2], 16)
            s.wait_ge(s_ss[0], 16 * 8)
            s.wait_ge(s_ss[1], 16 * 8)

        @blk.gpsimd
        def _(g):
            for m in range(NM2):
                for j, wt in ((0, gwT), (1, uwT)):
                    si = 2 * m + j
                    if si >= 4:
                        g.wait_ge(s_mm, si - 3)
                    g.dma_start(out=wsl[:, si % 4, :, :],
                                in_=wt[:, 128 * m:128 * (m + 1)]
                                .rearrange("(c p) m -> p c m", p=128)
                                ).then_inc(s_wl[si % 2], 16)
            for m2 in range(NDC):
                for kq in range(4):
                    si2 = 4 * m2 + kq
                    if si2 >= 4:
                        g.wait_ge(s_md, si2 - 3)
                    g.dma_start(out=wsl2[:, si2 % 4, :, :],
                                in_=dwT[2048 * kq:2048 * (kq + 1),
                                        128 * m2:128 * (m2 + 1)]
                                .rearrange("(c p) m -> p c m", p=128)
                                ).then_inc(s_w2[si2 % 2], 16)

        @blk.scalar
        def _(sc):
            for m in range(NDC):
                sc.wait_ge(s_h1a[m % 2], 16 * (m // 2 + 1))
                if m >= 2:
                    sc.wait_ge(s_pq, m - 1)
                sc.activation(v_f[:, m, :], h1t[:, m % 2, :], AF.Square)
                sc.drain(fusable=True).then_inc(s_sq, 1)
            sc.wait_ge(s_eps, 2)
            sc.activation(rstd[0:1, 0:512], psB[0:1, 0:512], AF.Ln)
            sc.activation(rstd[0:1, 512:1024], psB[0:1, 512:1024], AF.Ln)
            sc.activation(rstd[0:1, 0:1024], rstd[0:1, 0:1024], AF.Exp,
                          scale=-0.5)
            sc.drain(fusable=True).then_inc(s_sr, 1)
            for m in range(NM2):
                sc.wait_ge(s_mm, 2 * m + 1)
                if m >= 1:
                    sc.wait_ge(s_am, m)
                sc.activation(gs[:, 0, :], pga[:, :], AF.Silu)
                sc.drain(fusable=True).then_inc(s_sil, 1)

        @blk.vector
        def _(v):
            v.wait_ge(s_pr, 16)
            v.wait_ge(s_pq, NDC)
            v.tensor_scalar(psB[0:1, 0:512], psB[0:1, 0:512],
                            1.0 / DM, EPS, ALU.mult, ALU.add)
            v.drain(fusable=True).then_inc(s_eps, 1)
            v.tensor_scalar(psB[0:1, 512:1024], psB[0:1, 512:1024],
                            1.0 / DM, EPS, ALU.mult, ALU.add)
            v.drain(fusable=True).then_inc(s_eps, 1)
            for m in range(NDC):
                v.wait_ge(s_bc, 1)
                v.wait_ge(s_h1b[m % 2], 16 * (m // 2 + 1))
                v.scalar_tensor_tensor(v_f[:, m, :], h1t[:, m % 2, :],
                                       n2_sb[:, m:m + 1], psB[:, :],
                                       ALU.mult, ALU.mult)
                v.drain(fusable=True).then_inc(s_vh, 1)
            for m in range(NM2):
                v.wait_ge(s_sil, m + 1)
                v.wait_ge(s_mm, 2 * m + 2)
                v.tensor_tensor(a_f[:, m, :], gs[:, 0, :], pua[:, :], ALU.mult)
                v.drain(fusable=True).then_inc(s_am, 1)
            for m2 in range(NDC):
                v.wait_ge(s_md, 4 * m2 + 4)
                v.wait_ge(s_h1c[m2 % 2], 16 * (m2 // 2 + 1))
                if m2 >= 2:
                    v.wait_ge(s_ss[m2 % 2], 16 * (m2 // 2))
                v.tensor_tensor(sst[:, m2 % 2, :], pdn[:, :], h1t[:, m2 % 2, :],
                                ALU.add)
                v.drain(fusable=True).then_inc(s_se, 1)

        @blk.tensor
        def _(t):
            for m in range(NDC):
                t.wait_ge(s_sq, m + 1)
                t.matmul(psB[0:1, 0:512], onescol[:, :], v_f[:, m, 0:512],
                         start=(m == 0), stop=(m == NDC - 1))
                t.matmul(psB[0:1, 512:1024], onescol[:, :], v_f[:, m, 512:1024],
                         start=(m == 0), stop=(m == NDC - 1))
                t.drain().then_inc(s_pq, 1)
            t.wait_ge(s_sr, 1)
            t.matmul(psB[:, 0:512], onesrow[:, :], rstd[0:1, 0:512],
                     start=True, stop=True)
            t.matmul(psB[:, 512:1024], onesrow[:, :], rstd[0:1, 512:1024],
                     start=True, stop=True)
            t.drain().then_inc(s_bc, 1)
            t.wait_ge(s_vh, NDC)
            for m in range(NM2):
                for j in range(2):
                    si = 2 * m + j
                    t.wait_ge(s_wl[si % 2], 16 * (si // 2 + 1))
                    bank = pga if j == 0 else pua
                    gid = m
                    if gid >= 1:
                        if j == 0:
                            t.wait_ge(s_sil, gid)
                        else:
                            t.wait_ge(s_am, gid)
                    elif j == 0:
                        t.wait_ge(s_vh, NDC)
                    for k in range(NDC):
                        for nb in range(2):
                            t.matmul(bank[:, 512 * nb:512 * (nb + 1)],
                                     wsl[:, si % 4, k, :],
                                     v_f[:, k, 512 * nb:512 * (nb + 1)],
                                     start=(k == 0), stop=(k == NDC - 1))
                    t.drain().then_inc(s_mm, 1)
            t.wait_ge(s_am, NM2)
            for m2 in range(NDC):
                if m2 >= 1:
                    t.wait_ge(s_se, m2)
                for kq in range(4):
                    si2 = 4 * m2 + kq
                    t.wait_ge(s_w2[si2 % 2], 16 * (si2 // 2 + 1))
                    for k in range(NDC):
                        for nb in range(2):
                            t.matmul(pdn[:, 512 * nb:512 * (nb + 1)],
                                     wsl2[:, si2 % 4, k, :],
                                     a_f[:, 16 * kq + k, 512 * nb:512 * (nb + 1)],
                                     start=(kq == 0 and k == 0),
                                     stop=(kq == 3 and k == NDC - 1))
                    t.drain().then_inc(s_md, 1)

    pg.barrier()


def emit_p3(pg):
    nc = pg.nc
    xn_sp, s_sp, x1_sp = pg.io["xn_sp"], pg.io["s_sp"], pg.io["x1_sp"]
    x_in = pg.io["x"]
    fupT, fgT, fdT = pg.io["fupT"], pg.io["fgT"], pg.io["fdT"]

    def hadamard(n):
        h = np.array([[1.0]], np.float32)
        while h.shape[0] < n:
            h = np.block([[h, h], [h, -h]])
        return h

    H128 = pg.const("h128bf", _bf16(hadamard(128)), BF16)
    identbf = pg.const("identbf", _bf16(np.eye(128, dtype=np.float32)), BF16)
    pg.flush_consts()

    NM2 = 64
    RS2 = float(1.0 / np.sqrt(2048.0))
    RS8 = float(1.0 / np.sqrt(8192.0))

    sug_sb = pg.sbuf("p3_sug", [128, NDC], F32)
    wprod = pg.sbuf("p3_wprod", [128, 64], F32)
    wsd_sb = pg.sbuf("p3_wsd", [128, NDC], F32)
    sdp_sb = pg.sbuf("p3_sdp", [128, 64], F32)
    wsu_sb = pg.sbuf("p3_wsu", [128, 64], F32)
    wsg_sb = pg.sbuf("p3_wsg", [128, 64], F32)
    xf = pg.sbuf("p3_xf", [128, NDC, TOK], BF16)
    a_f = pg.sbuf("p3_a", [128, NM2, TOK], BF16)
    ss = pg.sbuf("p3_ss", [128, 2, TOK], BF16)
    gs = pg.sbuf("p3_gs", [128, 2, TOK], BF16)
    gsd = pg.sbuf("p3_gsd", [128, 2, TOK], BF16)
    wsl = pg.sbuf("p3_wsl", [128, 2, NDC, 128], BF16)
    wsl2 = pg.sbuf("p3_wsl2", [128, 2, NDC, 128], BF16)
    x_ld = pg.sbuf("p3_xld", [128, 2, DM], F32)

    pga = pg.psum("p3_pga", [128, TOK], F32)
    pua = pg.psum("p3_pua", [128, TOK], F32)
    pdn = pg.psum("p3_pdn", [128, TOK], F32)
    psT = [pg.psum(f"p3_psT{i}", [128, 128], BF16) for i in range(2)]

    s_pr = pg.sem("p3_pr")
    s_ld = pg.sem("p3_ld")
    s_ssl = [pg.sem("p3_ssl0"), pg.sem("p3_ssl1")]
    s_bf = pg.sem("p3_bf")
    s_fwm = pg.sem("p3_fwm")
    s_fa = pg.sem("p3_fa")
    s_wv = pg.sem("p3_wv")
    s_wl = [pg.sem("p3_wl0"), pg.sem("p3_wl1")]
    s_w2 = [pg.sem("p3_w20"), pg.sem("p3_w21")]
    s_mm = pg.sem("p3_mm")
    s_md = pg.sem("p3_md")
    s_sil = pg.sem("p3_sil")
    s_am = pg.sem("p3_am")
    s_bf8 = pg.sem("p3_bf8")
    s_fw8 = pg.sem("p3_fw8")
    s_fa8 = pg.sem("p3_fa8")
    s_dsl = pg.sem("p3_dsl")
    s_sg = pg.sem("p3_sg")
    s_ptr = pg.sem("p3_ptr")
    s_cmb = pg.sem("p3_cmb")
    s_xl = [pg.sem("p3_xl0"), pg.sem("p3_xl1")]
    s_xs = [pg.sem("p3_xs0"), pg.sem("p3_xs1")]

    with nc.Block() as blk:
        @blk.sync
        def _(s):
            for name, sb in (("sug", sug_sb), ("wsd", wsd_sb), ("sdp", sdp_sb),
                             ("wsu", wsu_sb), ("wsg", wsg_sb)):
                s.dma_start(out=sb[...], in_=pg.io[name]).then_inc(s_pr, 16)
            s.dma_start(out=xf[...],
                        in_=xn_sp.ap()[:, :, 3:TOKH].rearrange("m p t -> p m t")
                        ).then_inc(s_ld, 16)
            for tc in range(2):
                s.dma_start(out=x_ld[:, tc % 2, :],
                            in_=x_in[3 + 128 * tc:3 + 128 * (tc + 1), :]
                            ).then_inc(s_xl[tc % 2], 16)
            for m2 in range(NDC):
                if m2 >= 2:
                    s.wait_ge(s_sg, m2 - 1)
                s.dma_start(out=ss[:, m2 % 2, :], in_=s_sp.ap()[m2, :, :]
                            ).then_inc(s_ssl[m2 % 2], 16)
            for tc in range(2, NCH + 2):
                s.wait_ge(s_cmb, 16 * (tc - 1))
                s.dma_start(out=x1_sp.ap()[128 * (tc - 2):128 * (tc - 1), :],
                            in_=x_ld[:, tc % 2, :]).then_inc(s_xs[tc % 2], 16)
                if tc < NCH:
                    s.wait_ge(s_xs[tc % 2], 16 * (tc // 2))
                    s.dma_start(out=x_ld[:, tc % 2, :],
                                in_=x_in[3 + 128 * tc:3 + 128 * (tc + 1), :]
                                ).then_inc(s_xl[tc % 2], 16)
            s.wait_ge(s_xs[0], 16 * 4)
            s.wait_ge(s_xs[1], 16 * 4)

        @blk.gpsimd
        def _(g):
            for m in range(NM2):
                for j, wt in ((0, fgT), (1, fupT)):
                    si = 2 * m + j
                    if si >= 2:
                        g.wait_ge(s_mm, si - 1)
                    g.dma_start(out=wsl[:, si % 2, :, :],
                                in_=wt[:, 128 * m:128 * (m + 1)]
                                .rearrange("(c p) m -> p c m", p=128)
                                ).then_inc(s_wl[si % 2], 16)
            for m2 in range(NDC):
                for kq in range(4):
                    si2 = 4 * m2 + kq
                    if si2 >= 2:
                        g.wait_ge(s_md, si2 - 1)
                    g.dma_start(out=wsl2[:, si2 % 2, :, :],
                                in_=fdT[2048 * kq:2048 * (kq + 1),
                                        128 * m2:128 * (m2 + 1)]
                                .rearrange("(c p) m -> p c m", p=128)
                                ).then_inc(s_w2[si2 % 2], 16)

        @blk.scalar
        def _(sc):
            for m in range(NM2):
                sc.wait_ge(s_mm, 2 * m + 1)
                if m >= 2:
                    sc.wait_ge(s_am, m - 1)
                sc.activation(gs[:, m % 2, :], pga[:, :], AF.Silu,
                              scale=wsg_sb[:, m:m + 1])
                sc.drain(fusable=True).then_inc(s_sil, 1)
            for m2 in range(NDC):
                sc.wait_ge(s_md, 4 * m2 + 4)
                if m2 >= 2:
                    sc.wait_ge(s_sg, m2 - 1)
                sc.activation(gsd[:, m2 % 2, :], pdn[:, :], AF.Silu,
                              scale=wsd_sb[:, m2:m2 + 1])
                sc.drain(fusable=True).then_inc(s_dsl, 1)

        @blk.vector
        def _(v):
            v.wait_ge(s_pr, 16 * 5)
            v.wait_ge(s_ld, 16)
            v.tensor_tensor(wprod[:, :], wsu_sb[:, :], sdp_sb[:, :], ALU.mult)
            v.drain(fusable=True).then_inc(s_wv, 1)
            for m in range(NDC):
                v.tensor_scalar(xf[:, m, :], xf[:, m, :], sug_sb[:, m:m + 1],
                                None, ALU.mult)
            st = 8
            while st >= 1:
                for blk0 in range(0, NDC, 2 * st):
                    for c in range(blk0, blk0 + st):
                        a, b = xf[:, c, :], xf[:, c + st, :]
                        v.tensor_tensor(gs[:, 0, :], a, b, ALU.subtract)
                        v.tensor_tensor(a, a, b, ALU.add)
                        v.tensor_copy(b, gs[:, 0, :])
                st //= 2
            v.drain(fusable=True).then_inc(s_bf, 1)
            for c in range(NDC):
                v.wait_ge(s_fwm, c + 1)
                v.tensor_scalar(xf[:, c, :], (pga if c % 2 == 0 else pua)[:, :],
                                RS2, None, ALU.mult)
                v.drain(fusable=True).then_inc(s_fa, 1)
            for m in range(NM2):
                v.wait_ge(s_sil, m + 1)
                v.wait_ge(s_mm, 2 * m + 2)
                v.scalar_tensor_tensor(a_f[:, m, :], pua[:, :],
                                       wprod[:, m:m + 1], gs[:, m % 2, :],
                                       ALU.mult, ALU.mult)
                v.drain(fusable=True).then_inc(s_am, 1)
            st = 32
            while st >= 1:
                for blk0 in range(0, NM2, 2 * st):
                    for c in range(blk0, blk0 + st):
                        a, b = a_f[:, c, :], a_f[:, c + st, :]
                        v.tensor_tensor(gs[:, 0, :], a, b, ALU.subtract)
                        v.tensor_tensor(a, a, b, ALU.add)
                        v.tensor_copy(b, gs[:, 0, :])
                st //= 2
            v.drain(fusable=True).then_inc(s_bf8, 1)
            for c in range(NM2):
                v.wait_ge(s_fw8, c + 1)
                v.tensor_scalar(a_f[:, c, :], (pga if c % 2 == 0 else pua)[:, :],
                                RS8, None, ALU.mult)
                v.drain(fusable=True).then_inc(s_fa8, 1)
            for m2 in range(NDC):
                v.wait_ge(s_dsl, m2 + 1)
                v.wait_ge(s_ssl[m2 % 2], 16 * (m2 // 2 + 1))
                v.tensor_tensor(xf[:, m2, :], ss[:, m2 % 2, :],
                                gsd[:, m2 % 2, :], ALU.mult)
                v.drain(fusable=True).then_inc(s_sg, 1)
            for tc in range(NCH):
                for dc in range(NDC):
                    gtr = 16 * tc + dc
                    v.wait_ge(s_ptr, gtr + 1)
                    v.wait_ge(s_xl[tc % 2], 16 * (tc // 2 + 1))
                    if tc >= 2 and dc == 0:
                        v.wait_ge(s_xs[tc % 2], 16 * (tc // 2))
                    v.tensor_tensor(x_ld[:, tc % 2, 128 * dc:128 * (dc + 1)],
                                    psT[gtr % 2][:, :],
                                    x_ld[:, tc % 2, 128 * dc:128 * (dc + 1)],
                                    ALU.add)
                    v.drain(fusable=True).then_inc(s_cmb, 1)

        @blk.tensor
        def _(t):
            t.wait_ge(s_bf, 1)
            for c in range(NDC):
                if c >= 2:
                    t.wait_ge(s_fa, c - 1)
                bank = pga if c % 2 == 0 else pua
                t.matmul(bank[:, 0:512], H128[:, :], xf[:, c, 0:512],
                         start=True, stop=True)
                t.matmul(bank[:, 512:1024], H128[:, :], xf[:, c, 512:1024],
                         start=True, stop=True)
                t.drain().then_inc(s_fwm, 1)
            t.wait_ge(s_fa, NDC)
            t.wait_ge(s_wv, 1)
            for m in range(NM2):
                for j in range(2):
                    si = 2 * m + j
                    t.wait_ge(s_wl[si % 2], 16 * (si // 2 + 1))
                    bank = pga if j == 0 else pua
                    if m >= 1:
                        if j == 0:
                            t.wait_ge(s_sil, m)
                        else:
                            t.wait_ge(s_am, m)
                    for k in range(NDC):
                        for nb in range(2):
                            t.matmul(bank[:, 512 * nb:512 * (nb + 1)],
                                     wsl[:, si % 2, k, :],
                                     xf[:, k, 512 * nb:512 * (nb + 1)],
                                     start=(k == 0), stop=(k == NDC - 1))
                    t.drain().then_inc(s_mm, 1)
            t.wait_ge(s_bf8, 1)
            for c in range(NM2):
                if c == 0:
                    t.wait_ge(s_sil, NM2)
                elif c == 1:
                    t.wait_ge(s_am, NM2)
                else:
                    t.wait_ge(s_fa8, c - 1)
                bank = pga if c % 2 == 0 else pua
                t.matmul(bank[:, 0:512], H128[:, :], a_f[:, c, 0:512],
                         start=True, stop=True)
                t.matmul(bank[:, 512:1024], H128[:, :], a_f[:, c, 512:1024],
                         start=True, stop=True)
                t.drain().then_inc(s_fw8, 1)
            t.wait_ge(s_fa8, NM2)
            for m2 in range(NDC):
                if m2 >= 1:
                    t.wait_ge(s_dsl, m2)
                for kq in range(4):
                    si2 = 4 * m2 + kq
                    t.wait_ge(s_w2[si2 % 2], 16 * (si2 // 2 + 1))
                    for k in range(NDC):
                        for nb in range(2):
                            t.matmul(pdn[:, 512 * nb:512 * (nb + 1)],
                                     wsl2[:, si2 % 2, k, :],
                                     a_f[:, 16 * kq + k, 512 * nb:512 * (nb + 1)],
                                     start=(kq == 0 and k == 0),
                                     stop=(kq == 3 and k == NDC - 1))
                    t.drain().then_inc(s_md, 1)
            for tc in range(NCH):
                for dc in range(NDC):
                    gtr = 16 * tc + dc
                    t.wait_ge(s_sg, dc + 1)
                    if gtr >= 2:
                        t.wait_ge(s_cmb, gtr - 1)
                    t.transpose(psT[gtr % 2][:, :],
                                xf[:, dc, 128 * tc:128 * (tc + 1)],
                                identbf[:, :])
                    t.drain().then_inc(s_ptr, 1)

    pg.barrier()


def emit_p4(pg):
    nc = pg.nc
    x1_sp, xn1_sp, s2_sp = pg.io["x1_sp"], pg.io["xn1_sp"], pg.io["s2_sp"]

    identbf = pg.const("identbf", _bf16(np.eye(128, dtype=np.float32)), BF16)
    identf = pg.const("identf", np.eye(128, dtype=np.float32), F32)
    onesrow = pg.const("ones1x128f", np.ones((1, 128), np.float32), F32)
    pg.flush_consts()

    x1_ld = pg.sbuf("p4_x1", [128, 2, DM], F32)
    xn1T = pg.sbuf("p4_xT", [128, NCH, DM], BF16)
    xn1f = pg.sbuf("p4_xf", [128, NDC, TOK], BF16)
    junk = pg.sbuf("p4_junk", [128, DM], F32)
    stats = pg.sbuf("p4_stats", [128, 8], F32)
    est = pg.sbuf("p4_est", [128, DM], F32)
    ebuf = pg.sbuf("p4_ebuf", [128, 2, 256], F32)
    expT = pg.sbuf("p4_expT", [128, 2, 256], BF16)
    mbc = pg.sbuf("p4_mbc", [128, DM], F32)
    m_all = pg.sbuf("p4_mall", [128, NDC], F32)
    rz_all = pg.sbuf("p4_rz", [128, NDC], F32)
    mrow = pg.sbuf("p4_mrow", [1, DM], F32)
    e_st = pg.sbuf("p4_est2", [128, 2, DM], F32)
    s2st = pg.sbuf("p4_s2st", [128, 2, TOK], BF16)

    psT = [pg.psum(f"p4_psT{i}", [128, 128], BF16) for i in range(2)]
    psO = [pg.psum(f"p4_psO{i}", [128, TOK], F32) for i in range(2)]
    psS = pg.psum("p4_psS", [128, 512], F32)
    psR = pg.psum("p4_psR", [1, 128], F32)

    E_loc = nc.dram_tensor("p4_Eloc", [DM, DM], F32, kind="Internal")
    E_sum = nc.dram_tensor("p4_Esum", [DM, DM], F32, kind="Internal")

    s_xl = [pg.sem("p4_xl0"), pg.sem("p4_xl1")]
    cs = Chain(pg.sem("p4_cs"))
    cv = Chain(pg.sem("p4_cv"))
    s_sv = pg.sem("p4_sv")
    s_ptr = pg.sem("p4_ptr")
    s_vtr = pg.sem("p4_vtr")
    s_fms = [pg.sem("p4_fms0"), pg.sem("p4_fms1")]
    s_pe = pg.sem("p4_pe")
    s_ee = pg.sem("p4_ee")
    s_es = [pg.sem("p4_es0"), pg.sem("p4_es1")]
    s_cc = pg.sem("p4_cc")
    s_sl = pg.sem("p4_sl")
    s_sx = pg.sem("p4_sx")
    s_sm = pg.sem("p4_sm")
    s_mt = pg.sem("p4_mt")
    s_mr = pg.sem("p4_mr")
    s_mb = pg.sem("p4_mb")
    s_mbe = pg.sem("p4_mbe")
    s_el = [pg.sem("p4_el0"), pg.sem("p4_el1")]
    s_esb = pg.sem("p4_esb")
    s_ex = pg.sem("p4_ex")
    s_pa = pg.sem("p4_pa")
    s_paj = pg.sem("p4_paj")
    s_ae = pg.sem("p4_ae")
    s_ss = [pg.sem("p4_ss0"), pg.sem("p4_ss1")]

    with nc.Block() as blk:
        @blk.sync
        def _(s):
            for tc in range(NCH + 2):
                if tc < NCH:
                    if tc >= 2:
                        s.wait_ge(s_sv, tc - 1)
                    s.dma_start(out=x1_ld[:, tc % 2, :],
                                in_=x1_sp.ap()[128 * tc:128 * (tc + 1), :]
                                ).then_inc(s_xl[tc % 2], 16)
                if tc >= 2:
                    s.wait_ge(s_vtr, 16 * (tc - 1))
                    s.dma_start(out=xn1_sp.ap()[:, :, 128 * (tc - 2):128 * (tc - 1)]
                                .rearrange("c p t -> p c t"),
                                in_=xn1f[:, :, 128 * (tc - 2):128 * (tc - 1)]
                                ).then_inc(s_fms[tc % 2], 16)
            for i in range(NDC):
                s.wait_ge(s_cc, 1)
                if i >= 1:
                    s.wait_ge(s_sx, i)
                s.dma_start(out=est[:, :], in_=E_sum.ap()[128 * i:128 * (i + 1), :]
                            ).then_inc(s_sl, 16)
            for g in range(8):
                for J in range(NDC):
                    li = NDC * g + J
                    if li >= 2:
                        s.wait_ge(s_ex, li - 1)
                    s.dma_start(out=ebuf[:, li % 2, :],
                                in_=E_sum.ap()[128 * J:128 * (J + 1),
                                               256 * g:256 * g + 256]
                                ).then_inc(s_el[li % 2], 16)
                if g >= 1:
                    for il in range(2):
                        gi = 2 * (g - 1) + il
                        s.wait_ge(s_ae, gi + 1)
                        s.dma_start(out=s2_sp.ap()[gi, :, :],
                                    in_=s2st[:, gi % 2, :]).then_inc(s_ss[gi % 2], 16)
            for il in range(2):
                gi = 14 + il
                s.wait_ge(s_ae, gi + 1)
                s.dma_start(out=s2_sp.ap()[gi, :, :],
                            in_=s2st[:, gi % 2, :]).then_inc(s_ss[gi % 2], 16)
            s.wait_ge(s_ss[0], 16 * 8)
            s.wait_ge(s_ss[1], 16 * 8)

        @blk.gpsimd
        def _(g):
            for m in range(NDC):
                g.wait_ge(s_ee, 4 * (m + 1))
                g.dma_start(out=E_loc.ap()[128 * m:128 * (m + 1), :],
                            in_=e_st[:, m % 2, :]).then_inc(s_es[m % 2], 16)
            g.wait_ge(s_es[0], 16 * 8)
            g.wait_ge(s_es[1], 16 * 8)
            g.collective_compute(
                "AllReduce", ALU.add,
                replica_groups=[[0, 1, 2, 3], [4, 5, 6, 7]],
                ins=[E_loc.ap().opt()], outs=[E_sum.ap().opt()],
            ).then_inc(s_cc, 1)

        @blk.scalar
        def _(sc):
            for tc in range(NCH):
                xi = x1_ld[:, tc % 2, :]
                sc.wait_ge(s_xl[tc % 2], 16 * (tc // 2 + 1))
                if tc >= 1:
                    sc.wait_ge(cv.sem, 6 * (tc - 1) + 3)
                cs.step(sc, sc.activation(junk[:, :], xi, AF.Square,
                                          accum_out=stats[:, 0:1]))
                sc.wait_ge(cv.sem, 6 * tc + 5)
                cs.step(sc, sc.activation(stats[:, 4:5], stats[:, 4:5], AF.Ln))
                cs.sync(sc)
                cs.step(sc, sc.activation(stats[:, 5:6], stats[:, 4:5], AF.Exp,
                                          scale=-0.5))
                cs.step(sc, sc.activation(junk[:, 0:1], stats[:, 0:1], AF.Copy))
            for i in range(NDC):
                sc.wait_ge(s_sm, i + 1)
                sc.activation(junk[:, :], est[:, :], AF.Exp,
                              bias=stats[:, 6:7], accum_out=stats[:, 7:8])
                sc.drain(fusable=True).then_inc(s_sx, 1)
            for g in range(8):
                for J in range(NDC):
                    li = NDC * g + J
                    sc.wait_ge(s_esb, li + 1)
                    if li >= 2:
                        sc.wait_ge(s_paj, li - 1)
                    sc.activation(expT[:, li % 2, :], ebuf[:, li % 2, :], AF.Exp)
                    sc.drain(fusable=True).then_inc(s_ex, 1)

        @blk.vector
        def _(v):
            for tc in range(NCH):
                xi = x1_ld[:, tc % 2, :]
                v.wait_ge(s_xl[tc % 2], 16 * (tc // 2 + 1))
                cv.step(v, v.reduce_sum(stats[:, 1:2], xi, axis=AX.X))
                v.wait_ge(cs.sem, 4 * tc + 1)
                cv.sync(v)
                cv.step(v, v.tensor_scalar(stats[:, 2:3], stats[:, 1:2], 1.0 / DM, None, ALU.mult))
                cv.step(v, v.tensor_scalar(stats[:, 3:4], stats[:, 0:1], 1.0 / DM, None, ALU.mult))
                cv.sync(v)
                cv.step(v, v.tensor_tensor(stats[:, 4:5], stats[:, 2:3], stats[:, 2:3], ALU.mult))
                cv.sync(v)
                cv.step(v, v.tensor_tensor(stats[:, 4:5], stats[:, 3:4], stats[:, 4:5], ALU.subtract))
                cv.sync(v)
                cv.step(v, v.tensor_scalar(stats[:, 4:5], stats[:, 4:5], EPS, None,
                                           ALU.add))
                v.wait_ge(cs.sem, 4 * tc + 3)
                if tc >= 1:
                    v.wait_ge(s_ptr, 16 * tc)
                v.tensor_scalar(xn1T[:, tc, :], xi, stats[:, 2:3], stats[:, 5:6],
                                ALU.subtract, ALU.mult)
                v.drain(fusable=True).then_inc(s_sv, 1)
                for dc in range(NDC):
                    gtr = 16 * tc + dc
                    v.wait_ge(s_ptr, gtr + 1)
                    if tc >= 2 and dc == 0:
                        v.wait_ge(s_fms[tc % 2], 16 * (tc // 2))
                    v.tensor_copy(xn1f[:, dc, 128 * tc:128 * (tc + 1)],
                                  psT[gtr % 2][:, :])
                    v.drain(fusable=True).then_inc(s_vtr, 1)
            for m in range(NDC):
                for jb in range(4):
                    gi = 4 * m + jb
                    v.wait_ge(s_pe, gi + 1)
                    if m >= 2 and jb == 0:
                        v.wait_ge(s_es[m % 2], 16 * (m // 2))
                    v.tensor_copy(e_st[:, m % 2, 512 * jb:512 * (jb + 1)],
                                  psO[gi % 2][:, 0:512])
                    v.drain(fusable=True).then_inc(s_ee, 1)
            for i in range(NDC):
                v.wait_ge(s_sl, 16 * (i + 1))
                cv.step(v, v.reduce_max(stats[:, 6:7], est[:, :], axis=AX.X))
                cv.sync(v)
                cv.step(v, v.tensor_scalar(stats[:, 6:7], stats[:, 6:7], -1.0,
                                           None, ALU.mult))
                cv.sync(v)
                cv.step(v, v.tensor_copy(m_all[:, i:i + 1], stats[:, 6:7]))
                v.drain(fusable=True).then_inc(s_sm, 1)
                v.wait_ge(s_sx, i + 1)
                cv.step(v, v.reciprocal(stats[:, 7:8], stats[:, 7:8]))
                cv.sync(v)
                cv.step(v, v.tensor_copy(rz_all[:, i:i + 1], stats[:, 7:8]))
            for i in range(NDC):
                v.wait_ge(s_mt, i + 1)
                cv.step(v, v.tensor_copy(mrow[0:1, 128 * i:128 * (i + 1)],
                                         psR[0:1, :]))
                v.drain(fusable=True).then_inc(s_mr, 1)
            for j in range(4):
                v.wait_ge(s_mb, j + 1)
                v.tensor_copy(mbc[:, 512 * j:512 * (j + 1)], psS[:, 0:512])
                v.drain(fusable=True).then_inc(s_mbe, 1)
            for g in range(8):
                for J in range(NDC):
                    li = NDC * g + J
                    v.wait_ge(s_el[li % 2], 16 * (li // 2 + 1))
                    if li >= 2:
                        v.wait_ge(s_ex, li - 1)
                    v.tensor_tensor(ebuf[:, li % 2, :], ebuf[:, li % 2, :],
                                    mbc[:, 256 * g:256 * g + 256], ALU.add)
                    v.drain(fusable=True).then_inc(s_esb, 1)
                for il in range(2):
                    gi = 2 * g + il
                    v.wait_ge(s_pa, gi + 1)
                    if gi >= 2:
                        v.wait_ge(s_ss[gi % 2], 16 * (gi // 2))
                    v.tensor_scalar(s2st[:, gi % 2, :], psO[il][:, :],
                                    rz_all[:, gi:gi + 1], None, ALU.mult)
                    v.drain(fusable=True).then_inc(s_ae, 1)

        @blk.tensor
        def _(t):
            for tc in range(NCH):
                t.wait_ge(s_sv, tc + 1)
                for dc in range(NDC):
                    gtr = 16 * tc + dc
                    if gtr >= 2:
                        t.wait_ge(s_vtr, gtr - 1)
                    t.transpose(psT[gtr % 2][:, :],
                                xn1T[:, tc, 128 * dc:128 * (dc + 1)],
                                identbf[:, :])
                    t.drain().then_inc(s_ptr, 1)
            for m in range(NDC):
                for jb in range(4):
                    gi = 4 * m + jb
                    if gi >= 2:
                        t.wait_ge(s_ee, gi - 1)
                    for tc in range(NCH):
                        t.matmul(psO[gi % 2][:, 0:512],
                                 xn1T[:, tc, 128 * m:128 * (m + 1)],
                                 xn1T[:, tc, 512 * jb:512 * (jb + 1)],
                                 start=(tc == 0), stop=(tc == NCH - 1))
                    t.drain().then_inc(s_pe, 1)
            for i in range(NDC):
                t.wait_ge(s_sm, i + 1)
                if i >= 1:
                    t.wait_ge(s_mr, i)
                t.transpose(psR[0:1, :], m_all[:, i:i + 1], identf[:, :])
                t.drain().then_inc(s_mt, 1)
            for j in range(4):
                t.wait_ge(s_mr, NDC)
                if j >= 1:
                    t.wait_ge(s_mbe, j)
                t.matmul(psS[:, 0:512], onesrow[:, :],
                         mrow[0:1, 512 * j:512 * (j + 1)], start=True, stop=True)
                t.drain().then_inc(s_mb, 1)
            t.wait_ge(s_mbe, 4)
            for g in range(8):
                for il in range(2):
                    gi = 2 * g + il
                    if gi >= 2:
                        t.wait_ge(s_ae, gi - 1)
                for J in range(NDC):
                    li = NDC * g + J
                    t.wait_ge(s_ex, li + 1)
                    for il in range(2):
                        for nb in range(2):
                            t.matmul(psO[il][:, 512 * nb:512 * (nb + 1)],
                                     expT[:, li % 2, 128 * il:128 * (il + 1)],
                                     xn1f[:, J, 512 * nb:512 * (nb + 1)],
                                     start=(J == 0), stop=(J == NDC - 1))
                    t.drain().then_inc(s_paj, 1)
                t.drain().then_inc(s_pa, 2)

    pg.barrier()


def emit_p5(pg):
    nc = pg.nc
    xn1_sp, x1_sp, s2_sp = pg.io["xn1_sp"], pg.io["x1_sp"], pg.io["s2_sp"]
    out_t = pg.io["out"]
    mgT, muT, mdT = pg.io["mgT"], pg.io["muT"], pg.io["mdT"]
    sgT, suT, sdT = pg.io["sgT"], pg.io["suT"], pg.io["sdT"]

    identbf = pg.const("identbf", _bf16(np.eye(128, dtype=np.float32)), BF16)
    identf = pg.const("identf", np.eye(128, dtype=np.float32), F32)
    sel16 = np.zeros((16, 16, 128), np.float32)
    for e in range(16):
        sel16[e, e, :] = 1.0
    sel16c = pg.const("sel16f", sel16, F32)
    pg.flush_consts()

    BIG = 1e30
    xn1f = pg.sbuf("p5_xn", [128, NDC, TOK], BF16)
    gw_sb = pg.sbuf("p5_gw", [128, NDC, 16], BF16)
    eb_sb = pg.sbuf("p5_eb", [16, 1], F32)
    scoF = pg.sbuf("p5_sco", [16, TOK], F32)
    biF = pg.sbuf("p5_bi", [16, TOK], F32)
    scT = pg.sbuf("p5_scT", [128, NCH, 16], F32)
    biT = pg.sbuf("p5_biT", [128, NCH, 16], F32)
    wk = pg.sbuf("p5_wk", [128, 64], F32)
    msk0 = pg.sbuf("p5_msk", [128, 16], F32)
    cwTe = pg.sbuf("p5_cwT", [16, NCH, 128], F32)
    cwb = pg.sbuf("p5_cwb", [128, 2, TOK], BF16)
    aE = pg.sbuf("p5_aE", [128, 2, 8, TOK], BF16)
    a1 = pg.sbuf("p5_a1", [128, TOK], BF16)
    gsE = pg.sbuf("p5_gsE", [128, 2, TOK], BF16)
    g2st = pg.sbuf("p5_g2st", [128, 4, TOK], F32)
    g2t = pg.sbuf("p5_g2t", [128, 2, TOK], F32)
    s2t = pg.sbuf("p5_s2t", [128, 2, TOK], BF16)
    gg = pg.sbuf("p5_gg", [128, 2, TOK], BF16)
    wsl = pg.sbuf("p5_wsl", [128, 6, NDC, 128], BF16)
    wsl2 = pg.sbuf("p5_wsl2", [128, 6, 8, 128], BF16)
    x1_ld = pg.sbuf("p5_x1", [128, 2, DM], F32)
    o_st = pg.sbuf("p5_ost", [128, 2, DM], F32)

    pga = pg.psum("p5_pga", [128, TOK], F32)
    pua = pg.psum("p5_pua", [128, TOK], F32)
    pdn = pg.psum("p5_pdn", [128, TOK], F32)
    psm = pg.psum("p5_psm", [128, 512], F32)
    psT = pg.psum("p5_psT", [128, 128], BF16)

    g2_dram = nc.dram_tensor("p5_g2d", [NDC, 128, TOK], F32, kind="Internal")

    s_ld = pg.sem("p5_ld")
    s_gw = pg.sem("p5_gw")
    s_lg = pg.sem("p5_lg")
    s_sig = pg.sem("p5_sig")
    s_bi = pg.sem("p5_bi")
    s_rt = pg.sem("p5_rt")
    s_rte = pg.sem("p5_rte")
    s_cw = pg.sem("p5_cw")
    s_cwt = pg.sem("p5_cwt")
    s_cwe = pg.sem("p5_cwe")
    s_wl = [pg.sem("p5_wl0"), pg.sem("p5_wl1")]
    s_w2 = [pg.sem("p5_w20"), pg.sem("p5_w21")]
    s_mm = pg.sem("p5_mm")
    s_md = pg.sem("p5_md")
    s_cb = pg.sem("p5_cb")
    s_cbe = pg.sem("p5_cbe")
    s_sil = pg.sem("p5_sil")
    s_a1 = pg.sem("p5_a1")
    s_am = pg.sem("p5_am")
    s_gst = pg.sem("p5_gst")
    s_gac = pg.sem("p5_gac")
    s_g2l = [pg.sem("p5_g2l0"), pg.sem("p5_g2l1")]
    s_s2l = [pg.sem("p5_s2l0"), pg.sem("p5_s2l1")]
    s_fs = pg.sem("p5_fs")
    s_sg = pg.sem("p5_sg")
    s_ptr = pg.sem("p5_ptr")
    s_cmb = pg.sem("p5_cmb")
    s_xl = [pg.sem("p5_xl0"), pg.sem("p5_xl1")]
    s_os = [pg.sem("p5_os0"), pg.sem("p5_os1")]
    cv = Chain(pg.sem("p5_cv"))

    def gu_cnt(ei, fc, j):
        return (ei + 1) * 16 + 2 * fc + j + 1

    def dn_cnt(pi, m2):
        return (m2 + 1) if pi < 0 else (16 + pi * 16 + m2 + 1)

    with nc.Block() as blk:
        @blk.sync
        def _(s):
            s.dma_start(out=eb_sb[...], in_=pg.io["eb16"]).then_inc(s_ld, 16)
            s.dma_start(out=xn1f[...],
                        in_=xn1_sp.ap().rearrange("m p t -> p m t")).then_inc(s_ld, 16)
            for m2 in range(NDC):
                s.wait_ge(s_gac, 16 * 144)
                if m2 >= 2:
                    s.wait_ge(s_fs, m2 - 1)
                s.dma_start(out=g2t[:, m2 % 2, :], in_=g2_dram.ap()[m2, :, :]
                            ).then_inc(s_g2l[m2 % 2], 16)
                if m2 >= 2:
                    s.wait_ge(s_sg, m2 - 1)
                s.dma_start(out=s2t[:, m2 % 2, :], in_=s2_sp.ap()[m2, :, :]
                            ).then_inc(s_s2l[m2 % 2], 16)
            for tc in range(NCH + 2):
                if tc < NCH:
                    if tc >= 2:
                        s.wait_ge(s_cmb, 16 * (tc - 1))
                    s.dma_start(out=x1_ld[:, tc % 2, :],
                                in_=x1_sp.ap()[128 * tc:128 * (tc + 1), :]
                                ).then_inc(s_xl[tc % 2], 16)
                if tc >= 2:
                    s.wait_ge(s_cmb, 16 * (tc - 1))
                    s.dma_start(out=out_t[128 * (tc - 2):128 * (tc - 1), :],
                                in_=o_st[:, tc % 2, :]).then_inc(s_os[tc % 2], 16)
            s.wait_ge(s_os[0], 16 * 4)
            s.wait_ge(s_os[1], 16 * 4)

        @blk.gpsimd
        def _(g):
            g.dma_start(out=gw_sb[...],
                        in_=pg.io["gw1T"].rearrange("(c p) e -> p c e", p=128)
                        ).then_inc(s_gw, 16)

            def gu_slab_consumer(si):
                eip = si // 16 - 1
                return gu_cnt(eip, (si % 16) // 2, (si % 16) % 2)

            def dn_slab_consumer(si2):
                if si2 < 16:
                    return dn_cnt(-1, si2)
                r2 = si2 - 16
                return dn_cnt(r2 // 32, (r2 % 32) // 2)

            def emit_gu_slabs(ei):
                for fc in range(8):
                    for j in range(2):
                        si = (ei + 1) * 16 + 2 * fc + j
                        if ei < 0:
                            wt = (sgT if j == 0 else suT)[:, 128 * fc:128 * (fc + 1)]
                        else:
                            wt = (mgT if j == 0 else muT)[ei, :,
                                                          128 * fc:128 * (fc + 1)]
                        if si >= 6:
                            g.wait_ge(s_mm, gu_slab_consumer(si - 6))
                        g.dma_start(out=wsl[:, si % 6, :, :],
                                    in_=wt.rearrange("(c p) m -> p c m", p=128)
                                    ).then_inc(s_wl[si % 2], 16)

            def accum_dma(pi, m2):
                gseq = (pi + 1) * 16 + m2
                g.wait_ge(s_gst, gseq + 1)
                g.dma_start(out=g2_dram.ap()[m2, :, :],
                            in_=g2st[:, gseq % 4, :],
                            accum_op=(ALU.bypass if pi < 0 else ALU.add)
                            ).then_inc(s_gac, 16)

            def emit_dn_slabs(pi):
                g.wait_ge(s_gac, 16 * 16 * (pi + 1))
                for m2 in range(NDC):
                    for eo in range(2 if pi >= 0 else 1):
                        if pi < 0:
                            si2 = m2
                            wt = sdT[:, 128 * m2:128 * (m2 + 1)]
                        else:
                            si2 = 16 + 32 * pi + 2 * m2 + eo
                            wt = mdT[2 * pi + eo, :, 128 * m2:128 * (m2 + 1)]
                        if si2 >= 6:
                            g.wait_ge(s_md, dn_slab_consumer(si2 - 6))
                        g.dma_start(out=wsl2[:, si2 % 6, :, :],
                                    in_=wt.rearrange("(c p) m -> p c m", p=128)
                                    ).then_inc(s_w2[si2 % 2], 16)
                    if m2 >= 4:
                        accum_dma(pi, m2 - 4)
                for m2 in range(NDC - 4, NDC):
                    accum_dma(pi, m2)

            emit_gu_slabs(-1)
            emit_dn_slabs(-1)
            emit_gu_slabs(0)
            for pi in range(8):
                emit_gu_slabs(2 * pi + 1)
                emit_dn_slabs(pi)
                if pi < 7:
                    emit_gu_slabs(2 * pi + 2)

        @blk.scalar
        def _(sc):
            sc.wait_ge(s_lg, 1)
            sc.activation(scoF[:, 0:TOK], pga[0:16, 0:TOK], AF.Sigmoid)
            sc.drain(fusable=True).then_inc(s_sig, 1)
            for ei in range(-1, 16):
                for fc in range(8):
                    gid = (ei + 1) * 8 + fc
                    sc.wait_ge(s_mm, gu_cnt(ei, fc, 0))
                    if gid >= 2:
                        sc.wait_ge(s_a1, gid - 1)
                    sc.activation(gsE[:, gid % 2, :], pga[:, :], AF.Silu)
                    sc.drain(fusable=True).then_inc(s_sil, 1)
            for m2 in range(NDC):
                sc.wait_ge(s_g2l[m2 % 2], 16 * (m2 // 2 + 1))
                if m2 >= 2:
                    sc.wait_ge(s_sg, m2 - 1)
                sc.activation(gg[:, m2 % 2, :], g2t[:, m2 % 2, :], AF.Silu)
                sc.drain(fusable=True).then_inc(s_fs, 1)

        @blk.vector
        def _(v):
            v.wait_ge(s_ld, 32)
            v.wait_ge(s_sig, 1)
            v.tensor_scalar(biF[:, :], scoF[:, :], eb_sb[:, 0:1], None, ALU.add)
            v.drain(fusable=True).then_inc(s_bi, 1)
            for tc in range(NCH):
                for j in range(2):
                    gi = 2 * tc + j
                    v.wait_ge(s_rt, gi + 1)
                    dst = scT if j == 0 else biT
                    v.tensor_copy(dst[:, tc, :], psm[:, 0:16])
                    v.drain(fusable=True).then_inc(s_rte, 1)
            for tc in range(NCH):
                v.wait_ge(s_rte, 2 * tc + 2)
                if tc >= 1:
                    v.wait_ge(s_cwt, tc)
                b3 = biT[:, tc, :].rearrange("p (g s) -> p g s", g=4)
                W = wk
                cv.step(v, v.tensor_tensor(W[:, 0:4], b3[:, :, 0], b3[:, :, 1], ALU.max))
                cv.step(v, v.tensor_tensor(W[:, 4:8], b3[:, :, 0], b3[:, :, 1], ALU.min))
                cv.step(v, v.tensor_tensor(W[:, 8:12], b3[:, :, 2], b3[:, :, 3], ALU.max))
                cv.step(v, v.tensor_tensor(W[:, 12:16], b3[:, :, 2], b3[:, :, 3], ALU.min))
                cv.sync(v)
                cv.step(v, v.tensor_tensor(W[:, 16:20], W[:, 0:4], W[:, 8:12], ALU.max))
                cv.step(v, v.tensor_tensor(W[:, 20:24], W[:, 0:4], W[:, 8:12], ALU.min))
                cv.step(v, v.tensor_tensor(W[:, 24:28], W[:, 4:8], W[:, 12:16], ALU.max))
                cv.sync(v)
                cv.step(v, v.tensor_tensor(W[:, 28:32], W[:, 20:24], W[:, 24:28], ALU.max))
                cv.sync(v)
                cv.step(v, v.tensor_tensor(W[:, 32:36], W[:, 16:20], W[:, 28:32], ALU.add))
                cv.sync(v)
                cv.step(v, v.reduce_max(W[:, 36:37], W[:, 32:36], axis=AX.X))
                cv.sync(v)
                cv.step(v, v.tensor_scalar(W[:, 40:44], W[:, 32:36], W[:, 36:37],
                                           None, ALU.is_equal))
                cv.sync(v)
                cv.step(v, v.tensor_scalar(W[:, 40:44], W[:, 40:44], BIG, None, ALU.mult))
                cv.sync(v)
                cv.step(v, v.tensor_tensor(W[:, 44:48], W[:, 32:36], W[:, 40:44], ALU.subtract))
                cv.sync(v)
                cv.step(v, v.reduce_max(W[:, 36:37], W[:, 44:48], axis=AX.X))
                cv.sync(v)
                cv.step(v, v.tensor_scalar(W[:, 48:52], W[:, 32:36], W[:, 36:37],
                                           None, ALU.is_ge))
                cv.sync(v)
                m0 = msk0[:, :].rearrange("p (g s) -> p g s", g=4)
                for sslot in range(4):
                    cv.step(v, v.tensor_scalar(m0[:, :, sslot], W[:, 48:52], 1.0,
                                               None, ALU.subtract))
                cv.sync(v)
                cv.step(v, v.tensor_scalar(msk0[:, :], msk0[:, :], BIG, None, ALU.mult))
                cv.sync(v)
                cv.step(v, v.tensor_tensor(msk0[:, :], biT[:, tc, :], msk0[:, :], ALU.add))
                cv.sync(v)
                cv.step(v, v.tensor_copy(W[:, 48:64], msk0[:, :]))
                cv.sync(v)
                for it in range(3):
                    cv.step(v, v.reduce_max(W[:, 36:37], W[:, 48:64], axis=AX.X))
                    cv.sync(v)
                    cv.step(v, v.tensor_scalar(W[:, 0:16], W[:, 48:64], W[:, 36:37],
                                               None, ALU.is_equal))
                    cv.sync(v)
                    cv.step(v, v.tensor_scalar(W[:, 0:16], W[:, 0:16], BIG, None, ALU.mult))
                    cv.sync(v)
                    cv.step(v, v.tensor_tensor(W[:, 48:64], W[:, 48:64], W[:, 0:16],
                                               ALU.subtract))
                    cv.sync(v)
                cv.step(v, v.reduce_max(W[:, 36:37], W[:, 48:64], axis=AX.X))
                cv.sync(v)
                cv.step(v, v.tensor_scalar(W[:, 0:16], msk0[:, :], W[:, 36:37],
                                           None, ALU.is_ge))
                cv.sync(v)
                cv.step(v, v.tensor_tensor(W[:, 16:32], scT[:, tc, :], W[:, 0:16],
                                           ALU.mult))
                cv.sync(v)
                cv.step(v, v.reduce_sum(W[:, 36:37], W[:, 16:32], axis=AX.X))
                cv.sync(v)
                cv.step(v, v.tensor_scalar(W[:, 36:37], W[:, 36:37], 1e-20, None, ALU.add))
                cv.sync(v)
                cv.step(v, v.reciprocal(W[:, 37:38], W[:, 36:37]))
                cv.sync(v)
                cv.step(v, v.tensor_scalar(W[:, 16:32], W[:, 16:32], W[:, 37:38],
                                           2.5, ALU.mult, ALU.mult))
                v.drain(fusable=True).then_inc(s_cw, 1)
                v.wait_ge(s_cwt, tc + 1)
                v.tensor_copy(cwTe[:, tc, :], psm[0:16, 16:144])
                v.drain(fusable=True).then_inc(s_cwe, 1)
            for ei in range(-1, 16):
                if ei >= 0:
                    v.wait_ge(s_md, 16 + (ei // 2) * 16)
                    for half in range(2):
                        v.wait_ge(s_cb, 2 * ei + half + 1)
                        v.tensor_copy(cwb[:, ei % 2, 512 * half:512 * (half + 1)],
                                      psm[:, 0:512])
                        v.drain(fusable=True).then_inc(s_cbe, 1)
                for fc in range(8):
                    gid = (ei + 1) * 8 + fc
                    v.wait_ge(s_sil, gid + 1)
                    v.wait_ge(s_mm, gu_cnt(ei, fc, 1))
                    v.tensor_tensor(a1[:, :], gsE[:, gid % 2, :], pua[:, :], ALU.mult)
                    v.drain(fusable=True).then_inc(s_a1, 1)
                    if ei < 0:
                        v.tensor_copy(aE[:, 0, fc, :], a1[:, :])
                    else:
                        v.tensor_tensor(aE[:, ei % 2, fc, :], a1[:, :],
                                        cwb[:, ei % 2, :], ALU.mult)
                    v.drain(fusable=True).then_inc(s_am, 1)
                if ei == -1 or ei % 2 == 1:
                    pi = -1 if ei == -1 else ei // 2
                    for m2 in range(NDC):
                        gseq = (pi + 1) * 16 + m2
                        v.wait_ge(s_md, dn_cnt(pi, m2))
                        if gseq >= 4:
                            v.wait_ge(s_gac, 16 * (gseq - 3))
                        v.tensor_copy(g2st[:, gseq % 4, :], pdn[:, :])
                        v.drain(fusable=True).then_inc(s_gst, 1)
            for m2 in range(NDC):
                v.wait_ge(s_fs, m2 + 1)
                v.wait_ge(s_s2l[m2 % 2], 16 * (m2 // 2 + 1))
                v.tensor_tensor(xn1f[:, m2, :], s2t[:, m2 % 2, :], gg[:, m2 % 2, :],
                                ALU.mult)
                v.drain(fusable=True).then_inc(s_sg, 1)
            for tc in range(NCH):
                for dc in range(NDC):
                    gtr = 16 * tc + dc
                    v.wait_ge(s_ptr, gtr + 1)
                    v.wait_ge(s_xl[tc % 2], 16 * (tc // 2 + 1))
                    if tc >= 2 and dc == 0:
                        v.wait_ge(s_os[tc % 2], 16 * (tc // 2))
                    v.tensor_tensor(o_st[:, tc % 2, 128 * dc:128 * (dc + 1)],
                                    psT[:, :], x1_ld[:, tc % 2, 128 * dc:128 * (dc + 1)],
                                    ALU.add)
                    v.drain(fusable=True).then_inc(s_cmb, 1)

        @blk.tensor
        def _(t):
            t.wait_ge(s_gw, 16)
            t.wait_ge(s_ld, 32)
            for k in range(NDC):
                for nb in range(2):
                    t.matmul(pga[0:16, 512 * nb:512 * (nb + 1)], gw_sb[:, k, :],
                             xn1f[:, k, 512 * nb:512 * (nb + 1)],
                             start=(k == 0), stop=(k == NDC - 1))
            t.drain().then_inc(s_lg, 1)
            for tc in range(NCH):
                for j in range(2):
                    gi = 2 * tc + j
                    src = scoF if j == 0 else biF
                    if j == 1:
                        t.wait_ge(s_bi, 1)
                    else:
                        t.wait_ge(s_sig, 1)
                    if gi >= 1:
                        t.wait_ge(s_rte, gi)
                    t.transpose(psm[0:128, 0:16], src[:, 128 * tc:128 * (tc + 1)],
                                identf[0:16, 0:16])
                    t.drain().then_inc(s_rt, 1)
            for tc in range(NCH):
                t.wait_ge(s_cw, tc + 1)
                if tc >= 1:
                    t.wait_ge(s_cwe, tc)
                t.transpose(psm[0:16, 16:144], wk[:, 16:32], identf[:, :])
                t.drain().then_inc(s_cwt, 1)
            for ei in range(-1, 16):
                if ei >= 0:
                    t.wait_ge(s_cwe, NCH)
                    cwv = cwTe[:, :, :].rearrange("e c t -> e (c t)")
                    for half in range(2):
                        if 2 * ei + half >= 1:
                            t.wait_ge(s_cbe, 2 * ei + half)
                        t.matmul(psm[:, 0:512], sel16c[:, ei, :],
                                 cwv[:, 512 * half:512 * (half + 1)],
                                 start=True, stop=True)
                        t.drain().then_inc(s_cb, 1)
                for fc in range(8):
                    for j in range(2):
                        si = (ei + 1) * 16 + 2 * fc + j
                        t.wait_ge(s_wl[si % 2], 16 * (si // 2 + 1))
                        gid = (ei + 1) * 8 + fc
                        bank = pga if j == 0 else pua
                        if gid >= 1:
                            if j == 0:
                                t.wait_ge(s_sil, gid)
                            else:
                                t.wait_ge(s_a1, gid)
                        elif j == 0:
                            t.wait_ge(s_sig, 1)
                        for k in range(NDC):
                            for nb in range(2):
                                t.matmul(bank[:, 512 * nb:512 * (nb + 1)],
                                         wsl[:, si % 6, k, :],
                                         xn1f[:, k, 512 * nb:512 * (nb + 1)],
                                         start=(k == 0), stop=(k == NDC - 1))
                        t.drain().then_inc(s_mm, 1)
                if ei == -1 or ei % 2 == 1:
                    pi = -1 if ei == -1 else ei // 2
                    t.wait_ge(s_am, ((ei + 2) * 8) if ei >= 0 else 8)
                    for m2 in range(NDC):
                        gseq = (pi + 1) * 16 + m2
                        if gseq >= 1:
                            t.wait_ge(s_gst, gseq)
                        for eo in range(2 if pi >= 0 else 1):
                            if pi < 0:
                                si2 = m2
                            else:
                                si2 = 16 + 32 * pi + 2 * m2 + eo
                            t.wait_ge(s_w2[si2 % 2], 16 * (si2 // 2 + 1))
                            first = eo == 0
                            last = (eo == (1 if pi >= 0 else 0))
                            for k in range(8):
                                for nb in range(2):
                                    t.matmul(pdn[:, 512 * nb:512 * (nb + 1)],
                                             wsl2[:, si2 % 6, k, :],
                                             aE[:, eo, k, 512 * nb:512 * (nb + 1)],
                                             start=(first and k == 0),
                                             stop=(last and k == 7))
                        t.drain().then_inc(s_md, 1)
            t.wait_ge(s_sg, NDC)
            for tc in range(NCH):
                for dc in range(NDC):
                    gtr = 16 * tc + dc
                    if gtr >= 1:
                        t.wait_ge(s_cmb, gtr)
                    t.transpose(psT[:, :], xn1f[:, dc, 128 * tc:128 * (tc + 1)],
                                identbf[:, :])
                    t.drain().then_inc(s_ptr, 1)

    pg.barrier()


def _col128(v):
    v = np.ascontiguousarray(v, dtype=np.float32)
    return np.ascontiguousarray(v.reshape(-1, 128).T)


def host_layout(inputs):
    T = lambda a: np.ascontiguousarray(np.asarray(a, dtype=np.float32).T)
    w = {}
    w["ipT"] = T(inputs["L0_attn_in_proj_w"])
    w["opT"] = T(inputs["L0_attn_out_proj_w"])
    w["gwT"] = T(inputs["L0_attn_gate_w"])
    w["uwT"] = T(inputs["L0_attn_up_w"])
    w["dwT"] = T(inputs["L0_attn_down_w"])
    w["fupT"] = T(inputs["L0_ffn_W_up"])
    w["fgT"] = T(inputs["L0_ffn_W_gate"])
    w["fdT"] = T(inputs["L0_ffn_W_down"])
    w["mgT"] = np.ascontiguousarray(np.asarray(inputs["L1_ffn_gate_proj"], dtype=np.float32).transpose(0, 2, 1))
    w["muT"] = np.ascontiguousarray(np.asarray(inputs["L1_ffn_up_proj"], dtype=np.float32).transpose(0, 2, 1))
    w["mdT"] = np.ascontiguousarray(np.asarray(inputs["L1_ffn_down_proj"], dtype=np.float32).transpose(0, 2, 1))
    w["sgT"] = T(inputs["L1_ffn_shared_gate"])
    w["suT"] = T(inputs["L1_ffn_shared_up"])
    w["sdT"] = T(inputs["L1_ffn_shared_down"])
    w["gw1T"] = T(inputs["L1_ffn_gate_weight"])
    w["eb16"] = np.ascontiguousarray(np.asarray(inputs["L1_ffn_expert_bias"], dtype=np.float32).reshape(16, 1))
    w["zb128"] = _col128(inputs["L0_attn_z_bias"])
    w["cb128"] = _col128(inputs["L0_attn_conv_b"])
    cw = np.ascontiguousarray(np.asarray(inputs["L0_attn_conv_w"], dtype=np.float32))
    w["cw128"] = np.ascontiguousarray(cw.reshape(48, 128, 4).transpose(1, 0, 2))
    w["n1w"] = _col128(inputs["L0_attn_norm1_w"])
    w["n2w"] = _col128(inputs["L0_attn_norm2_w"])
    w["sug"] = _col128(inputs["L0_ffn_SU_ug"])
    w["sdp"] = _col128(inputs["L0_ffn_SU_dp"])
    wug = np.asarray(inputs["L0_ffn_Wscale_ug"], dtype=np.float32)
    w["wsu"] = _col128(wug[:DFF])
    w["wsg"] = _col128(wug[DFF:])
    w["wsd"] = _col128(inputs["L0_ffn_Wscale_dp"])
    D = np.asarray(inputs["L0_attn_D"], dtype=np.float32)
    w["D64"] = _col128(np.repeat(D, HD))
    xf = np.asarray(inputs["L0_attn_hidden_states"], dtype=np.float32)
    xs = []
    for core in range(8):
        b, q = core // 4, core % 4
        lo = TOK * q
        if q == 0:
            sl = np.concatenate([np.zeros((3, DM), np.float32), xf[b, 0:TOK + 0]], axis=0)
        else:
            sl = xf[b, lo - 3:lo + TOK]
        xs.append(np.ascontiguousarray(sl))
    return w, xs


EMITTERS = {"p0": emit_p0, "p1": emit_p1, "p2": emit_p2, "p3": emit_p3, "p4": emit_p4, "p5": emit_p5}


def build_program(phases):
    pg = Pg()
    nc = pg.nc
    produced = {BOUND[b][2]: None for b in BOUND}
    inside = set(phases)
    if any("x" in CONSUMES[p] for p in phases):
        pg.io["x"] = nc.dram_tensor("x", [TOKH, DM], F32, kind="ExternalInput").ap()
    if "p1" in phases:
        pg.io["gmask"] = nc.dram_tensor("gmask", [64, 8], F32, kind="ExternalInput").ap()
    if "p5" in phases:
        pg.io["out"] = nc.dram_tensor("out", [TOK, DM], F32, kind="ExternalOutput").ap()
    for name, (shape, dt, producer) in BOUND.items():
        used = any(name in CONSUMES[p] for p in phases)
        made = producer in inside
        if not (used or made):
            continue
        if made and used:
            kind = "Internal"
        elif made:
            kind = "ExternalOutput"
        else:
            kind = "ExternalInput"
        pg.io[name] = nc.dram_tensor(name, shape, dt, kind=kind)
    for p in phases:
        for name, shape in WEIGHTS[p].items():
            if name not in pg.io:
                pg.io[name] = nc.dram_tensor(name, shape, F32, kind="ExternalInput").ap()
    for p in phases:
        pg.begin_phase()
        EMITTERS[p](pg)
        pg.end_phase()
    return pg


def _run(pg, in_map, n_cores=8):
    res = run_bass_kernel_spmd(pg.nc, in_map, core_ids=list(range(n_cores)))
    return res


def kernel(**inputs):
    w, xs = host_layout(inputs)
    if FUSED:
        groups = [PHASE_ORDER]
    else:
        groups = [[p] for p in PHASE_ORDER]

    state = [dict() for _ in range(8)]
    out = None
    for phases in groups:
        pg = build_program(phases)
        in_maps = []
        for core in range(8):
            m = {}
            if any("x" in CONSUMES[p] for p in phases):
                m["x"] = xs[core]
            if "p1" in phases:
                q = core % 4
                g = np.zeros((64, 8), np.float32)
                for r in range(4):
                    g[:, r] = 1.0 if r < q else 0.0
                    g[:, 4 + r] = 0.0 if r < q else 1.0
                m["gmask"] = g
            for p in phases:
                for name in WEIGHTS[p]:
                    m[name] = w[name]
            for name in BOUND:
                if name in pg.io and any(name in CONSUMES[p] for p in phases) \
                        and BOUND[name][2] not in phases:
                    m[name] = state[core][name]
            in_maps.append(m)
        res = _run(pg, in_maps)
        for core in range(8):
            r = res.results[core]
            for name in BOUND:
                if name in r:
                    state[core][name] = r[name]
            if "out" in r:
                if out is None:
                    out = np.zeros((2, 4096, DM), np.float32)
                b, q = core // 4, core % 4
                out[b, TOK * q:TOK * (q + 1)] = r["out"]
        pg.stack.close()
    return out
